# version 15
# baseline (speedup 1.0000x reference)
import numpy as np
import concourse.bass as bass
import concourse.mybir as mybir
from concourse.bass_utils import run_bass_kernel_spmd

F32 = mybir.dt.float32
F32R = mybir.dt.float32r
Act = mybir.ActivationFunctionType
Alu = mybir.AluOpType

C = 1024
NCH = C // 128
TILE = 512
NSL = TILE // 128

EXCHANGE = True


def build(T_c, exchange):
    NT = T_c // TILE
    nc = bass.Bass("TRN2", target_bir_lowering=False,
                   **({"num_devices": 8} if exchange else {}))

    x_d = nc.dram_tensor("x", [T_c, C], F32R, kind="ExternalInput").ap()
    xprev_d = nc.dram_tensor("xprev_fm", [128, NCH], F32, kind="ExternalInput").ap()
    w_d = {n: nc.dram_tensor(n, [C, C], F32R, kind="ExternalInput").ap()
           for n in ["WkT", "WvT", "WrT", "WoT", "fWrT", "fWkT", "fWvT"]}
    tm_d = nc.dram_tensor("tm_fm", [128, NCH], F32, kind="ExternalInput").ap()
    mur_d = nc.dram_tensor("mur_fm", [128, NCH], F32, kind="ExternalInput").ap()
    muk_d = nc.dram_tensor("muk_fm", [128, NCH], F32, kind="ExternalInput").ap()
    td_d = nc.dram_tensor("td_fm", [128, NCH], F32, kind="ExternalInput").ap()
    tf_d = nc.dram_tensor("tf_fm", [128, NCH], F32, kind="ExternalInput").ap()
    out_d = nc.dram_tensor("out", [T_c, C], F32, kind="ExternalOutput").ap()

    r_d = nc.dram_tensor("r_s", [NT, NCH, 128, TILE], F32R, kind="Internal").ap()
    wkv_d = nc.dram_tensor("wkv_s", [NT, NCH, 128, TILE], F32, kind="Internal").ap()
    sumk_d = nc.dram_tensor("sumk_s", [NT, NCH, 128, TILE], F32, kind="Internal").ap()
    y1_d = nc.dram_tensor("y1_s", [NT, NSL, 128, C], F32R, kind="Internal").ap()

    if exchange:
        flagc_d = nc.dram_tensor("flagc", [128, 1], F32, kind="ExternalInput").ap()
        flagu_d = nc.dram_tensor("flagu", [128, 1], F32, kind="ExternalInput").ap()
        cc1_in = nc.dram_tensor("cc1_in", [128, 2 * NCH], F32, kind="Internal")
        cc1_out = nc.dram_tensor("cc1_out", [128, 2 * NCH], F32, kind="Internal")
        cc2_in = nc.dram_tensor("cc2_in", [1, C], F32, kind="Internal")
        cc2_out = nc.dram_tensor("cc2_out", [1, C], F32, kind="Internal")

    xr_view = x_d.rearrange("(a p) c -> p a c", p=128)

    def st1(ti, j):
        return 272 * ti + 16 + 32 * (j + 1)
    ST1_ALL = 272 * NT
    ST2_ALL = ST1_ALL + 16 * NT * NSL
    def ld2(ti, part):
        return 16 * (ti * (NSL + 1 + 2 * NCH) + part)

    from contextlib import ExitStack
    with ExitStack() as ctx:
        sbuf = lambda *a: ctx.enter_context(nc.sbuf_tensor(*a))

        cpool = sbuf("cpool", [128, 8], F32)
        CB_60, CB_M1, CB_1, CB_EPS16, CB_LNEPS, CB_M05, CB_M512 = range(7)
        ident = sbuf("ident", [128, 128], F32R)
        tm_c = sbuf("tm_c", [128, NCH], F32)
        mur_c = sbuf("mur_c", [128, NCH], F32)
        muk_c = sbuf("muk_c", [128, NCH], F32)
        mur1_c = sbuf("mur1_c", [128, NCH], F32)
        muk1_c = sbuf("muk1_c", [128, NCH], F32)
        td_c = sbuf("td_c", [128, NCH], F32)
        tf_c = sbuf("tf_c", [128, NCH], F32)
        dec_c = sbuf("dec_c", [128, NCH], F32)
        wdec_c = sbuf("wdec_c", [128, NCH], F32)
        etf_c = sbuf("etf_c", [128, NCH], F32)
        w512_c = sbuf("w512_c", [128, NCH], F32)
        dexpt = sbuf("dexpt", [128, NCH, TILE], F32R)
        carry_kv = sbuf("carry_kv", [128, NCH], F32)
        carry_k = sbuf("carry_k", [128, NCH], F32)
        i_kv = sbuf("i_kv", [128, NCH], F32)
        i_k = sbuf("i_k", [128, NCH], F32)
        sq_junk = sbuf("sq_junk", [128, 512], F32)
        sums = sbuf("sums", [128, NSL], F32)
        sumsq = sbuf("sumsq", [128, 2 * NSL], F32)
        m_col = sbuf("m_col", [128, NSL], F32)
        v_col = sbuf("v_col", [128, NSL], F32)
        is_col = sbuf("is_col", [128, NSL], F32)
        b_col = sbuf("b_col", [128, NSL], F32)
        wslotA = sbuf("wslotA", [128, NCH, C], F32R)
        wslotB = sbuf("wslotB", [128, NCH, C], F32R)
        wslotC = sbuf("wslotC", [128, NCH, C], F32R)
        if exchange:
            flagc = sbuf("flagc_s", [128, 1], F32)
            flagu = sbuf("flagu_s", [128, 1], F32)
            pay1 = sbuf("pay1", [128, 2 * NCH], F32)
            rcv1 = sbuf("rcv1", [128, 2 * NCH], F32)
            pay2 = sbuf("pay2", [128, C], F32)
            rcv2fm = sbuf("rcv2fm", [128, NCH], F32)
            ycol0 = sbuf("ycol0", [128, NCH], F32)

        psA = [ctx.enter_context(nc.psum_tensor(f"psA{i}", [128, 512], F32))
               for i in range(2)]
        psB = [ctx.enter_context(nc.psum_tensor(f"psB{i}", [128, 512], F32))
               for i in range(2)]
        psW = [ctx.enter_context(nc.psum_tensor(f"psW{i}", [128, C], F32))
               for i in range(2)]
        psK3 = [psA[0][:], psA[1][:], psW[0][:, 0:512]]
        psV3 = [psB[0][:], psB[1][:], psW[0][:, 512:C]]

        S = lambda name: ctx.enter_context(nc.semaphore(name))
        s_init = S("s_init")
        s_wl = S("s_wl")
        s_wlo = S("s_wlo")
        s_wlr = S("s_wlr")
        s_wfk = S("s_wfk")
        s_wfr = S("s_wfr")
        s_wfv = S("s_wfv")
        s_xld = [S("s_xld0"), S("s_xld1")]
        s_T = S("s_T")
        s_ev = S("s_ev")
        s_bnd = S("s_bnd")
        s_bndd = S("s_bndd")
        s_xm = S("s_xm")
        s_kmm = S("s_kmm")
        s_k = S("s_k")
        s_vmm = S("s_vmm")
        s_kv = S("s_kv")
        s_scan = S("s_scan")
        s_tts = S("s_tts")
        s_gw = S("s_gw")
        s_str = [S("s_str0"), S("s_str1")]
        s_rev = S("s_rev")
        s_l2r = [S("s_l2r0"), S("s_l2r1"), S("s_l2r2")]
        s_stw = [S("s_stw0"), S("s_stw1")]
        s_sty = [S("s_sty0"), S("s_sty1"), S("s_sty2"), S("s_sty3")]
        s_sto = [S("s_sto0"), S("s_sto1")]
        s_l2x = [S("s_l2x0"), S("s_l2x1"), S("s_l2x2"), S("s_l2x3")]
        s_l2w = [S("s_l2w0"), S("s_l2w1"), S("s_l2w2")]
        s_rmm = S("s_rmm")
        s_e = S("s_e")
        s_den = S("s_den")
        s_inv = S("s_inv")
        s_rw = S("s_rw")
        s_omm = S("s_omm")
        s_y1p = S("s_y1p")
        s_sq = S("s_sq")
        s_var = S("s_var")
        s_rsq = S("s_rsq")
        s_y1w = S("s_y1w")
        s_l3t = [S("s_l3t0"), S("s_l3t1")]
        s_l3r = [S("s_l3r0"), S("s_l3r1")]
        s_T3 = S("s_T3")
        s_ev3 = S("s_ev3")
        s_bnd3 = S("s_bnd3")
        s_mixk = S("s_mixk")
        s_mixr = S("s_mixr")
        s_kfmm = S("s_kfmm")
        s_kact = S("s_kact")
        s_rfmm = S("s_rfmm")
        s_fe = S("s_fe")
        s_sigf = S("s_sigf")
        s_vomm = S("s_vomm")
        s_pr = S("s_pr")
        s_sq3 = S("s_sq3")
        s_var3 = S("s_var3")
        s_rsq3 = S("s_rsq3")
        s_ow = S("s_ow")
        s_pay1 = S("s_pay1")
        s_ccd = S("s_ccd")
        s_ccc = S("s_ccc")
        s_ikv = S("s_ikv")
        s_dx1 = S("s_dx1")
        s_dx2 = S("s_dx2")
        s_dx3 = S("s_dx3")
        s_pay2 = S("s_pay2")
        s_rowmm = S("s_rowmm")
        s_ycol = S("s_ycol")
        s_misc = S("s_misc")
        s_misc2 = S("s_misc2")

        blk = ctx.enter_context(nc.Block())

        ph0 = ExitStack()
        iot_r = ph0.enter_context(nc.sbuf_tensor("iot_r", [128, 128], F32))
        iot_c = ph0.enter_context(nc.sbuf_tensor("iot_c", [128, 128], F32))
        if True:

            @blk.sync
            def _(sync):
                sync.dma_start(out=tm_c[:], in_=tm_d).then_inc(s_misc, 16)
                sync.dma_start(out=mur_c[:], in_=mur_d).then_inc(s_misc, 16)
                sync.dma_start(out=muk_c[:], in_=muk_d).then_inc(s_misc, 16)
                sync.dma_start(out=td_c[:], in_=td_d).then_inc(s_misc, 16)
                sync.dma_start(out=tf_c[:], in_=tf_d).then_inc(s_misc, 16)
                if exchange:
                    sync.dma_start(out=flagc[:], in_=flagc_d).then_inc(s_misc2, 16)
                    sync.dma_start(out=flagu[:], in_=flagu_d).then_inc(s_misc2, 16)

            @blk.gpsimd
            def _(g):
                g.iota(iot_r[:], [[0, 128]], channel_multiplier=1,
                       allow_small_or_imprecise_dtypes=True)
                g.iota(iot_c[:], [[1, 128]], channel_multiplier=0,
                       allow_small_or_imprecise_dtypes=True).then_inc(s_init, 1)

            @blk.vector
            def _(v):
                for col, val in [(CB_60, 60.0), (CB_M1, -1.0), (CB_1, 1.0),
                                 (CB_EPS16, 1e-16), (CB_LNEPS, 1e-5),
                                 (CB_M05, -0.5), (CB_M512, -512.0)]:
                    v.memset(cpool[:, col:col + 1], val)
                v.memset(carry_kv[:], 0.0)
                v.memset(carry_k[:], 0.0)
                v.memset(i_kv[:], 0.0)
                v.memset(i_k[:], 0.0)
                v.wait_ge(s_misc, 16 * 5)
                v.tensor_scalar(mur1_c[:], mur_c[:], -1.0, None, Alu.add)
                v.tensor_scalar(muk1_c[:], muk_c[:], -1.0, None, Alu.add)
                v.wait_ge(s_init, 1)
                v.tensor_tensor(ident[:], iot_r[:], iot_c[:],
                                Alu.is_equal).then_inc(s_init, 1)

            @blk.scalar
            def _(sc):
                sc.wait_ge(s_misc, 16 * 5)
                sc.wait_ge(s_init, 2)
                sc.activation(dec_c[:], td_c[:], Act.Exp)
                sc.drain()
                sc.activation(wdec_c[:], dec_c[:], Act.Exp,
                              scale=cpool[:, CB_M1:CB_M1 + 1])
                sc.activation(w512_c[:], dec_c[:], Act.Exp,
                              scale=cpool[:, CB_M512:CB_M512 + 1])
                sc.activation(etf_c[:], tf_c[:], Act.Exp).then_inc(s_init, 1)


        @blk.gpsimd
        def _(g):
            g.wait_ge(s_init, 3)
            g.iota(sq_junk[:, 0:TILE], [[1, TILE]], channel_multiplier=0,
                   allow_small_or_imprecise_dtypes=True).then_inc(s_dx1, 1)

        @blk.vector
        def _(v):
            v.wait_ge(s_dx1, 1)
            v.wait_ge(s_init, 3)
            for j in range(NCH):
                i = v.tensor_scalar(dexpt[:, j, :], sq_junk[:, 0:TILE],
                                    dec_c[:, j:j + 1], None, Alu.mult)
            i.then_inc(s_dx2, 1)

        @blk.scalar
        def _(sc):
            sc.wait_ge(s_dx2, 1)
            for j in range(NCH):
                i = sc.activation(dexpt[:, j, :],
                                  dexpt[:, j, :].bitcast(F32), Act.Exp,
                                  scale=cpool[:, CB_M1:CB_M1 + 1])
            i.then_inc(s_dx3, 1)

        ph0.close()
        ph1 = ExitStack()
        x_tm0 = ph1.enter_context(nc.sbuf_tensor("x_tm0", [128, C], F32R))
        x_tm1 = ph1.enter_context(nc.sbuf_tensor("x_tm1", [128, C], F32R))
        x_fm = ph1.enter_context(nc.sbuf_tensor("x_fm", [128, NCH, TILE + 1], F32))
        d1 = [ph1.enter_context(nc.sbuf_tensor(f"d1_{i}", [128, TILE], F32))
              for i in range(2)]
        rb = [ph1.enter_context(nc.sbuf_tensor(f"rb{i}", [128, TILE], F32R))
              for i in range(2)]
        xm_fm = ph1.enter_context(nc.sbuf_tensor("xm_fm", [128, NCH, TILE], F32R))
        k_sb0 = ph1.enter_context(nc.sbuf_tensor("k_sb0", [128, TILE], F32))
        k_sb1 = ph1.enter_context(nc.sbuf_tensor("k_sb1", [128, TILE], F32))
        k_sb2 = ph1.enter_context(nc.sbuf_tensor("k_sb2", [128, TILE], F32))
        kv_sb0 = ph1.enter_context(nc.sbuf_tensor("kv_sb0", [128, TILE], F32))
        kv_sb1 = ph1.enter_context(nc.sbuf_tensor("kv_sb1", [128, TILE], F32))
        kv_sb2 = ph1.enter_context(nc.sbuf_tensor("kv_sb2", [128, TILE], F32))
        s_buf0 = ph1.enter_context(nc.sbuf_tensor("s_buf0", [128, TILE + 1], F32))
        s_buf1 = ph1.enter_context(nc.sbuf_tensor("s_buf1", [128, TILE + 1], F32))
        s_buf2 = ph1.enter_context(nc.sbuf_tensor("s_buf2", [128, TILE + 1], F32))
        sk_buf0 = ph1.enter_context(nc.sbuf_tensor("sk_buf0", [128, TILE + 1], F32))
        sk_buf1 = ph1.enter_context(nc.sbuf_tensor("sk_buf1", [128, TILE + 1], F32))
        sk_buf2 = ph1.enter_context(nc.sbuf_tensor("sk_buf2", [128, TILE + 1], F32))
        wkv_sb0 = ph1.enter_context(nc.sbuf_tensor("wkv_sb0", [128, TILE], F32))
        wkv_sb1 = ph1.enter_context(nc.sbuf_tensor("wkv_sb1", [128, TILE], F32))
        sumk_sb0 = ph1.enter_context(nc.sbuf_tensor("sumk_sb0", [128, TILE], F32))
        sumk_sb1 = ph1.enter_context(nc.sbuf_tensor("sumk_sb1", [128, TILE], F32))
        if True:
            x_tm = [x_tm0, x_tm1]
            k_sb, kv_sb = [k_sb0, k_sb1, k_sb2], [kv_sb0, kv_sb1, kv_sb2]
            s_bufs, sk_bufs = [s_buf0, s_buf1, s_buf2], [sk_buf0, sk_buf1, sk_buf2]
            wkv_sb, sumk_sb = [wkv_sb0, wkv_sb1], [sumk_sb0, sumk_sb1]

            @blk.gpsimd
            def _(g):
                g.wait_ge(s_init, 3)
                g.nop().then_inc(s_init, 1)
                g.dma_start(out=x_fm[:, :, 0:1],
                            in_=xprev_d.rearrange("p (a b) -> p a b", b=1),
                            allow_slow_non_contiguous=True
                            ).then_inc(s_bndd, 16)


            @blk.scalar
            def _(sc):
                sc.dma_start(out=wslotA[:], in_=w_d["WkT"].rearrange(
                    "(a p) c -> p a c", p=128)).then_inc(s_wl, 16)
                sc.dma_start(out=wslotB[:], in_=w_d["WvT"].rearrange(
                    "(a p) c -> p a c", p=128)).then_inc(s_wl, 16)
                sc.dma_start(out=wslotC[:], in_=w_d["WrT"].rearrange(
                    "(a p) c -> p a c", p=128)).then_inc(s_wlr, 16)

            @blk.sync
            def _(sync):
                sync.wait_ge(s_init, 3)
                for ti in range(NT):
                    for sl in range(NSL):
                        gs = ti * NSL + sl
                        if gs >= 2:
                            sync.wait_ge(s_T, gs - 1)
                        sync.dma_start(
                            out=x_tm[gs % 2][:],
                            in_=xr_view[:, gs, :],
                        ).then_inc(s_xld[gs % 2], 16)

            @blk.tensor
            def _(t):
                t.wait_ge(s_init, 2)
                for ti in range(NT):
                    for sl in range(NSL):
                        gs = ti * NSL + sl
                        t.wait_ge(s_xld[gs % 2], 16 * (gs // 2 + 1))
                        if gs >= 1:
                            t.wait_ge(s_ev, gs)
                        if ti >= 1 and sl == 0:
                            t.wait_ge(s_k, ti * NCH)
                        for j in range(NCH):
                            dst = psA[0] if j < 4 else psA[1]
                            i = t.transpose(
                                dst[:].bitcast(F32R)[:, (j % 4) * 128:(j % 4 + 1) * 128],
                                x_tm[gs % 2][:, j * 128:(j + 1) * 128],
                                ident[:])
                        i.then_inc(s_T, 1)
                    t.wait_ge(s_xm, ti + 1)
                    for j in range(NCH):
                        g = ti * NCH + j
                        if g >= 3:
                            t.wait_ge(s_k, g - 2)
                        pk = psK3[g % 3]
                        for c in range(NCH):
                            i = t.matmul(pk,
                                         wslotA[:, c, j * 128:(j + 1) * 128],
                                         xm_fm[:, c, :],
                                         start=(c == 0), stop=(c == NCH - 1))
                        i.then_inc(s_kmm, 1)
                        if g >= 3:
                            t.wait_ge(s_kv, g - 2)
                        pv = psV3[g % 3]
                        for c in range(NCH):
                            i = t.matmul(pv,
                                         wslotB[:, c, j * 128:(j + 1) * 128],
                                         xm_fm[:, c, :],
                                         start=(c == 0), stop=(c == NCH - 1))
                        i.then_inc(s_vmm, 1)
                    for j in range(NCH):
                        g = ti * NCH + j
                        if g >= 2:
                            t.wait_ge(s_rev, g - 1)
                        for c in range(NCH):
                            i = t.matmul(
                                psW[1][:, (g % 2) * 512:(g % 2 + 1) * 512],
                                wslotC[:, c, j * 128:(j + 1) * 128],
                                xm_fm[:, c, :],
                                start=(c == 0), stop=(c == NCH - 1))
                        i.then_inc(s_rmm, 1)

            @blk.scalar
            def _(sc):
                sc.wait_ge(s_wl, 32)
                sc.wait_ge(s_wlr, 16)
                for ti in range(NT):
                    for sl in range(NSL):
                        gs = ti * NSL + sl
                        sc.wait_ge(s_T, gs + 1)
                        if ti >= 1 and sl == 0:
                            sc.wait_ge(s_xm, ti)
                        if sl == NSL - 1 and ti >= 1:
                            sc.wait_ge(s_bnd, ti)
                        sc.copy(x_fm[:, 0:4, 1 + sl * 128:1 + (sl + 1) * 128],
                                psA[0][:].bitcast(F32R).rearrange(
                                    "p (a b) -> p a b", a=4))
                        sc.copy(x_fm[:, 4:8, 1 + sl * 128:1 + (sl + 1) * 128],
                                psA[1][:].bitcast(F32R).rearrange(
                                    "p (a b) -> p a b", a=4)).then_inc(s_ev, 1)
                    for j in range(NCH):
                        g = ti * NCH + j
                        sc.wait_ge(s_kmm, g + 1)
                        if g >= 3:
                            sc.wait_ge(s_scan, g - 2)
                        kb = k_sb[g % 3]
                        sc.activation(kb[:], psK3[g % 3], Act.Relu,
                                      bias=cpool[:, CB_60:CB_60 + 1],
                                      scale=cpool[:, CB_M1:CB_M1 + 1])
                        sc.drain()
                        sc.activation(kb[:], kb[:], Act.Exp,
                                      bias=cpool[:, CB_60:CB_60 + 1],
                                      scale=cpool[:, CB_M1:CB_M1 + 1]
                                      ).then_inc(s_k, 1)
                    for j in range(NCH):
                        g = ti * NCH + j
                        sc.wait_ge(s_rmm, g + 1)
                        if g >= 2:
                            sc.wait_ge(s_str[g % 2], 16 * (g // 2))
                        sc.copy(rb[g % 2][:],
                                psW[1][:, (g % 2) * 512:(g % 2 + 1) * 512]
                                .bitcast(F32R)).then_inc(s_rev, 1)

            @blk.vector
            def _(v):
                v.wait_ge(s_init, 4)
                for ti in range(NT):
                    v.wait_ge(s_ev, ti * NSL + NSL)
                    if ti == 0:
                        v.wait_ge(s_bndd, 16)
                    else:
                        v.wait_ge(s_rmm, ti * NCH)
                    for j in range(NCH + 1):
                        v.drain()
                        if j < NCH:
                            v.tensor_tensor(d1[j % 2][:],
                                            x_fm[:, j, 1:TILE + 1],
                                            x_fm[:, j, 0:TILE], Alu.subtract)
                        if j >= 1:
                            jj = j - 1
                            i = v.scalar_tensor_tensor(
                                xm_fm[:, jj, :], d1[jj % 2][:],
                                tm_c[:, jj:jj + 1],
                                x_fm[:, jj, 0:TILE], Alu.mult, Alu.add)
                    i.then_inc(s_xm, 1)
                    v.drain()
                    v.tensor_copy(x_fm[:, :, 0:1],
                                  x_fm[:, :, TILE:TILE + 1]).then_inc(s_bnd, 1)
                    g0 = ti * NCH
                    v.wait_ge(s_k, g0 + 1)
                    v.wait_ge(s_vmm, g0 + 1)

                    v.tensor_tensor(kv_sb[g0 % 3][:], k_sb[g0 % 3][:],
                                    psV3[g0 % 3], Alu.mult).then_inc(s_kv, 1)
                    v.tensor_copy(s_bufs[g0 % 3][:, 0:1], carry_kv[:, 0:1])
                    v.tensor_copy(sk_bufs[g0 % 3][:, 0:1], carry_k[:, 0:1])
                    for j in range(NCH + 1):
                        g = ti * NCH + j
                        v.drain()
                        if j < NCH:
                            wj = wdec_c[:, j:j + 1].broadcast_to([128, TILE])
                            v.tensor_tensor_scan(s_bufs[g % 3][:, 1:TILE + 1],
                                                 wj, kv_sb[g % 3][:],
                                                 s_bufs[g % 3][:, 0:1],
                                                 Alu.mult, Alu.add)
                            i = v.tensor_tensor_scan(sk_bufs[g % 3][:, 1:TILE + 1],
                                                 wj, k_sb[g % 3][:],
                                                 sk_bufs[g % 3][:, 0:1],
                                                 Alu.mult, Alu.add)
                        if j >= 1:
                            jj, gg = j - 1, g - 1
                            if gg >= 2:
                                v.wait_ge(s_stw[gg % 2], 32 * (gg // 2))
                            v.scalar_tensor_tensor(
                                wkv_sb[gg % 2][:], kv_sb[gg % 3][:],
                                etf_c[:, jj:jj + 1],
                                s_bufs[gg % 3][:, 0:TILE], Alu.mult, Alu.add)
                            v.scalar_tensor_tensor(
                                sumk_sb[gg % 2][:], k_sb[gg % 3][:],
                                etf_c[:, jj:jj + 1],
                                sk_bufs[gg % 3][:, 0:TILE], Alu.mult, Alu.add)
                            v.tensor_copy(carry_kv[:, jj:jj + 1],
                                          s_bufs[gg % 3][:, TILE:TILE + 1])
                            v.tensor_copy(carry_k[:, jj:jj + 1],
                                          sk_bufs[gg % 3][:, TILE:TILE + 1]
                                          ).then_inc(s_scan, 1)
                        if j + 1 < NCH:
                            gn = g + 1
                            v.wait_ge(s_k, gn + 1)
                            v.wait_ge(s_vmm, gn + 1)

                            v.tensor_tensor(kv_sb[gn % 3][:], k_sb[gn % 3][:],
                                            psV3[gn % 3], Alu.mult
                                            ).then_inc(s_kv, 1)
                            v.tensor_copy(s_bufs[gn % 3][:, 0:1],
                                          carry_kv[:, j + 1:j + 2])
                            v.tensor_copy(sk_bufs[gn % 3][:, 0:1],
                                          carry_k[:, j + 1:j + 2])

            @blk.gpsimd
            def _(g):
                for ti in range(NT):
                    for j in range(NCH):
                        gg = ti * NCH + j
                        g.wait_ge(s_scan, gg + 1)
                        g.dma_start(out=wkv_d[ti, j], in_=wkv_sb[gg % 2][:]
                                    ).then_inc(s_stw[gg % 2], 16)
                        g.dma_start(out=sumk_d[ti, j], in_=sumk_sb[gg % 2][:]
                                    ).then_inc(s_stw[gg % 2], 16)
                    for j in range(NCH):
                        gg = ti * NCH + j
                        g.wait_ge(s_rev, gg + 1)
                        g.dma_start(out=r_d[ti, j], in_=rb[gg % 2][:]
                                    ).then_inc(s_str[gg % 2], 16)

            if exchange:
                @blk.vector
                def _(v):
                    v.wait_ge(s_misc2, 32)
                    v.drain()
                    v.tensor_scalar(pay1[:, 0:NCH], carry_kv[:],
                                    flagc[:, 0:1], None, Alu.mult)
                    v.tensor_scalar(pay1[:, NCH:2 * NCH], carry_k[:],
                                    flagc[:, 0:1], None, Alu.mult
                                    ).then_inc(s_pay1, 1)

                @blk.gpsimd
                def _(g):
                    g.wait_ge(s_pay1, 1)
                    g.dma_start(out=cc1_in.ap(), in_=pay1[:]).then_inc(s_ccd, 16)
                    g.wait_ge(s_ccd, 16)
                    g.collective_compute(
                        "AllReduce", Alu.add,
                        replica_groups=[[0, 1], [2, 3], [4, 5], [6, 7]],
                        ins=[cc1_in.ap()], outs=[cc1_out.ap()],
                    ).then_inc(s_ccc, 1)
                    g.wait_ge(s_ccc, 1)
                    g.dma_start(out=rcv1[:], in_=cc1_out.ap()).then_inc(s_ccd, 16)

                @blk.vector
                def _(v):
                    v.wait_ge(s_ccd, 32)
                    v.tensor_scalar(i_kv[:], rcv1[:, 0:NCH], flagu[:, 0:1],
                                    None, Alu.mult)
                    v.tensor_scalar(i_k[:], rcv1[:, NCH:2 * NCH], flagu[:, 0:1],
                                    None, Alu.mult).then_inc(s_ikv, 1)

        ph1.close()
        ph2a = ExitStack()
        rb_in = [ph2a.enter_context(nc.sbuf_tensor(f"rb_in{i}", [128, TILE], F32R)) for i in range(3)]
        wkv_in = [ph2a.enter_context(nc.sbuf_tensor(f"wkv_in{i}", [128, TILE], F32)) for i in range(3)]
        sumk_in = [ph2a.enter_context(nc.sbuf_tensor(f"sumk_in{i}", [128, TILE], F32)) for i in range(3)]
        e_sb = [ph2a.enter_context(nc.sbuf_tensor(f"e_sb{i}", [128, TILE], F32)) for i in range(3)]
        rwkv = ph2a.enter_context(nc.sbuf_tensor("rwkv", [128, NCH, TILE], F32R))
        y1p = [ph2a.enter_context(nc.sbuf_tensor(f"y1p{i}", [128, C], F32R)) for i in range(4)]
        if True:

            @blk.scalar
            def _(sc):
                sc.wait_ge(s_kmm, NT * NCH)
                sc.dma_start(out=wslotA[:], in_=w_d["fWrT"].rearrange(
                    "(a p) c -> p a c", p=128)).then_inc(s_wfr, 16)
                sc.wait_ge(s_vmm, NT * NCH)
                sc.dma_start(out=wslotB[:], in_=w_d["fWkT"].rearrange(
                    "(a p) c -> p a c", p=128)).then_inc(s_wfk, 16)
                sc.wait_ge(s_rmm, NT * NCH)
                sc.dma_start(out=wslotC[:], in_=w_d["WoT"].rearrange(
                    "(a p) c -> p a c", p=128)).then_inc(s_wlo, 16)

            @blk.sync
            def _(sync):
                sync.wait_ge(s_str[0], 16 * NT * NCH // 2)
                sync.wait_ge(s_str[1], 16 * NT * NCH // 2)
                sync.wait_ge(s_stw[0], 32 * 4 * NT)
                sync.wait_ge(s_stw[1], 32 * 4 * NT)
                for ti in range(NT):
                    for j in range(NCH):
                        g = ti * NCH + j
                        if g >= 3:
                            sync.wait_ge(s_e, g - 2)
                        sync.dma_start(out=rb_in[g % 3][:], in_=r_d[ti, j]
                                       ).then_inc(s_l2r[g % 3], 16)
                        if g >= 3:
                            sync.wait_ge(s_rw, g - 2)
                        sync.dma_start(out=wkv_in[g % 3][:], in_=wkv_d[ti, j]
                                       ).then_inc(s_l2w[g % 3], 16)
                        sync.dma_start(out=sumk_in[g % 3][:], in_=sumk_d[ti, j]
                                       ).then_inc(s_l2w[g % 3], 16)
                    for sl in range(NSL):
                        gs = ti * NSL + sl
                        if gs >= 4:
                            sync.wait_ge(s_sty[gs % 4], 16 * (gs // 4))
                        sync.dma_start(out=y1p[gs % 4][:],
                                       in_=xr_view[:, gs, :]
                                       ).then_inc(s_l2x[gs % 4], 16)

            @blk.tensor
            def _(t):
                t.wait_ge(s_wlo, 16)
                t.wait_ge(s_rev, NT * NCH)

                def wo_proj(t, ti):
                    t.wait_ge(s_rw, ti * NCH + NCH)
                    for sl in range(NSL):
                        gs = ti * NSL + sl
                        if gs >= 2:
                            t.wait_ge(s_y1p, gs - 1)
                        for j in range(NCH):
                            for h in range(2):
                                i = t.matmul(
                                    psW[sl % 2][:, h * 512:(h + 1) * 512],
                                    rwkv[:, j, sl * 128:(sl + 1) * 128],
                                    wslotC[:, j, h * 512:(h + 1) * 512],
                                    start=(j == 0), stop=(j == NCH - 1))
                        i.then_inc(s_omm, 1)

                for ti in range(NT):
                    wo_proj(t, ti)

            @blk.scalar
            def _(sc):
                for ti in range(NT):
                    for j in range(NCH):
                        g = ti * NCH + j
                        sc.wait_ge(s_l2r[g % 3], 16 * (g // 3 + 1))
                        if g >= 3:
                            sc.wait_ge(s_rw, g - 2)
                        sc.activation(e_sb[g % 3][:],
                                      rb_in[g % 3][:].bitcast(F32), Act.Exp,
                                      scale=cpool[:, CB_M1:CB_M1 + 1]
                                      ).then_inc(s_e, 1)
                        sc.wait_ge(s_den, g + 1)
                        sc.drain()
                        sc.activation(e_sb[g % 3][:], e_sb[g % 3][:],
                                      Act.Ln,
                                      bias=cpool[:, CB_EPS16:CB_EPS16 + 1])
                        sc.drain()
                        sc.activation(e_sb[g % 3][:], e_sb[g % 3][:],
                                      Act.Exp, scale=cpool[:, CB_M1:CB_M1 + 1]
                                      ).then_inc(s_inv, 1)
                    for sl in range(NSL):
                        gs = ti * NSL + sl
                        sc.wait_ge(s_y1p, gs + 1)
                        sc.drain()
                        sc.activation(sq_junk[:],
                                      y1p[gs % 4][:, 0:512].bitcast(F32),
                                      Act.Square,
                                      accum_out=sumsq[:, 2 * sl:2 * sl + 1])
                        sc.drain()
                        sc.activation(sq_junk[:],
                                      y1p[gs % 4][:, 512:C].bitcast(F32),
                                      Act.Square,
                                      accum_out=sumsq[:, 2 * sl + 1:2 * sl + 2]
                                      ).then_inc(s_sq, 1)
                    sc.wait_ge(s_var, ti + 1)
                    sc.activation(is_col[:], v_col[:], Act.Ln,
                                  bias=cpool[:, CB_LNEPS:CB_LNEPS + 1])
                    sc.drain()
                    sc.activation(is_col[:], is_col[:], Act.Exp,
                                  scale=cpool[:, CB_M05:CB_M05 + 1]
                                  ).then_inc(s_rsq, 1)

            @blk.vector
            def _(v):
                if exchange:
                    v.wait_ge(s_ikv, 1)
                v.wait_ge(s_dx3, 1)
                for ti in range(NT):
                    if ti >= 1:
                        v.wait_ge(s_omm, ti * NSL)
                    for j in range(NCH):
                        g = ti * NCH + j
                        v.wait_ge(s_l2w[g % 3], 32 * (g // 3 + 1))
                        v.wait_ge(s_e, g + 1)
                        if g >= 3:
                            v.wait_ge(s_inv, g - 2)
                        if exchange:
                            v.scalar_tensor_tensor(
                                wkv_in[g % 3][:], dexpt[:, j, :].bitcast(F32),
                                i_kv[:, j:j + 1], wkv_in[g % 3][:],
                                Alu.mult, Alu.add)
                            v.scalar_tensor_tensor(
                                sumk_in[g % 3][:], dexpt[:, j, :].bitcast(F32),
                                i_k[:, j:j + 1], sumk_in[g % 3][:],
                                Alu.mult, Alu.add)
                            v.drain()
                        v.scalar_tensor_tensor(
                            e_sb[g % 3][:], e_sb[g % 3][:], 1.0,
                            sumk_in[g % 3][:], Alu.add, Alu.mult
                            ).then_inc(s_den, 1)
                        v.wait_ge(s_inv, g + 1)
                        v.tensor_tensor(rwkv[:, j, :], wkv_in[g % 3][:],
                                        e_sb[g % 3][:], Alu.mult
                                        ).then_inc(s_rw, 1)

                    if exchange:
                        v.tensor_tensor(i_kv[:], i_kv[:], w512_c[:], Alu.mult)
                        v.tensor_tensor(i_k[:], i_k[:], w512_c[:], Alu.mult)
                        v.drain()
                    for sl in range(NSL):
                        gs = ti * NSL + sl
                        v.wait_ge(s_omm, gs + 1)
                        v.wait_ge(s_l2x[gs % 4], 16 * (gs // 4 + 1))
                        v.scalar_tensor_tensor(
                            y1p[gs % 4][:], psW[sl % 2][:], 1.0,
                            y1p[gs % 4][:].bitcast(F32),
                            Alu.mult, Alu.add,
                            accum_out=sums[:, sl:sl + 1]).then_inc(s_y1p, 1)
                    v.wait_ge(s_sq, ti * NSL + NSL)
                    v.drain()
                    v.tensor_scalar(m_col[:], sums[:], 1.0 / C, None, Alu.mult)
                    v.tensor_tensor(v_col[:], sumsq[:].rearrange(
                        "p (a b) -> p a b", b=2)[:, :, 0],
                        sumsq[:].rearrange("p (a b) -> p a b", b=2)[:, :, 1],
                        Alu.add)
                    v.drain()
                    v.tensor_tensor(b_col[:], m_col[:], m_col[:], Alu.mult)
                    v.drain()
                    v.scalar_tensor_tensor(v_col[:], v_col[:], 1.0 / C,
                                           b_col[:], Alu.mult, Alu.subtract
                                           ).then_inc(s_var, 1)
                    v.wait_ge(s_rsq, ti + 1)
                    v.scalar_tensor_tensor(b_col[:], m_col[:], -1.0,
                                           is_col[:], Alu.mult, Alu.mult)
                    v.drain()
                    for sl in range(NSL):
                        gs = ti * NSL + sl
                        v.tensor_scalar(y1p[gs % 4][:],
                                        y1p[gs % 4][:].bitcast(F32),
                                        is_col[:, sl:sl + 1],
                                        b_col[:, sl:sl + 1],
                                        Alu.mult, Alu.add).then_inc(s_y1w, 1)


            @blk.gpsimd
            def _(g):
                for ti in range(NT):
                    for sl in range(NSL):
                        gs = ti * NSL + sl
                        g.wait_ge(s_y1w, gs + 1)
                        g.dma_start(out=y1_d[ti, sl], in_=y1p[gs % 4][:]
                                    ).then_inc(s_sty[gs % 4], 16)

            if exchange:
                LASTS = (NT * NSL - 1) % 4

                @blk.vector
                def _(v):
                    v.wait_ge(s_y1w, NT * NSL)
                    v.drain()
                    v.tensor_scalar(pay2[:], y1p[LASTS][:].bitcast(F32),
                                    flagc[:, 0:1], None, Alu.mult
                                    ).then_inc(s_pay2, 1)

                @blk.gpsimd
                def _(g):
                    g.wait_ge(s_pay2, 1)
                    g.dma_start(out=cc2_in.ap(), in_=pay2[127:128, :]
                                ).then_inc(s_ccd, 16)
                    g.wait_ge(s_ccd, 48)
                    g.collective_compute(
                        "AllReduce", Alu.add,
                        replica_groups=[[0, 1], [2, 3], [4, 5], [6, 7]],
                        ins=[cc2_in.ap()], outs=[cc2_out.ap()],
                    ).then_inc(s_ccc, 1)
                    g.wait_ge(s_ccc, 2)

                @blk.gpsimd
                def _(g):
                    g.dma_start(out=rcv2fm[:],
                                in_=cc2_out.ap().rearrange(
                                    "o (a p) -> (o p) a", p=128),
                                allow_slow_non_contiguous=True
                                ).then_inc(s_ccd, 16)

                @blk.vector
                def _(v):
                    v.wait_ge(s_ccd, 64)
                    v.tensor_scalar(ycol0[:], rcv2fm[:],
                                    flagu[:, 0:1], None, Alu.mult
                                    ).then_inc(s_ycol, 1)

        ph2a.close()
        ph2b = ExitStack()
        d_buf = [ph2b.enter_context(nc.sbuf_tensor(f"d_buf{i}", [128, TILE], F32))
                 for i in range(2)]
        y1t = [ph2b.enter_context(nc.sbuf_tensor(f"y1t{i}", [128, C], F32R))
               for i in range(2)]
        y1r = [ph2b.enter_context(nc.sbuf_tensor(f"y1r{i}", [128, C], F32R))
               for i in range(2)]
        y1fm = ph2b.enter_context(nc.sbuf_tensor("y1fm", [128, NCH, TILE + 1], F32))
        ybcol = ph2b.enter_context(nc.sbuf_tensor("ybcol", [128, NCH], F32))
        xr_fm = ph2b.enter_context(nc.sbuf_tensor("xr_fm", [128, NCH, TILE], F32R))
        xk_fm = dexpt
        kact = ph2b.enter_context(nc.sbuf_tensor("kact", [128, NCH, TILE], F32R))
        sigf = [ph2b.enter_context(nc.sbuf_tensor(f"sigf{i}", [128, C], F32))
                for i in range(2)]
        y2p = [ph2b.enter_context(nc.sbuf_tensor(f"y2p{i}", [128, C], F32))
               for i in range(2)]
        if True:
            @blk.sync
            def _(sync):
                for _q in range(4):
                    sync.wait_ge(s_sty[_q], 16 * NT)
                sync.wait_ge(s_y1w, NT * NSL)

                def y1t_loads(sync, ti):
                    for sl in range(NSL):
                        gs = ti * NSL + sl
                        if gs >= 2:
                            sync.wait_ge(s_T3, gs - 1)
                        sync.dma_start(out=y1t[gs % 2][:], in_=y1_d[ti, sl]
                                       ).then_inc(s_l3t[gs % 2], 16)

                y1t_loads(sync, 0)
                for ti in range(NT):
                    if ti + 1 < NT:
                        y1t_loads(sync, ti + 1)
                    for sl in range(NSL):
                        gs = ti * NSL + sl
                        if gs >= 2:
                            sync.wait_ge(s_pr, gs - 1)
                        sync.dma_start(out=y1r[gs % 2][:], in_=y1_d[ti, sl]
                                       ).then_inc(s_l3r[gs % 2], 16)

            @blk.vector
            def _(v):
                i = v.memset(ybcol[:], 0.0)
                if exchange:
                    v.wait_ge(s_ycol, 1)
                    v.drain()
                    i = v.tensor_copy(ybcol[:], ycol0[:])
                i.then_inc(s_bnd3, 1)
                v.drain()

            @blk.tensor
            def _(t):
                t.wait_ge(s_wfk, 16)

                def transp(t, k, sl):
                    gs = k * NSL + sl
                    t.wait_ge(s_l3t[gs % 2], 16 * (gs // 2 + 1))
                    if gs >= 1:
                        t.wait_ge(s_ev3, gs)
                    for j in range(NCH):
                        dst = psA[0] if j < 4 else psA[1]
                        i = t.transpose(
                            dst[:].bitcast(F32R)[:, (j % 4) * 128:(j % 4 + 1) * 128],
                            y1t[gs % 2][:, j * 128:(j + 1) * 128],
                            ident[:])
                    i.then_inc(s_T3, 1)

                def rf_mm(t, ti, sl):
                    gs = ti * NSL + sl
                    if gs == 0:
                        t.wait_ge(s_wfr, 16)
                    if gs >= 2:
                        t.wait_ge(s_pr, gs - 1)
                    for c in range(NCH):
                        for h in range(2):
                            i = t.matmul(
                                psW[sl % 2][:, h * 512:(h + 1) * 512],
                                xr_fm[:, c, sl * 128:(sl + 1) * 128],
                                wslotA[:, c, h * 512:(h + 1) * 512],
                                start=(c == 0), stop=(c == NCH - 1))
                    i.then_inc(s_rfmm, 1)

                def vo_mm(t, ti, sl):
                    gs = ti * NSL + sl
                    t.wait_ge(s_kact, ti * NCH + NCH)
                    if gs == 0:
                        t.wait_ge(s_wfv, 16)
                    t.wait_ge(s_fe, gs + 1)
                    for j in range(NCH):
                        for h in range(2):
                            i = t.matmul(
                                psW[sl % 2][:, h * 512:(h + 1) * 512],
                                kact[:, j, sl * 128:(sl + 1) * 128],
                                wslotC[:, j, h * 512:(h + 1) * 512],
                                start=(j == 0), stop=(j == NCH - 1))
                    i.then_inc(s_vomm, 1)

                for sl in range(NSL):
                    transp(t, 0, sl)
                for ti in range(NT):
                    nxt = ti + 1 < NT
                    if nxt:
                        transp(t, ti + 1, 0)
                    t.wait_ge(s_mixk, ti * NCH + NCH)
                    for j in range(NCH):
                        g = ti * NCH + j
                        if g >= 2:
                            t.wait_ge(s_kact, g - 1)
                        for c in range(NCH):
                            i = t.matmul(psB[j % 2][:],
                                         wslotB[:, c, j * 128:(j + 1) * 128],
                                         xk_fm[:, c, :],
                                         start=(c == 0), stop=(c == NCH - 1))
                        i.then_inc(s_kfmm, 1)
                    t.wait_ge(s_mixr, ti * NCH + NCH)
                    for sl in range(NSL):
                        if nxt and sl + 1 < NSL:
                            transp(t, ti + 1, sl + 1)
                        rf_mm(t, ti, sl)
                        vo_mm(t, ti, sl)

            @blk.scalar
            def _(sc):
                sc.wait_ge(s_omm, NT * NSL)
                sc.dma_start(out=wslotC[:], in_=w_d["fWvT"].rearrange(
                    "(a p) c -> p a c", p=128)).then_inc(s_wfv, 16)

                def evac(sc, k, sl):
                    gs = k * NSL + sl
                    sc.wait_ge(s_T3, gs + 1)
                    if k >= 1 and sl == 0:
                        sc.wait_ge(s_mixr, k * NCH)
                    if sl == NSL - 1:
                        sc.wait_ge(s_bnd3, k + 1)
                    sc.copy(y1fm[:, 0:4, 1 + sl * 128:1 + (sl + 1) * 128],
                            psA[0][:].bitcast(F32R).rearrange(
                                "p (a b) -> p a b", a=4))
                    sc.copy(y1fm[:, 4:8, 1 + sl * 128:1 + (sl + 1) * 128],
                            psA[1][:].bitcast(F32R).rearrange(
                                "p (a b) -> p a b", a=4)).then_inc(s_ev3, 1)

                def sig_block(sc, ti, sl):
                    gs = ti * NSL + sl
                    sc.wait_ge(s_rfmm, gs + 1)
                    sf = sigf[sl % 2]
                    if gs >= 2:
                        sc.wait_ge(s_pr, gs - 1)
                    sc.activation(sf[:], psW[sl % 2][:], Act.Exp,
                                  scale=cpool[:, CB_M1:CB_M1 + 1]
                                  ).then_inc(s_fe, 1)
                    sc.drain()
                    sc.activation(sf[:], sf[:], Act.Ln,
                                  bias=cpool[:, CB_1:CB_1 + 1])
                    sc.drain()
                    sc.activation(sf[:], sf[:], Act.Exp,
                                  scale=cpool[:, CB_M1:CB_M1 + 1]
                                  ).then_inc(s_sigf, 1)

                def sq_rsq(sc, ti, sl):
                    gs = ti * NSL + sl
                    sc.wait_ge(s_pr, gs + 1)
                    sc.drain()
                    sc.activation(sq_junk[:], y2p[sl % 2][:, 0:512],
                                  Act.Square,
                                  accum_out=sumsq[:, 2 * sl:2 * sl + 1])
                    sc.drain()
                    sc.activation(sq_junk[:], y2p[sl % 2][:, 512:C],
                                  Act.Square,
                                  accum_out=sumsq[:, 2 * sl + 1:2 * sl + 2]
                                  ).then_inc(s_sq3, 1)
                    sc.wait_ge(s_var3, gs + 1)
                    sc.activation(is_col[:, sl:sl + 1], v_col[:, sl:sl + 1],
                                  Act.Ln,
                                  bias=cpool[:, CB_LNEPS:CB_LNEPS + 1])
                    sc.drain()
                    sc.activation(is_col[:, sl:sl + 1], is_col[:, sl:sl + 1],
                                  Act.Exp, scale=cpool[:, CB_M05:CB_M05 + 1]
                                  ).then_inc(s_rsq3, 1)

                for sl in range(NSL):
                    evac(sc, 0, sl)
                for ti in range(NT):
                    nxt = ti + 1 < NT
                    for j in range(NCH):
                        g = ti * NCH + j
                        sc.wait_ge(s_kfmm, g + 1)
                        sc.drain()
                        sc.activation(sq_junk[:], psB[j % 2][:], Act.Relu)
                        sc.drain()
                        sc.activation(kact[:, j, :], sq_junk[:], Act.Square
                                      ).then_inc(s_kact, 1)
                    for sl in range(NSL):
                        if nxt:
                            evac(sc, ti + 1, sl)
                        sig_block(sc, ti, sl)
                        if sl >= 1:
                            sq_rsq(sc, ti, sl - 1)
                    sq_rsq(sc, ti, NSL - 1)

            @blk.vector
            def _(v):
                i = v.memset(ybcol[:], 0.0)
                if exchange:
                    v.wait_ge(s_ycol, 1)
                    v.drain()
                    i = v.tensor_copy(ybcol[:], ycol0[:])
                i.then_inc(s_bnd3, 1)
                v.drain()

                def xk_pass(v, k):
                    if k >= 1:
                        v.wait_ge(s_kfmm, k * NCH)
                    if k == 0:
                        v.tensor_copy(y1fm[:, :, 0:1],
                                      ybcol[:].rearrange("p (a b) -> p a b", b=1))
                    v.wait_ge(s_ev3, k * NSL + NSL)
                    for j in range(NCH + 1):
                        v.drain()
                        if j < NCH:
                            v.tensor_tensor(d_buf[j % 2][:],
                                            y1fm[:, j, 1:TILE + 1],
                                            y1fm[:, j, 0:TILE], Alu.subtract)
                        if j >= 1:
                            jj = j - 1
                            v.scalar_tensor_tensor(
                                xk_fm[:, jj, :], d_buf[jj % 2][:],
                                muk1_c[:, jj:jj + 1],
                                y1fm[:, jj, 1:TILE + 1], Alu.mult, Alu.add
                                ).then_inc(s_mixk, 1)
                    v.drain()

                def xr_pass(v, k):
                    if k >= 1:
                        v.wait_ge(s_rfmm, k * NSL)
                    for j in range(NCH + 1):
                        v.drain()
                        if j < NCH:
                            v.tensor_tensor(d_buf[j % 2][:],
                                            y1fm[:, j, 1:TILE + 1],
                                            y1fm[:, j, 0:TILE], Alu.subtract)
                        if j >= 1:
                            jj = j - 1
                            v.scalar_tensor_tensor(
                                xr_fm[:, jj, :], d_buf[jj % 2][:],
                                mur1_c[:, jj:jj + 1],
                                y1fm[:, jj, 1:TILE + 1], Alu.mult, Alu.add
                                ).then_inc(s_mixr, 1)
                    v.tensor_copy(y1fm[:, :, 0:1],
                                  y1fm[:, :, TILE:TILE + 1]).then_inc(s_bnd3, 1)
                    v.drain()

                def prod_ln(v, ti, sl):
                    gs = ti * NSL + sl
                    v.wait_ge(s_vomm, gs + 1)
                    v.wait_ge(s_sigf, gs + 1)
                    v.wait_ge(s_l3r[gs % 2], 16 * (gs // 2 + 1))
                    if gs >= 2:
                        v.wait_ge(s_sto[gs % 2], 16 * (gs // 2))
                    v.tensor_tensor(y2p[sl % 2][:], psW[sl % 2][:],
                                    sigf[sl % 2][:], Alu.mult)
                    v.drain()
                    v.scalar_tensor_tensor(
                        y2p[sl % 2][:], y2p[sl % 2][:], 1.0,
                        y1r[gs % 2][:].bitcast(F32),
                        Alu.mult, Alu.add,
                        accum_out=sums[:, sl:sl + 1]).then_inc(s_pr, 1)
                    v.wait_ge(s_sq3, gs + 1)
                    v.drain()
                    v.tensor_scalar(m_col[:, sl:sl + 1], sums[:, sl:sl + 1],
                                    1.0 / C, None, Alu.mult)
                    v.tensor_tensor(v_col[:, sl:sl + 1],
                                    sumsq[:, 2 * sl:2 * sl + 1],
                                    sumsq[:, 2 * sl + 1:2 * sl + 2], Alu.add)
                    v.drain()
                    v.tensor_tensor(b_col[:, sl:sl + 1], m_col[:, sl:sl + 1],
                                    m_col[:, sl:sl + 1], Alu.mult)
                    v.drain()
                    v.scalar_tensor_tensor(v_col[:, sl:sl + 1],
                                           v_col[:, sl:sl + 1], 1.0 / C,
                                           b_col[:, sl:sl + 1],
                                           Alu.mult, Alu.subtract
                                           ).then_inc(s_var3, 1)
                    v.wait_ge(s_rsq3, gs + 1)
                    v.scalar_tensor_tensor(b_col[:, sl:sl + 1],
                                           m_col[:, sl:sl + 1], -1.0,
                                           is_col[:, sl:sl + 1],
                                           Alu.mult, Alu.mult)
                    v.drain()
                    v.tensor_scalar(y2p[sl % 2][:], y2p[sl % 2][:],
                                    is_col[:, sl:sl + 1],
                                    b_col[:, sl:sl + 1],
                                    Alu.mult, Alu.add).then_inc(s_ow, 1)

                xk_pass(v, 0)
                xr_pass(v, 0)
                for ti in range(NT):
                    nxt = ti + 1 < NT
                    prod_ln(v, ti, 0)
                    prod_ln(v, ti, 1)
                    if nxt:
                        xk_pass(v, ti + 1)
                    prod_ln(v, ti, 2)
                    prod_ln(v, ti, 3)
                    if nxt:
                        xr_pass(v, ti + 1)

            @blk.gpsimd
            def _(g):
                for ti in range(NT):
                    for sl in range(NSL):
                        gs = ti * NSL + sl
                        g.wait_ge(s_ow, gs + 1)
                        g.dma_start(out=out_d[gs * 128:(gs + 1) * 128, :],
                                    in_=y2p[sl % 2][:]).then_inc(s_sto[gs % 2], 16)
                g.wait_ge(s_sto[0], 16 * 2 * NT)
                g.wait_ge(s_sto[1], 16 * 2 * NT)
        ph2b.close()
    return nc


_PROG_CACHE = {}


def _get_prog(T_c, exchange):
    key = (T_c, exchange)
    if key not in _PROG_CACHE:
        _PROG_CACHE[key] = build(T_c, exchange)
    return _PROG_CACHE[key]


def _fm(vec):
    return np.ascontiguousarray(np.asarray(vec, np.float32).reshape(NCH, 128).T)


def kernel(L0_attn_x, L0_attn_time_decay, L0_attn_time_first, L0_attn_time_mix,
           L0_attn_Wk, L0_attn_Wv, L0_attn_Wr, L0_attn_Wo,
           L0_ffn_W_r, L0_ffn_W_k, L0_ffn_W_v, L0_ffn_mu_r, L0_ffn_mu_k):
    x = np.ascontiguousarray(np.asarray(L0_attn_x, np.float32))
    B, T, _ = x.shape
    wT = {"WkT": L0_attn_Wk, "WvT": L0_attn_Wv, "WrT": L0_attn_Wr,
          "WoT": L0_attn_Wo, "fWrT": L0_ffn_W_r, "fWkT": L0_ffn_W_k,
          "fWvT": L0_ffn_W_v}
    wT = {k: np.ascontiguousarray(np.asarray(v, np.float32).T)
          for k, v in wT.items()}
    base = {
        "tm_fm": _fm(np.asarray(L0_attn_time_mix, np.float32).reshape(C)),
        "mur_fm": _fm(L0_ffn_mu_r),
        "muk_fm": _fm(L0_ffn_mu_k),
        "td_fm": _fm(np.repeat(np.asarray(L0_attn_time_decay, np.float32)[:, 0], 64)),
        "tf_fm": _fm(np.repeat(np.asarray(L0_attn_time_first, np.float32)[:, 0], 64)),
    }
    base.update(wT)
    zrow = np.zeros((128, NCH), np.float32)

    if not EXCHANGE:
        nc = _get_prog(T, False)
        in_maps = [dict(base, x=x[core % B], xprev_fm=zrow) for core in range(8)]
        res = run_bass_kernel_spmd(nc, in_maps, core_ids=list(range(8)))
        return np.stack([res.results[b]["out"] for b in range(B)], axis=0)
    else:
        T_c = T // 2
        nc = _get_prog(T_c, True)
        one = np.ones((128, 1), np.float32)
        zero = np.zeros((128, 1), np.float32)
        in_maps = []
        for core in range(8):
            b, half = core // 2, core % 2
            t0 = half * T_c
            in_maps.append(dict(
                base,
                x=np.ascontiguousarray(x[b, t0:t0 + T_c]),
                xprev_fm=(zrow if half == 0 else _fm(x[b, t0 - 1])),
                flagc=(one if half == 0 else zero),
                flagu=(zero if half == 0 else one)))
        res = run_bass_kernel_spmd(nc, in_maps, core_ids=list(range(8)))
        return np.stack(
            [np.concatenate([res.results[2 * b]["out"],
                             res.results[2 * b + 1]["out"]], axis=0)
             for b in range(B)], axis=0)


# version 16
# speedup vs baseline: 1.0954x; 1.0954x over previous
import numpy as np
import concourse.bass as bass
import concourse.mybir as mybir
from concourse.bass_utils import run_bass_kernel_spmd

F32 = mybir.dt.float32
F32R = mybir.dt.float32r
Act = mybir.ActivationFunctionType
Alu = mybir.AluOpType

C = 1024
NCH = C // 128
TILE = 512
NSL = TILE // 128

EXCHANGE = True


def build(T_c, exchange):
    NT = T_c // TILE
    nc = bass.Bass("TRN2", target_bir_lowering=False,
                   **({"num_devices": 8} if exchange else {}))

    x_d = nc.dram_tensor("x", [T_c, C], F32R, kind="ExternalInput").ap()
    xprev_d = nc.dram_tensor("xprev_fm", [128, NCH], F32, kind="ExternalInput").ap()
    w_d = {n: nc.dram_tensor(n, [C, C], F32R, kind="ExternalInput").ap()
           for n in ["WkT", "WvT", "WrT", "WoT", "fWrT", "fWkT", "fWvT"]}
    tm_d = nc.dram_tensor("tm_fm", [128, NCH], F32, kind="ExternalInput").ap()
    mur_d = nc.dram_tensor("mur_fm", [128, NCH], F32, kind="ExternalInput").ap()
    muk_d = nc.dram_tensor("muk_fm", [128, NCH], F32, kind="ExternalInput").ap()
    td_d = nc.dram_tensor("td_fm", [128, NCH], F32, kind="ExternalInput").ap()
    tf_d = nc.dram_tensor("tf_fm", [128, NCH], F32, kind="ExternalInput").ap()
    out_d = nc.dram_tensor("out", [T_c, C], F32, kind="ExternalOutput").ap()

    xm_d = nc.dram_tensor("xm_s", [NT, 128, NCH, TILE], F32R, kind="Internal").ap()
    wkv_d = nc.dram_tensor("wkv_s", [NT, NCH, 128, TILE], F32, kind="Internal").ap()
    sumk_d = nc.dram_tensor("sumk_s", [NT, NCH, 128, TILE], F32, kind="Internal").ap()
    y1_d = nc.dram_tensor("y1_s", [NT, NSL, 128, C], F32R, kind="Internal").ap()

    if exchange:
        flagc_d = nc.dram_tensor("flagc", [128, 1], F32, kind="ExternalInput").ap()
        flagu_d = nc.dram_tensor("flagu", [128, 1], F32, kind="ExternalInput").ap()
        cc1_in = nc.dram_tensor("cc1_in", [128, 2 * NCH], F32, kind="Internal")
        cc1_out = nc.dram_tensor("cc1_out", [128, 2 * NCH], F32, kind="Internal")
        cc2_in = nc.dram_tensor("cc2_in", [1, C], F32, kind="Internal")
        cc2_out = nc.dram_tensor("cc2_out", [1, C], F32, kind="Internal")

    xr_view = x_d.rearrange("(a p) c -> p a c", p=128)

    def st1(ti, j):
        return 272 * ti + 16 + 32 * (j + 1)
    ST1_ALL = 272 * NT
    ST2_ALL = ST1_ALL + 16 * NT * NSL
    def ld2(ti, part):
        return 16 * (ti * (NSL + 1 + 2 * NCH) + part)

    from contextlib import ExitStack
    with ExitStack() as ctx:
        sbuf = lambda *a: ctx.enter_context(nc.sbuf_tensor(*a))

        cpool = sbuf("cpool", [128, 8], F32)
        CB_60, CB_M1, CB_1, CB_EPS16, CB_LNEPS, CB_M05, CB_M512 = range(7)
        ident = sbuf("ident", [128, 128], F32R)
        tm_c = sbuf("tm_c", [128, NCH], F32)
        mur_c = sbuf("mur_c", [128, NCH], F32)
        muk_c = sbuf("muk_c", [128, NCH], F32)
        mur1_c = sbuf("mur1_c", [128, NCH], F32)
        muk1_c = sbuf("muk1_c", [128, NCH], F32)
        td_c = sbuf("td_c", [128, NCH], F32)
        tf_c = sbuf("tf_c", [128, NCH], F32)
        dec_c = sbuf("dec_c", [128, NCH], F32)
        wdec_c = sbuf("wdec_c", [128, NCH], F32)
        etf_c = sbuf("etf_c", [128, NCH], F32)
        w512_c = sbuf("w512_c", [128, NCH], F32)
        dexpt = sbuf("dexpt", [128, NCH, TILE], F32R)
        carry_kv = sbuf("carry_kv", [128, NCH], F32)
        carry_k = sbuf("carry_k", [128, NCH], F32)
        i_kv = sbuf("i_kv", [128, NCH], F32)
        i_k = sbuf("i_k", [128, NCH], F32)
        sq_junk = sbuf("sq_junk", [128, 512], F32)
        sums = sbuf("sums", [128, NSL], F32)
        sumsq = sbuf("sumsq", [128, 2 * NSL], F32)
        m_col = sbuf("m_col", [128, NSL], F32)
        v_col = sbuf("v_col", [128, NSL], F32)
        is_col = sbuf("is_col", [128, NSL], F32)
        b_col = sbuf("b_col", [128, NSL], F32)
        wslotA = sbuf("wslotA", [128, NCH, C], F32R)
        wslotB = sbuf("wslotB", [128, NCH, C], F32R)
        wslotC = sbuf("wslotC", [128, NCH, C], F32R)
        if exchange:
            flagc = sbuf("flagc_s", [128, 1], F32)
            flagu = sbuf("flagu_s", [128, 1], F32)
            pay1 = sbuf("pay1", [128, 2 * NCH], F32)
            rcv1 = sbuf("rcv1", [128, 2 * NCH], F32)
            pay2 = sbuf("pay2", [128, C], F32)
            rcv2fm = sbuf("rcv2fm", [128, NCH], F32)
            ycol0 = sbuf("ycol0", [128, NCH], F32)

        psA = [ctx.enter_context(nc.psum_tensor(f"psA{i}", [128, 512], F32))
               for i in range(2)]
        psB = [ctx.enter_context(nc.psum_tensor(f"psB{i}", [128, 512], F32))
               for i in range(2)]
        psW = [ctx.enter_context(nc.psum_tensor(f"psW{i}", [128, C], F32))
               for i in range(2)]
        psK3 = [psA[0][:], psA[1][:], psW[0][:, 0:512]]
        psV3 = [psB[0][:], psB[1][:], psW[0][:, 512:C]]

        S = lambda name: ctx.enter_context(nc.semaphore(name))
        s_init = S("s_init")
        s_wl = S("s_wl")
        s_wlo = S("s_wlo")
        s_wlr = S("s_wlr")
        s_wfk = S("s_wfk")
        s_wfr = S("s_wfr")
        s_wfv = S("s_wfv")
        s_xld = [S("s_xld0"), S("s_xld1")]
        s_T = S("s_T")
        s_ev = S("s_ev")
        s_bnd = S("s_bnd")
        s_bndd = S("s_bndd")
        s_xm = S("s_xm")
        s_kmm = S("s_kmm")
        s_k = S("s_k")
        s_vmm = S("s_vmm")
        s_kv = S("s_kv")
        s_scan = S("s_scan")
        s_tts = S("s_tts")
        s_gw = S("s_gw")
        s_stx = S("s_stx")
        s_stw = [S("s_stw0"), S("s_stw1")]
        s_sty = [S("s_sty0"), S("s_sty1"), S("s_sty2"), S("s_sty3")]
        s_sto = [S("s_sto0"), S("s_sto1")]
        s_l2x = [S("s_l2x0"), S("s_l2x1"), S("s_l2x2"), S("s_l2x3")]
        s_l2m = [S("s_l2m0"), S("s_l2m1")]
        s_l2w = [S("s_l2w0"), S("s_l2w1"), S("s_l2w2")]
        s_rmm = S("s_rmm")
        s_e = S("s_e")
        s_den = S("s_den")
        s_inv = S("s_inv")
        s_rw = S("s_rw")
        s_omm = S("s_omm")
        s_y1p = S("s_y1p")
        s_sq = S("s_sq")
        s_var = S("s_var")
        s_rsq = S("s_rsq")
        s_y1w = S("s_y1w")
        s_l3t = [S("s_l3t0"), S("s_l3t1")]
        s_l3r = [S("s_l3r0"), S("s_l3r1")]
        s_T3 = S("s_T3")
        s_ev3 = S("s_ev3")
        s_bnd3 = S("s_bnd3")
        s_mixk = S("s_mixk")
        s_mixr = S("s_mixr")
        s_kfmm = S("s_kfmm")
        s_kact = S("s_kact")
        s_rfmm = S("s_rfmm")
        s_fe = S("s_fe")
        s_sigf = S("s_sigf")
        s_vomm = S("s_vomm")
        s_pr = S("s_pr")
        s_sq3 = S("s_sq3")
        s_var3 = S("s_var3")
        s_rsq3 = S("s_rsq3")
        s_ow = S("s_ow")
        s_pay1 = S("s_pay1")
        s_ccd = S("s_ccd")
        s_ccc = S("s_ccc")
        s_ikv = S("s_ikv")
        s_dx1 = S("s_dx1")
        s_dx2 = S("s_dx2")
        s_dx3 = S("s_dx3")
        s_pay2 = S("s_pay2")
        s_rowmm = S("s_rowmm")
        s_ycol = S("s_ycol")
        s_misc = S("s_misc")
        s_misc2 = S("s_misc2")

        blk = ctx.enter_context(nc.Block())

        ph0 = ExitStack()
        iot_r = ph0.enter_context(nc.sbuf_tensor("iot_r", [128, 128], F32))
        iot_c = ph0.enter_context(nc.sbuf_tensor("iot_c", [128, 128], F32))
        if True:

            @blk.sync
            def _(sync):
                sync.dma_start(out=tm_c[:], in_=tm_d).then_inc(s_misc, 16)
                sync.dma_start(out=mur_c[:], in_=mur_d).then_inc(s_misc, 16)
                sync.dma_start(out=muk_c[:], in_=muk_d).then_inc(s_misc, 16)
                sync.dma_start(out=td_c[:], in_=td_d).then_inc(s_misc, 16)
                sync.dma_start(out=tf_c[:], in_=tf_d).then_inc(s_misc, 16)
                if exchange:
                    sync.dma_start(out=flagc[:], in_=flagc_d).then_inc(s_misc2, 16)
                    sync.dma_start(out=flagu[:], in_=flagu_d).then_inc(s_misc2, 16)

            @blk.gpsimd
            def _(g):
                g.iota(iot_r[:], [[0, 128]], channel_multiplier=1,
                       allow_small_or_imprecise_dtypes=True)
                g.iota(iot_c[:], [[1, 128]], channel_multiplier=0,
                       allow_small_or_imprecise_dtypes=True).then_inc(s_init, 1)

            @blk.vector
            def _(v):
                for col, val in [(CB_60, 60.0), (CB_M1, -1.0), (CB_1, 1.0),
                                 (CB_EPS16, 1e-16), (CB_LNEPS, 1e-5),
                                 (CB_M05, -0.5), (CB_M512, -512.0)]:
                    v.memset(cpool[:, col:col + 1], val)
                v.memset(carry_kv[:], 0.0)
                v.memset(carry_k[:], 0.0)
                v.memset(i_kv[:], 0.0)
                v.memset(i_k[:], 0.0)
                v.wait_ge(s_misc, 16 * 5)
                v.tensor_scalar(mur1_c[:], mur_c[:], -1.0, None, Alu.add)
                v.tensor_scalar(muk1_c[:], muk_c[:], -1.0, None, Alu.add)
                v.wait_ge(s_init, 1)
                v.tensor_tensor(ident[:], iot_r[:], iot_c[:],
                                Alu.is_equal).then_inc(s_init, 1)

            @blk.scalar
            def _(sc):
                sc.wait_ge(s_misc, 16 * 5)
                sc.wait_ge(s_init, 2)
                sc.activation(dec_c[:], td_c[:], Act.Exp)
                sc.drain()
                sc.activation(wdec_c[:], dec_c[:], Act.Exp,
                              scale=cpool[:, CB_M1:CB_M1 + 1])
                sc.activation(w512_c[:], dec_c[:], Act.Exp,
                              scale=cpool[:, CB_M512:CB_M512 + 1])
                sc.activation(etf_c[:], tf_c[:], Act.Exp).then_inc(s_init, 1)


        @blk.gpsimd
        def _(g):
            g.wait_ge(s_init, 3)
            g.iota(sq_junk[:, 0:TILE], [[1, TILE]], channel_multiplier=0,
                   allow_small_or_imprecise_dtypes=True).then_inc(s_dx1, 1)

        @blk.vector
        def _(v):
            v.wait_ge(s_dx1, 1)
            v.wait_ge(s_init, 3)
            for j in range(NCH):
                i = v.tensor_scalar(dexpt[:, j, :], sq_junk[:, 0:TILE],
                                    dec_c[:, j:j + 1], None, Alu.mult)
            i.then_inc(s_dx2, 1)

        @blk.scalar
        def _(sc):
            sc.wait_ge(s_dx2, 1)
            for j in range(NCH):
                i = sc.activation(dexpt[:, j, :],
                                  dexpt[:, j, :].bitcast(F32), Act.Exp,
                                  scale=cpool[:, CB_M1:CB_M1 + 1])
            i.then_inc(s_dx3, 1)

        ph0.close()
        ph1 = ExitStack()
        x_tm0 = ph1.enter_context(nc.sbuf_tensor("x_tm0", [128, C], F32R))
        x_tm1 = ph1.enter_context(nc.sbuf_tensor("x_tm1", [128, C], F32R))
        x_fm = ph1.enter_context(nc.sbuf_tensor("x_fm", [128, NCH, TILE + 1], F32))
        d_full = ph1.enter_context(nc.sbuf_tensor("d_full", [128, NCH, TILE], F32))
        xm_fm = ph1.enter_context(nc.sbuf_tensor("xm_fm", [128, NCH, TILE], F32R))
        k_sb0 = ph1.enter_context(nc.sbuf_tensor("k_sb0", [128, TILE], F32))
        k_sb1 = ph1.enter_context(nc.sbuf_tensor("k_sb1", [128, TILE], F32))
        k_sb2 = ph1.enter_context(nc.sbuf_tensor("k_sb2", [128, TILE], F32))
        kv_sb0 = ph1.enter_context(nc.sbuf_tensor("kv_sb0", [128, TILE], F32))
        kv_sb1 = ph1.enter_context(nc.sbuf_tensor("kv_sb1", [128, TILE], F32))
        kv_sb2 = ph1.enter_context(nc.sbuf_tensor("kv_sb2", [128, TILE], F32))
        s_buf0 = ph1.enter_context(nc.sbuf_tensor("s_buf0", [128, TILE + 1], F32))
        s_buf1 = ph1.enter_context(nc.sbuf_tensor("s_buf1", [128, TILE + 1], F32))
        s_buf2 = ph1.enter_context(nc.sbuf_tensor("s_buf2", [128, TILE + 1], F32))
        sk_buf0 = ph1.enter_context(nc.sbuf_tensor("sk_buf0", [128, TILE + 1], F32))
        sk_buf1 = ph1.enter_context(nc.sbuf_tensor("sk_buf1", [128, TILE + 1], F32))
        sk_buf2 = ph1.enter_context(nc.sbuf_tensor("sk_buf2", [128, TILE + 1], F32))
        wkv_sb0 = ph1.enter_context(nc.sbuf_tensor("wkv_sb0", [128, TILE], F32))
        wkv_sb1 = ph1.enter_context(nc.sbuf_tensor("wkv_sb1", [128, TILE], F32))
        sumk_sb0 = ph1.enter_context(nc.sbuf_tensor("sumk_sb0", [128, TILE], F32))
        sumk_sb1 = ph1.enter_context(nc.sbuf_tensor("sumk_sb1", [128, TILE], F32))
        if True:
            x_tm = [x_tm0, x_tm1]
            k_sb, kv_sb = [k_sb0, k_sb1, k_sb2], [kv_sb0, kv_sb1, kv_sb2]
            s_bufs, sk_bufs = [s_buf0, s_buf1, s_buf2], [sk_buf0, sk_buf1, sk_buf2]
            wkv_sb, sumk_sb = [wkv_sb0, wkv_sb1], [sumk_sb0, sumk_sb1]

            @blk.gpsimd
            def _(g):
                g.wait_ge(s_init, 3)
                g.nop().then_inc(s_init, 1)
                g.dma_start(out=x_fm[:, :, 0:1],
                            in_=xprev_d.rearrange("p (a b) -> p a b", b=1),
                            allow_slow_non_contiguous=True
                            ).then_inc(s_bndd, 16)


            @blk.scalar
            def _(sc):
                sc.dma_start(out=wslotA[:], in_=w_d["WkT"].rearrange(
                    "(a p) c -> p a c", p=128)).then_inc(s_wl, 16)
                sc.dma_start(out=wslotB[:], in_=w_d["WvT"].rearrange(
                    "(a p) c -> p a c", p=128)).then_inc(s_wl, 16)
                sc.dma_start(out=wslotC[:], in_=w_d["WoT"].rearrange(
                    "(a p) c -> p a c", p=128)).then_inc(s_wlo, 16)

            @blk.sync
            def _(sync):
                sync.wait_ge(s_init, 3)
                for ti in range(NT):
                    for sl in range(NSL):
                        gs = ti * NSL + sl
                        if gs >= 2:
                            sync.wait_ge(s_T, gs - 1)
                        sync.dma_start(
                            out=x_tm[gs % 2][:],
                            in_=xr_view[:, gs, :],
                        ).then_inc(s_xld[gs % 2], 16)

            @blk.tensor
            def _(t):
                t.wait_ge(s_init, 2)
                for ti in range(NT):
                    for sl in range(NSL):
                        gs = ti * NSL + sl
                        t.wait_ge(s_xld[gs % 2], 16 * (gs // 2 + 1))
                        if gs >= 1:
                            t.wait_ge(s_ev, gs)
                        if ti >= 1 and sl == 0:
                            t.wait_ge(s_k, ti * NCH)
                        for j in range(NCH):
                            dst = psA[0] if j < 4 else psA[1]
                            i = t.transpose(
                                dst[:].bitcast(F32R)[:, (j % 4) * 128:(j % 4 + 1) * 128],
                                x_tm[gs % 2][:, j * 128:(j + 1) * 128],
                                ident[:])
                        i.then_inc(s_T, 1)
                    t.wait_ge(s_xm, ti + 1)
                    for j in range(NCH):
                        g = ti * NCH + j
                        if g >= 3:
                            t.wait_ge(s_k, g - 2)
                        pk = psK3[g % 3]
                        for c in range(NCH):
                            i = t.matmul(pk,
                                         wslotA[:, c, j * 128:(j + 1) * 128],
                                         xm_fm[:, c, :],
                                         start=(c == 0), stop=(c == NCH - 1))
                        i.then_inc(s_kmm, 1)
                        if g >= 3:
                            t.wait_ge(s_kv, g - 2)
                        pv = psV3[g % 3]
                        for c in range(NCH):
                            i = t.matmul(pv,
                                         wslotB[:, c, j * 128:(j + 1) * 128],
                                         xm_fm[:, c, :],
                                         start=(c == 0), stop=(c == NCH - 1))
                        i.then_inc(s_vmm, 1)

            @blk.scalar
            def _(sc):
                sc.wait_ge(s_wl, 32)
                for ti in range(NT):
                    for sl in range(NSL):
                        gs = ti * NSL + sl
                        sc.wait_ge(s_T, gs + 1)
                        if ti >= 1 and sl == 0:
                            sc.wait_ge(s_xm, ti)
                        if sl == NSL - 1 and ti >= 1:
                            sc.wait_ge(s_bnd, ti)
                        sc.copy(x_fm[:, 0:4, 1 + sl * 128:1 + (sl + 1) * 128],
                                psA[0][:].bitcast(F32R).rearrange(
                                    "p (a b) -> p a b", a=4))
                        sc.copy(x_fm[:, 4:8, 1 + sl * 128:1 + (sl + 1) * 128],
                                psA[1][:].bitcast(F32R).rearrange(
                                    "p (a b) -> p a b", a=4)).then_inc(s_ev, 1)
                    for j in range(NCH):
                        g = ti * NCH + j
                        sc.wait_ge(s_kmm, g + 1)
                        if g >= 3:
                            sc.wait_ge(s_scan, g - 2)
                        kb = k_sb[g % 3]
                        sc.activation(kb[:], psK3[g % 3], Act.Relu,
                                      bias=cpool[:, CB_60:CB_60 + 1],
                                      scale=cpool[:, CB_M1:CB_M1 + 1])
                        sc.drain()
                        sc.activation(kb[:], kb[:], Act.Exp,
                                      bias=cpool[:, CB_60:CB_60 + 1],
                                      scale=cpool[:, CB_M1:CB_M1 + 1]
                                      ).then_inc(s_k, 1)

            @blk.vector
            def _(v):
                v.wait_ge(s_init, 4)
                for ti in range(NT):
                    v.wait_ge(s_ev, ti * NSL + NSL)
                    if ti == 0:
                        v.wait_ge(s_bndd, 16)
                    else:
                        v.wait_ge(s_stx, 16 * ti)
                    v.drain()
                    v.tensor_tensor(d_full[:], x_fm[:, :, 1:TILE + 1],
                                    x_fm[:, :, 0:TILE], Alu.subtract)
                    v.drain()
                    for j in range(NCH):
                        i = v.scalar_tensor_tensor(
                            xm_fm[:, j, :], d_full[:, j, :], tm_c[:, j:j + 1],
                            x_fm[:, j, 0:TILE], Alu.mult, Alu.add)
                    i.then_inc(s_xm, 1)
                    v.drain()
                    v.tensor_copy(x_fm[:, :, 0:1],
                                  x_fm[:, :, TILE:TILE + 1]).then_inc(s_bnd, 1)
                    g0 = ti * NCH
                    v.wait_ge(s_k, g0 + 1)
                    v.wait_ge(s_vmm, g0 + 1)

                    v.tensor_tensor(kv_sb[g0 % 3][:], k_sb[g0 % 3][:],
                                    psV3[g0 % 3], Alu.mult).then_inc(s_kv, 1)
                    v.tensor_copy(s_bufs[g0 % 3][:, 0:1], carry_kv[:, 0:1])
                    v.tensor_copy(sk_bufs[g0 % 3][:, 0:1], carry_k[:, 0:1])
                    for j in range(NCH + 1):
                        g = ti * NCH + j
                        v.drain()
                        if j < NCH:
                            wj = wdec_c[:, j:j + 1].broadcast_to([128, TILE])
                            v.tensor_tensor_scan(s_bufs[g % 3][:, 1:TILE + 1],
                                                 wj, kv_sb[g % 3][:],
                                                 s_bufs[g % 3][:, 0:1],
                                                 Alu.mult, Alu.add)
                            i = v.tensor_tensor_scan(sk_bufs[g % 3][:, 1:TILE + 1],
                                                 wj, k_sb[g % 3][:],
                                                 sk_bufs[g % 3][:, 0:1],
                                                 Alu.mult, Alu.add)
                        if j >= 1:
                            jj, gg = j - 1, g - 1
                            if gg >= 2:
                                v.wait_ge(s_stw[gg % 2], 32 * (gg // 2))
                            v.scalar_tensor_tensor(
                                wkv_sb[gg % 2][:], kv_sb[gg % 3][:],
                                etf_c[:, jj:jj + 1],
                                s_bufs[gg % 3][:, 0:TILE], Alu.mult, Alu.add)
                            v.scalar_tensor_tensor(
                                sumk_sb[gg % 2][:], k_sb[gg % 3][:],
                                etf_c[:, jj:jj + 1],
                                sk_bufs[gg % 3][:, 0:TILE], Alu.mult, Alu.add)
                            v.tensor_copy(carry_kv[:, jj:jj + 1],
                                          s_bufs[gg % 3][:, TILE:TILE + 1])
                            v.tensor_copy(carry_k[:, jj:jj + 1],
                                          sk_bufs[gg % 3][:, TILE:TILE + 1]
                                          ).then_inc(s_scan, 1)
                        if j + 1 < NCH:
                            gn = g + 1
                            v.wait_ge(s_k, gn + 1)
                            v.wait_ge(s_vmm, gn + 1)

                            v.tensor_tensor(kv_sb[gn % 3][:], k_sb[gn % 3][:],
                                            psV3[gn % 3], Alu.mult
                                            ).then_inc(s_kv, 1)
                            v.tensor_copy(s_bufs[gn % 3][:, 0:1],
                                          carry_kv[:, j + 1:j + 2])
                            v.tensor_copy(sk_bufs[gn % 3][:, 0:1],
                                          carry_k[:, j + 1:j + 2])

            @blk.gpsimd
            def _(g):
                for ti in range(NT):
                    g.wait_ge(s_xm, ti + 1)
                    g.dma_start(out=xm_d[ti], in_=xm_fm[:]).then_inc(s_stx, 16)
                    for j in range(NCH):
                        gg = ti * NCH + j
                        g.wait_ge(s_scan, gg + 1)
                        g.dma_start(out=wkv_d[ti, j], in_=wkv_sb[gg % 2][:]
                                    ).then_inc(s_stw[gg % 2], 16)
                        g.dma_start(out=sumk_d[ti, j], in_=sumk_sb[gg % 2][:]
                                    ).then_inc(s_stw[gg % 2], 16)

            if exchange:
                @blk.vector
                def _(v):
                    v.wait_ge(s_misc2, 32)
                    v.drain()
                    v.tensor_scalar(pay1[:, 0:NCH], carry_kv[:],
                                    flagc[:, 0:1], None, Alu.mult)
                    v.tensor_scalar(pay1[:, NCH:2 * NCH], carry_k[:],
                                    flagc[:, 0:1], None, Alu.mult
                                    ).then_inc(s_pay1, 1)

                @blk.gpsimd
                def _(g):
                    g.wait_ge(s_pay1, 1)
                    g.dma_start(out=cc1_in.ap(), in_=pay1[:]).then_inc(s_ccd, 16)
                    g.wait_ge(s_ccd, 16)
                    g.collective_compute(
                        "AllReduce", Alu.add,
                        replica_groups=[[0, 1], [2, 3], [4, 5], [6, 7]],
                        ins=[cc1_in.ap()], outs=[cc1_out.ap()],
                    ).then_inc(s_ccc, 1)
                    g.wait_ge(s_ccc, 1)
                    g.dma_start(out=rcv1[:], in_=cc1_out.ap()).then_inc(s_ccd, 16)

                @blk.vector
                def _(v):
                    v.wait_ge(s_ccd, 32)
                    v.tensor_scalar(i_kv[:], rcv1[:, 0:NCH], flagu[:, 0:1],
                                    None, Alu.mult)
                    v.tensor_scalar(i_k[:], rcv1[:, NCH:2 * NCH], flagu[:, 0:1],
                                    None, Alu.mult).then_inc(s_ikv, 1)

        ph1.close()
        ph2a = ExitStack()
        xm2 = [ph2a.enter_context(nc.sbuf_tensor(f"xm2_{i}", [128, NCH, TILE], F32R)) for i in range(2)]
        wkv_in = [ph2a.enter_context(nc.sbuf_tensor(f"wkv_in{i}", [128, TILE], F32)) for i in range(3)]
        sumk_in = [ph2a.enter_context(nc.sbuf_tensor(f"sumk_in{i}", [128, TILE], F32)) for i in range(3)]
        e_sb = [ph2a.enter_context(nc.sbuf_tensor(f"e_sb{i}", [128, TILE], F32)) for i in range(3)]
        rwkv = ph2a.enter_context(nc.sbuf_tensor("rwkv", [128, NCH, TILE], F32R))
        y1p = [ph2a.enter_context(nc.sbuf_tensor(f"y1p{i}", [128, C], F32R)) for i in range(4)]
        if True:

            @blk.scalar
            def _(sc):
                sc.wait_ge(s_kmm, NT * NCH)
                sc.dma_start(out=wslotA[:], in_=w_d["WrT"].rearrange(
                    "(a p) c -> p a c", p=128)).then_inc(s_wlr, 16)
                sc.wait_ge(s_vmm, NT * NCH)
                sc.dma_start(out=wslotB[:], in_=w_d["fWkT"].rearrange(
                    "(a p) c -> p a c", p=128)).then_inc(s_wfk, 16)

            @blk.sync
            def _(sync):
                sync.wait_ge(s_stx, 16 * NT)
                sync.wait_ge(s_stw[0], 32 * 4 * NT)
                sync.wait_ge(s_stw[1], 32 * 4 * NT)
                def xm2_load(sync, ti):
                    if ti >= 2:
                        sync.wait_ge(s_rmm, (ti - 1) * NCH)
                    sync.dma_start(out=xm2[ti % 2][:],
                                   in_=xm_d[ti]).then_inc(s_l2m[ti % 2], 16)

                xm2_load(sync, 0)
                xm2_load(sync, 1)
                for ti in range(NT):
                    for j in range(NCH):
                        g = ti * NCH + j
                        if g >= 3:
                            sync.wait_ge(s_rw, g - 2)
                        sync.dma_start(out=wkv_in[g % 3][:], in_=wkv_d[ti, j]
                                       ).then_inc(s_l2w[g % 3], 16)
                        sync.dma_start(out=sumk_in[g % 3][:], in_=sumk_d[ti, j]
                                       ).then_inc(s_l2w[g % 3], 16)
                    for sl in range(NSL):
                        gs = ti * NSL + sl
                        if gs >= 4:
                            sync.wait_ge(s_sty[gs % 4], 16 * (gs // 4))
                        sync.dma_start(out=y1p[gs % 4][:],
                                       in_=xr_view[:, gs, :]
                                       ).then_inc(s_l2x[gs % 4], 16)
                    if ti + 2 < NT:
                        xm2_load(sync, ti + 2)

            @blk.tensor
            def _(t):
                t.wait_ge(s_wlr, 16)
                t.wait_ge(s_wlo, 16)

                def r_proj(t, ti, j0, j1):
                    if j0 == 0:
                        t.wait_ge(s_l2m[ti % 2], 16 * (ti // 2 + 1))
                    for j in range(j0, j1):
                        g = ti * NCH + j
                        if g >= 2:
                            t.wait_ge(s_e, g - 1)
                        for c in range(NCH):
                            i = t.matmul(psA[j % 2][:],
                                         wslotA[:, c, j * 128:(j + 1) * 128],
                                         xm2[ti % 2][:, c, :],
                                         start=(c == 0), stop=(c == NCH - 1))
                        i.then_inc(s_rmm, 1)

                def wo_proj(t, ti):
                    t.wait_ge(s_rw, ti * NCH + NCH)
                    for sl in range(NSL):
                        gs = ti * NSL + sl
                        if gs >= 2:
                            t.wait_ge(s_y1p, gs - 1)
                        for j in range(NCH):
                            for h in range(2):
                                i = t.matmul(
                                    psW[sl % 2][:, h * 512:(h + 1) * 512],
                                    rwkv[:, j, sl * 128:(sl + 1) * 128],
                                    wslotC[:, j, h * 512:(h + 1) * 512],
                                    start=(j == 0), stop=(j == NCH - 1))
                        i.then_inc(s_omm, 1)

                for ti in range(NT):
                    r_proj(t, ti, 0, NCH)
                    wo_proj(t, ti)

            @blk.scalar
            def _(sc):
                for ti in range(NT):
                    for j in range(NCH):
                        g = ti * NCH + j
                        sc.wait_ge(s_rmm, g + 1)
                        if g >= 3:
                            sc.wait_ge(s_rw, g - 2)
                        sc.activation(e_sb[g % 3][:], psA[j % 2][:], Act.Exp,
                                      scale=cpool[:, CB_M1:CB_M1 + 1]
                                      ).then_inc(s_e, 1)
                        sc.wait_ge(s_den, g + 1)
                        sc.drain()
                        sc.activation(e_sb[g % 3][:], e_sb[g % 3][:],
                                      Act.Ln,
                                      bias=cpool[:, CB_EPS16:CB_EPS16 + 1])
                        sc.drain()
                        sc.activation(e_sb[g % 3][:], e_sb[g % 3][:],
                                      Act.Exp, scale=cpool[:, CB_M1:CB_M1 + 1]
                                      ).then_inc(s_inv, 1)
                    for sl in range(NSL):
                        gs = ti * NSL + sl
                        sc.wait_ge(s_y1p, gs + 1)
                        sc.drain()
                        sc.activation(sq_junk[:],
                                      y1p[gs % 4][:, 0:512].bitcast(F32),
                                      Act.Square,
                                      accum_out=sumsq[:, 2 * sl:2 * sl + 1])
                        sc.drain()
                        sc.activation(sq_junk[:],
                                      y1p[gs % 4][:, 512:C].bitcast(F32),
                                      Act.Square,
                                      accum_out=sumsq[:, 2 * sl + 1:2 * sl + 2]
                                      ).then_inc(s_sq, 1)
                    sc.wait_ge(s_var, ti + 1)
                    sc.activation(is_col[:], v_col[:], Act.Ln,
                                  bias=cpool[:, CB_LNEPS:CB_LNEPS + 1])
                    sc.drain()
                    sc.activation(is_col[:], is_col[:], Act.Exp,
                                  scale=cpool[:, CB_M05:CB_M05 + 1]
                                  ).then_inc(s_rsq, 1)

            @blk.vector
            def _(v):
                if exchange:
                    v.wait_ge(s_ikv, 1)
                v.wait_ge(s_dx3, 1)
                for ti in range(NT):
                    if ti >= 1:
                        v.wait_ge(s_omm, ti * NSL)
                    for j in range(NCH):
                        g = ti * NCH + j
                        v.wait_ge(s_l2w[g % 3], 32 * (g // 3 + 1))
                        v.wait_ge(s_e, g + 1)
                        if g >= 3:
                            v.wait_ge(s_inv, g - 2)
                        if exchange:
                            v.scalar_tensor_tensor(
                                wkv_in[g % 3][:], dexpt[:, j, :].bitcast(F32),
                                i_kv[:, j:j + 1], wkv_in[g % 3][:],
                                Alu.mult, Alu.add)
                            v.scalar_tensor_tensor(
                                sumk_in[g % 3][:], dexpt[:, j, :].bitcast(F32),
                                i_k[:, j:j + 1], sumk_in[g % 3][:],
                                Alu.mult, Alu.add)
                            v.drain()
                        v.scalar_tensor_tensor(
                            e_sb[g % 3][:], e_sb[g % 3][:], 1.0,
                            sumk_in[g % 3][:], Alu.add, Alu.mult
                            ).then_inc(s_den, 1)
                        v.wait_ge(s_inv, g + 1)
                        v.tensor_tensor(rwkv[:, j, :], wkv_in[g % 3][:],
                                        e_sb[g % 3][:], Alu.mult
                                        ).then_inc(s_rw, 1)

                    if exchange:
                        v.tensor_tensor(i_kv[:], i_kv[:], w512_c[:], Alu.mult)
                        v.tensor_tensor(i_k[:], i_k[:], w512_c[:], Alu.mult)
                        v.drain()
                    for sl in range(NSL):
                        gs = ti * NSL + sl
                        v.wait_ge(s_omm, gs + 1)
                        v.wait_ge(s_l2x[gs % 4], 16 * (gs // 4 + 1))
                        v.scalar_tensor_tensor(
                            y1p[gs % 4][:], psW[sl % 2][:], 1.0,
                            y1p[gs % 4][:].bitcast(F32),
                            Alu.mult, Alu.add,
                            accum_out=sums[:, sl:sl + 1]).then_inc(s_y1p, 1)
                    v.wait_ge(s_sq, ti * NSL + NSL)
                    v.drain()
                    v.tensor_scalar(m_col[:], sums[:], 1.0 / C, None, Alu.mult)
                    v.tensor_tensor(v_col[:], sumsq[:].rearrange(
                        "p (a b) -> p a b", b=2)[:, :, 0],
                        sumsq[:].rearrange("p (a b) -> p a b", b=2)[:, :, 1],
                        Alu.add)
                    v.drain()
                    v.tensor_tensor(b_col[:], m_col[:], m_col[:], Alu.mult)
                    v.drain()
                    v.scalar_tensor_tensor(v_col[:], v_col[:], 1.0 / C,
                                           b_col[:], Alu.mult, Alu.subtract
                                           ).then_inc(s_var, 1)
                    v.wait_ge(s_rsq, ti + 1)
                    v.scalar_tensor_tensor(b_col[:], m_col[:], -1.0,
                                           is_col[:], Alu.mult, Alu.mult)
                    v.drain()
                    for sl in range(NSL):
                        gs = ti * NSL + sl
                        v.tensor_scalar(y1p[gs % 4][:],
                                        y1p[gs % 4][:].bitcast(F32),
                                        is_col[:, sl:sl + 1],
                                        b_col[:, sl:sl + 1],
                                        Alu.mult, Alu.add).then_inc(s_y1w, 1)


            @blk.gpsimd
            def _(g):
                for ti in range(NT):
                    for sl in range(NSL):
                        gs = ti * NSL + sl
                        g.wait_ge(s_y1w, gs + 1)
                        g.dma_start(out=y1_d[ti, sl], in_=y1p[gs % 4][:]
                                    ).then_inc(s_sty[gs % 4], 16)

            if exchange:
                LASTS = (NT * NSL - 1) % 4

                @blk.vector
                def _(v):
                    v.wait_ge(s_y1w, NT * NSL)
                    v.drain()
                    v.tensor_scalar(pay2[:], y1p[LASTS][:].bitcast(F32),
                                    flagc[:, 0:1], None, Alu.mult
                                    ).then_inc(s_pay2, 1)

                @blk.gpsimd
                def _(g):
                    g.wait_ge(s_pay2, 1)
                    g.dma_start(out=cc2_in.ap(), in_=pay2[127:128, :]
                                ).then_inc(s_ccd, 16)
                    g.wait_ge(s_ccd, 48)
                    g.collective_compute(
                        "AllReduce", Alu.add,
                        replica_groups=[[0, 1], [2, 3], [4, 5], [6, 7]],
                        ins=[cc2_in.ap()], outs=[cc2_out.ap()],
                    ).then_inc(s_ccc, 1)
                    g.wait_ge(s_ccc, 2)

                @blk.gpsimd
                def _(g):
                    g.dma_start(out=rcv2fm[:],
                                in_=cc2_out.ap().rearrange(
                                    "o (a p) -> (o p) a", p=128),
                                allow_slow_non_contiguous=True
                                ).then_inc(s_ccd, 16)

                @blk.vector
                def _(v):
                    v.wait_ge(s_ccd, 64)
                    v.tensor_scalar(ycol0[:], rcv2fm[:],
                                    flagu[:, 0:1], None, Alu.mult
                                    ).then_inc(s_ycol, 1)

        ph2a.close()
        ph2b = ExitStack()
        d_buf = [ph2b.enter_context(nc.sbuf_tensor(f"d_buf{i}", [128, TILE], F32))
                 for i in range(2)]
        y1t = [ph2b.enter_context(nc.sbuf_tensor(f"y1t{i}", [128, C], F32R))
               for i in range(2)]
        y1r = [ph2b.enter_context(nc.sbuf_tensor(f"y1r{i}", [128, C], F32R))
               for i in range(2)]
        y1fm = ph2b.enter_context(nc.sbuf_tensor("y1fm", [128, NCH, TILE + 1], F32))
        ybcol = ph2b.enter_context(nc.sbuf_tensor("ybcol", [128, NCH], F32))
        xr_fm = ph2b.enter_context(nc.sbuf_tensor("xr_fm", [128, NCH, TILE], F32R))
        xk_fm = dexpt
        kact = ph2b.enter_context(nc.sbuf_tensor("kact", [128, NCH, TILE], F32R))
        sigf = [ph2b.enter_context(nc.sbuf_tensor(f"sigf{i}", [128, C], F32))
                for i in range(2)]
        y2p = [ph2b.enter_context(nc.sbuf_tensor(f"y2p{i}", [128, C], F32))
               for i in range(2)]
        if True:
            @blk.sync
            def _(sync):
                for _q in range(4):
                    sync.wait_ge(s_sty[_q], 16 * NT)
                sync.wait_ge(s_rmm, NT * NCH)
                sync.wait_ge(s_y1w, NT * NSL)

                def y1t_loads(sync, ti):
                    for sl in range(NSL):
                        gs = ti * NSL + sl
                        if gs >= 2:
                            sync.wait_ge(s_T3, gs - 1)
                        sync.dma_start(out=y1t[gs % 2][:], in_=y1_d[ti, sl]
                                       ).then_inc(s_l3t[gs % 2], 16)

                y1t_loads(sync, 0)
                for ti in range(NT):
                    if ti + 1 < NT:
                        y1t_loads(sync, ti + 1)
                    for sl in range(NSL):
                        gs = ti * NSL + sl
                        if gs >= 2:
                            sync.wait_ge(s_pr, gs - 1)
                        sync.dma_start(out=y1r[gs % 2][:], in_=y1_d[ti, sl]
                                       ).then_inc(s_l3r[gs % 2], 16)

            @blk.vector
            def _(v):
                i = v.memset(ybcol[:], 0.0)
                if exchange:
                    v.wait_ge(s_ycol, 1)
                    v.drain()
                    i = v.tensor_copy(ybcol[:], ycol0[:])
                i.then_inc(s_bnd3, 1)
                v.drain()

            @blk.tensor
            def _(t):
                t.wait_ge(s_wfk, 16)

                def transp(t, k, sl):
                    gs = k * NSL + sl
                    t.wait_ge(s_l3t[gs % 2], 16 * (gs // 2 + 1))
                    if gs >= 1:
                        t.wait_ge(s_ev3, gs)
                    for j in range(NCH):
                        dst = psA[0] if j < 4 else psA[1]
                        i = t.transpose(
                            dst[:].bitcast(F32R)[:, (j % 4) * 128:(j % 4 + 1) * 128],
                            y1t[gs % 2][:, j * 128:(j + 1) * 128],
                            ident[:])
                    i.then_inc(s_T3, 1)

                def rf_mm(t, ti, sl):
                    gs = ti * NSL + sl
                    if gs == 0:
                        t.wait_ge(s_wfr, 16)
                    if gs >= 2:
                        t.wait_ge(s_pr, gs - 1)
                    for c in range(NCH):
                        for h in range(2):
                            i = t.matmul(
                                psW[sl % 2][:, h * 512:(h + 1) * 512],
                                xr_fm[:, c, sl * 128:(sl + 1) * 128],
                                wslotA[:, c, h * 512:(h + 1) * 512],
                                start=(c == 0), stop=(c == NCH - 1))
                    i.then_inc(s_rfmm, 1)

                def vo_mm(t, ti, sl):
                    gs = ti * NSL + sl
                    t.wait_ge(s_kact, ti * NCH + NCH)
                    if gs == 0:
                        t.wait_ge(s_wfv, 16)
                    t.wait_ge(s_fe, gs + 1)
                    for j in range(NCH):
                        for h in range(2):
                            i = t.matmul(
                                psW[sl % 2][:, h * 512:(h + 1) * 512],
                                kact[:, j, sl * 128:(sl + 1) * 128],
                                wslotC[:, j, h * 512:(h + 1) * 512],
                                start=(j == 0), stop=(j == NCH - 1))
                    i.then_inc(s_vomm, 1)

                for sl in range(NSL):
                    transp(t, 0, sl)
                for ti in range(NT):
                    nxt = ti + 1 < NT
                    if nxt:
                        transp(t, ti + 1, 0)
                    t.wait_ge(s_mixk, ti * NCH + NCH)
                    for j in range(NCH):
                        g = ti * NCH + j
                        if g >= 2:
                            t.wait_ge(s_kact, g - 1)
                        for c in range(NCH):
                            i = t.matmul(psB[j % 2][:],
                                         wslotB[:, c, j * 128:(j + 1) * 128],
                                         xk_fm[:, c, :],
                                         start=(c == 0), stop=(c == NCH - 1))
                        i.then_inc(s_kfmm, 1)
                    t.wait_ge(s_mixr, ti * NCH + NCH)
                    for sl in range(NSL):
                        if nxt and sl + 1 < NSL:
                            transp(t, ti + 1, sl + 1)
                        rf_mm(t, ti, sl)
                        vo_mm(t, ti, sl)

            @blk.scalar
            def _(sc):
                sc.wait_ge(s_rmm, NT * NCH)
                sc.dma_start(out=wslotA[:], in_=w_d["fWrT"].rearrange(
                    "(a p) c -> p a c", p=128)).then_inc(s_wfr, 16)
                sc.wait_ge(s_omm, NT * NSL)
                sc.dma_start(out=wslotC[:], in_=w_d["fWvT"].rearrange(
                    "(a p) c -> p a c", p=128)).then_inc(s_wfv, 16)

                def evac(sc, k, sl):
                    gs = k * NSL + sl
                    sc.wait_ge(s_T3, gs + 1)
                    if k >= 1 and sl == 0:
                        sc.wait_ge(s_mixr, k * NCH)
                    if sl == NSL - 1:
                        sc.wait_ge(s_bnd3, k + 1)
                    sc.copy(y1fm[:, 0:4, 1 + sl * 128:1 + (sl + 1) * 128],
                            psA[0][:].bitcast(F32R).rearrange(
                                "p (a b) -> p a b", a=4))
                    sc.copy(y1fm[:, 4:8, 1 + sl * 128:1 + (sl + 1) * 128],
                            psA[1][:].bitcast(F32R).rearrange(
                                "p (a b) -> p a b", a=4)).then_inc(s_ev3, 1)

                def sig_block(sc, ti, sl):
                    gs = ti * NSL + sl
                    sc.wait_ge(s_rfmm, gs + 1)
                    sf = sigf[sl % 2]
                    if gs >= 2:
                        sc.wait_ge(s_pr, gs - 1)
                    sc.activation(sf[:], psW[sl % 2][:], Act.Exp,
                                  scale=cpool[:, CB_M1:CB_M1 + 1]
                                  ).then_inc(s_fe, 1)
                    sc.drain()
                    sc.activation(sf[:], sf[:], Act.Ln,
                                  bias=cpool[:, CB_1:CB_1 + 1])
                    sc.drain()
                    sc.activation(sf[:], sf[:], Act.Exp,
                                  scale=cpool[:, CB_M1:CB_M1 + 1]
                                  ).then_inc(s_sigf, 1)

                def sq_rsq(sc, ti, sl):
                    gs = ti * NSL + sl
                    sc.wait_ge(s_pr, gs + 1)
                    sc.drain()
                    sc.activation(sq_junk[:], y2p[sl % 2][:, 0:512],
                                  Act.Square,
                                  accum_out=sumsq[:, 2 * sl:2 * sl + 1])
                    sc.drain()
                    sc.activation(sq_junk[:], y2p[sl % 2][:, 512:C],
                                  Act.Square,
                                  accum_out=sumsq[:, 2 * sl + 1:2 * sl + 2]
                                  ).then_inc(s_sq3, 1)
                    sc.wait_ge(s_var3, gs + 1)
                    sc.activation(is_col[:, sl:sl + 1], v_col[:, sl:sl + 1],
                                  Act.Ln,
                                  bias=cpool[:, CB_LNEPS:CB_LNEPS + 1])
                    sc.drain()
                    sc.activation(is_col[:, sl:sl + 1], is_col[:, sl:sl + 1],
                                  Act.Exp, scale=cpool[:, CB_M05:CB_M05 + 1]
                                  ).then_inc(s_rsq3, 1)

                for sl in range(NSL):
                    evac(sc, 0, sl)
                for ti in range(NT):
                    nxt = ti + 1 < NT
                    for j in range(NCH):
                        g = ti * NCH + j
                        sc.wait_ge(s_kfmm, g + 1)
                        sc.drain()
                        sc.activation(sq_junk[:], psB[j % 2][:], Act.Relu)
                        sc.drain()
                        sc.activation(kact[:, j, :], sq_junk[:], Act.Square
                                      ).then_inc(s_kact, 1)
                    for sl in range(NSL):
                        if nxt:
                            evac(sc, ti + 1, sl)
                        sig_block(sc, ti, sl)
                        if sl >= 1:
                            sq_rsq(sc, ti, sl - 1)
                    sq_rsq(sc, ti, NSL - 1)

            @blk.vector
            def _(v):
                i = v.memset(ybcol[:], 0.0)
                if exchange:
                    v.wait_ge(s_ycol, 1)
                    v.drain()
                    i = v.tensor_copy(ybcol[:], ycol0[:])
                i.then_inc(s_bnd3, 1)
                v.drain()

                def xk_pass(v, k):
                    if k >= 1:
                        v.wait_ge(s_kfmm, k * NCH)
                    if k == 0:
                        v.tensor_copy(y1fm[:, :, 0:1],
                                      ybcol[:].rearrange("p (a b) -> p a b", b=1))
                    v.wait_ge(s_ev3, k * NSL + NSL)
                    for j in range(NCH + 1):
                        v.drain()
                        if j < NCH:
                            v.tensor_tensor(d_buf[j % 2][:],
                                            y1fm[:, j, 1:TILE + 1],
                                            y1fm[:, j, 0:TILE], Alu.subtract)
                        if j >= 1:
                            jj = j - 1
                            v.scalar_tensor_tensor(
                                xk_fm[:, jj, :], d_buf[jj % 2][:],
                                muk1_c[:, jj:jj + 1],
                                y1fm[:, jj, 1:TILE + 1], Alu.mult, Alu.add
                                ).then_inc(s_mixk, 1)
                    v.drain()

                def xr_pass(v, k):
                    if k >= 1:
                        v.wait_ge(s_rfmm, k * NSL)
                    for j in range(NCH + 1):
                        v.drain()
                        if j < NCH:
                            v.tensor_tensor(d_buf[j % 2][:],
                                            y1fm[:, j, 1:TILE + 1],
                                            y1fm[:, j, 0:TILE], Alu.subtract)
                        if j >= 1:
                            jj = j - 1
                            v.scalar_tensor_tensor(
                                xr_fm[:, jj, :], d_buf[jj % 2][:],
                                mur1_c[:, jj:jj + 1],
                                y1fm[:, jj, 1:TILE + 1], Alu.mult, Alu.add
                                ).then_inc(s_mixr, 1)
                    v.tensor_copy(y1fm[:, :, 0:1],
                                  y1fm[:, :, TILE:TILE + 1]).then_inc(s_bnd3, 1)
                    v.drain()

                def prod_ln(v, ti, sl):
                    gs = ti * NSL + sl
                    v.wait_ge(s_vomm, gs + 1)
                    v.wait_ge(s_sigf, gs + 1)
                    v.wait_ge(s_l3r[gs % 2], 16 * (gs // 2 + 1))
                    if gs >= 2:
                        v.wait_ge(s_sto[gs % 2], 16 * (gs // 2))
                    v.tensor_tensor(y2p[sl % 2][:], psW[sl % 2][:],
                                    sigf[sl % 2][:], Alu.mult)
                    v.drain()
                    v.scalar_tensor_tensor(
                        y2p[sl % 2][:], y2p[sl % 2][:], 1.0,
                        y1r[gs % 2][:].bitcast(F32),
                        Alu.mult, Alu.add,
                        accum_out=sums[:, sl:sl + 1]).then_inc(s_pr, 1)
                    v.wait_ge(s_sq3, gs + 1)
                    v.drain()
                    v.tensor_scalar(m_col[:, sl:sl + 1], sums[:, sl:sl + 1],
                                    1.0 / C, None, Alu.mult)
                    v.tensor_tensor(v_col[:, sl:sl + 1],
                                    sumsq[:, 2 * sl:2 * sl + 1],
                                    sumsq[:, 2 * sl + 1:2 * sl + 2], Alu.add)
                    v.drain()
                    v.tensor_tensor(b_col[:, sl:sl + 1], m_col[:, sl:sl + 1],
                                    m_col[:, sl:sl + 1], Alu.mult)
                    v.drain()
                    v.scalar_tensor_tensor(v_col[:, sl:sl + 1],
                                           v_col[:, sl:sl + 1], 1.0 / C,
                                           b_col[:, sl:sl + 1],
                                           Alu.mult, Alu.subtract
                                           ).then_inc(s_var3, 1)
                    v.wait_ge(s_rsq3, gs + 1)
                    v.scalar_tensor_tensor(b_col[:, sl:sl + 1],
                                           m_col[:, sl:sl + 1], -1.0,
                                           is_col[:, sl:sl + 1],
                                           Alu.mult, Alu.mult)
                    v.drain()
                    v.tensor_scalar(y2p[sl % 2][:], y2p[sl % 2][:],
                                    is_col[:, sl:sl + 1],
                                    b_col[:, sl:sl + 1],
                                    Alu.mult, Alu.add).then_inc(s_ow, 1)

                xk_pass(v, 0)
                xr_pass(v, 0)
                for ti in range(NT):
                    nxt = ti + 1 < NT
                    prod_ln(v, ti, 0)
                    prod_ln(v, ti, 1)
                    if nxt:
                        xk_pass(v, ti + 1)
                    prod_ln(v, ti, 2)
                    prod_ln(v, ti, 3)
                    if nxt:
                        xr_pass(v, ti + 1)

            @blk.gpsimd
            def _(g):
                for ti in range(NT):
                    for sl in range(NSL):
                        gs = ti * NSL + sl
                        g.wait_ge(s_ow, gs + 1)
                        g.dma_start(out=out_d[gs * 128:(gs + 1) * 128, :],
                                    in_=y2p[sl % 2][:]).then_inc(s_sto[gs % 2], 16)
                g.wait_ge(s_sto[0], 16 * 2 * NT)
                g.wait_ge(s_sto[1], 16 * 2 * NT)
        ph2b.close()
    return nc


_PROG_CACHE = {}


def _get_prog(T_c, exchange):
    key = (T_c, exchange)
    if key not in _PROG_CACHE:
        _PROG_CACHE[key] = build(T_c, exchange)
    return _PROG_CACHE[key]


def _fm(vec):
    return np.ascontiguousarray(np.asarray(vec, np.float32).reshape(NCH, 128).T)


def kernel(L0_attn_x, L0_attn_time_decay, L0_attn_time_first, L0_attn_time_mix,
           L0_attn_Wk, L0_attn_Wv, L0_attn_Wr, L0_attn_Wo,
           L0_ffn_W_r, L0_ffn_W_k, L0_ffn_W_v, L0_ffn_mu_r, L0_ffn_mu_k):
    x = np.ascontiguousarray(np.asarray(L0_attn_x, np.float32))
    B, T, _ = x.shape
    wT = {"WkT": L0_attn_Wk, "WvT": L0_attn_Wv, "WrT": L0_attn_Wr,
          "WoT": L0_attn_Wo, "fWrT": L0_ffn_W_r, "fWkT": L0_ffn_W_k,
          "fWvT": L0_ffn_W_v}
    wT = {k: np.ascontiguousarray(np.asarray(v, np.float32).T)
          for k, v in wT.items()}
    base = {
        "tm_fm": _fm(np.asarray(L0_attn_time_mix, np.float32).reshape(C)),
        "mur_fm": _fm(L0_ffn_mu_r),
        "muk_fm": _fm(L0_ffn_mu_k),
        "td_fm": _fm(np.repeat(np.asarray(L0_attn_time_decay, np.float32)[:, 0], 64)),
        "tf_fm": _fm(np.repeat(np.asarray(L0_attn_time_first, np.float32)[:, 0], 64)),
    }
    base.update(wT)
    zrow = np.zeros((128, NCH), np.float32)

    if not EXCHANGE:
        nc = _get_prog(T, False)
        in_maps = [dict(base, x=x[core % B], xprev_fm=zrow) for core in range(8)]
        res = run_bass_kernel_spmd(nc, in_maps, core_ids=list(range(8)))
        return np.stack([res.results[b]["out"] for b in range(B)], axis=0)
    else:
        T_c = T // 2
        nc = _get_prog(T_c, True)
        one = np.ones((128, 1), np.float32)
        zero = np.zeros((128, 1), np.float32)
        in_maps = []
        for core in range(8):
            b, half = core // 2, core % 2
            t0 = half * T_c
            in_maps.append(dict(
                base,
                x=np.ascontiguousarray(x[b, t0:t0 + T_c]),
                xprev_fm=(zrow if half == 0 else _fm(x[b, t0 - 1])),
                flagc=(one if half == 0 else zero),
                flagu=(zero if half == 0 else one)))
        res = run_bass_kernel_spmd(nc, in_maps, core_ids=list(range(8)))
        return np.stack(
            [np.concatenate([res.results[2 * b]["out"],
                             res.results[2 * b + 1]["out"]], axis=0)
             for b in range(B)], axis=0)


# version 18
# speedup vs baseline: 1.1109x; 1.0141x over previous
import numpy as np
import concourse.bass as bass
import concourse.mybir as mybir
from concourse.bass_utils import run_bass_kernel_spmd

F32 = mybir.dt.float32
F32R = mybir.dt.float32r
Act = mybir.ActivationFunctionType
Alu = mybir.AluOpType

C = 1024
NCH = C // 128
TILE = 512
NSL = TILE // 128

EXCHANGE = True


def build(T_c, exchange):
    NT = T_c // TILE
    nc = bass.Bass("TRN2", target_bir_lowering=False,
                   **({"num_devices": 8} if exchange else {}))

    x_d = nc.dram_tensor("x", [T_c, C], F32R, kind="ExternalInput").ap()
    xprev_d = nc.dram_tensor("xprev_fm", [128, NCH], F32, kind="ExternalInput").ap()
    w_d = {n: nc.dram_tensor(n, [C, C], F32R, kind="ExternalInput").ap()
           for n in ["WkT", "WvT", "WrT", "WoT", "fWrT", "fWkT", "fWvT"]}
    tm_d = nc.dram_tensor("tm_fm", [128, NCH], F32, kind="ExternalInput").ap()
    mur_d = nc.dram_tensor("mur_fm", [128, NCH], F32, kind="ExternalInput").ap()
    muk_d = nc.dram_tensor("muk_fm", [128, NCH], F32, kind="ExternalInput").ap()
    td_d = nc.dram_tensor("td_fm", [128, NCH], F32, kind="ExternalInput").ap()
    tf_d = nc.dram_tensor("tf_fm", [128, NCH], F32, kind="ExternalInput").ap()
    out_d = nc.dram_tensor("out", [T_c, C], F32, kind="ExternalOutput").ap()

    xm_d = nc.dram_tensor("xm_s", [NT, NCH, 128, TILE], F32R, kind="Internal").ap()
    wkv_d = nc.dram_tensor("wkv_s", [NT, NCH, 128, TILE], F32, kind="Internal").ap()
    sumk_d = nc.dram_tensor("sumk_s", [NT, NCH, 128, TILE], F32, kind="Internal").ap()
    y1_d = nc.dram_tensor("y1_s", [NT, NSL, 128, C], F32R, kind="Internal").ap()

    if exchange:
        flagc_d = nc.dram_tensor("flagc", [128, 1], F32, kind="ExternalInput").ap()
        flagu_d = nc.dram_tensor("flagu", [128, 1], F32, kind="ExternalInput").ap()
        cc1_in = nc.dram_tensor("cc1_in", [128, 2 * NCH], F32, kind="Internal")
        cc1_out = nc.dram_tensor("cc1_out", [128, 2 * NCH], F32, kind="Internal")
        cc2_in = nc.dram_tensor("cc2_in", [1, C], F32, kind="Internal")
        cc2_out = nc.dram_tensor("cc2_out", [1, C], F32, kind="Internal")

    xr_view = x_d.rearrange("(a p) c -> p a c", p=128)

    def st1(ti, j):
        return 272 * ti + 16 + 32 * (j + 1)
    ST1_ALL = 272 * NT
    ST2_ALL = ST1_ALL + 16 * NT * NSL
    def ld2(ti, part):
        return 16 * (ti * (NSL + 1 + 2 * NCH) + part)

    from contextlib import ExitStack
    with ExitStack() as ctx:
        sbuf = lambda *a: ctx.enter_context(nc.sbuf_tensor(*a))

        cpool = sbuf("cpool", [128, 8], F32)
        CB_60, CB_M1, CB_1, CB_EPS16, CB_LNEPS, CB_M05, CB_M512 = range(7)
        ident = sbuf("ident", [128, 128], F32R)
        tm_c = sbuf("tm_c", [128, NCH], F32)
        mur_c = sbuf("mur_c", [128, NCH], F32)
        muk_c = sbuf("muk_c", [128, NCH], F32)
        mur1_c = sbuf("mur1_c", [128, NCH], F32)
        muk1_c = sbuf("muk1_c", [128, NCH], F32)
        mur1n_c = sbuf("mur1n_c", [128, NCH], F32)
        muk1n_c = sbuf("muk1n_c", [128, NCH], F32)
        td_c = sbuf("td_c", [128, NCH], F32)
        tf_c = sbuf("tf_c", [128, NCH], F32)
        dec_c = sbuf("dec_c", [128, NCH], F32)
        wdec_c = sbuf("wdec_c", [128, NCH], F32)
        etf_c = sbuf("etf_c", [128, NCH], F32)
        w512_c = sbuf("w512_c", [128, NCH], F32)
        dexpt = sbuf("dexpt", [128, NCH, TILE], F32R)
        carry_kv = sbuf("carry_kv", [128, NCH], F32)
        carry_k = sbuf("carry_k", [128, NCH], F32)
        i_kv = sbuf("i_kv", [128, NCH], F32)
        i_k = sbuf("i_k", [128, NCH], F32)
        sq_junk = sbuf("sq_junk", [128, 512], F32)
        sums = sbuf("sums", [128, NSL], F32)
        sumsq = sbuf("sumsq", [128, 2 * NSL], F32)
        m_col = sbuf("m_col", [128, NSL], F32)
        v_col = sbuf("v_col", [128, NSL], F32)
        is_col = sbuf("is_col", [128, NSL], F32)
        b_col = sbuf("b_col", [128, NSL], F32)
        wslotA = sbuf("wslotA", [128, NCH, C], F32R)
        wslotB = sbuf("wslotB", [128, NCH, C], F32R)
        wslotC = sbuf("wslotC", [128, NCH, C], F32R)
        if exchange:
            flagc = sbuf("flagc_s", [128, 1], F32)
            flagu = sbuf("flagu_s", [128, 1], F32)
            pay1 = sbuf("pay1", [128, 2 * NCH], F32)
            rcv1 = sbuf("rcv1", [128, 2 * NCH], F32)
            pay2 = sbuf("pay2", [128, C], F32)
            rcv2fm = sbuf("rcv2fm", [128, NCH], F32)
            ycol0 = sbuf("ycol0", [128, NCH], F32)

        psA = [ctx.enter_context(nc.psum_tensor(f"psA{i}", [128, 512], F32))
               for i in range(2)]
        psB = [ctx.enter_context(nc.psum_tensor(f"psB{i}", [128, 512], F32))
               for i in range(2)]
        psW = [ctx.enter_context(nc.psum_tensor(f"psW{i}", [128, C], F32))
               for i in range(2)]
        psK3 = [psA[0][:], psA[1][:], psW[0][:, 0:512]]
        psV3 = [psB[0][:], psB[1][:], psW[0][:, 512:C]]

        S = lambda name: ctx.enter_context(nc.semaphore(name))
        s_init = S("s_init")
        s_wl = S("s_wl")
        s_wlo = S("s_wlo")
        s_wlr = S("s_wlr")
        s_wfk = S("s_wfk")
        s_wfr = S("s_wfr")
        s_wfv = S("s_wfv")
        s_xld = [S("s_xld0"), S("s_xld1")]
        s_T = S("s_T")
        s_ev = S("s_ev")
        s_bnd = S("s_bnd")
        s_bndd = S("s_bndd")
        s_xm = S("s_xm")
        s_kmm = S("s_kmm")
        s_k = S("s_k")
        s_vmm = S("s_vmm")
        s_kv = S("s_kv")
        s_scan = S("s_scan")
        s_tts = S("s_tts")
        s_gw = S("s_gw")
        s_stx = S("s_stx")
        s_stw = [S("s_stw0"), S("s_stw1")]
        s_sty = [S("s_sty0"), S("s_sty1"), S("s_sty2"), S("s_sty3")]
        s_sto = [S("s_sto0"), S("s_sto1")]
        s_l2x = [S("s_l2x0"), S("s_l2x1"), S("s_l2x2"), S("s_l2x3")]
        s_l2m = [S("s_l2m0"), S("s_l2m1")]
        s_l2w = [S("s_l2w0"), S("s_l2w1"), S("s_l2w2")]
        s_rmm = S("s_rmm")
        s_e = S("s_e")
        s_den = S("s_den")
        s_inv = S("s_inv")
        s_rw = S("s_rw")
        s_omm = S("s_omm")
        s_y1p = S("s_y1p")
        s_sq = S("s_sq")
        s_var = S("s_var")
        s_rsq = S("s_rsq")
        s_y1w = S("s_y1w")
        s_l3t = [S("s_l3t0"), S("s_l3t1")]
        s_l3r = [S("s_l3r0"), S("s_l3r1")]
        s_T3 = S("s_T3")
        s_ev3 = S("s_ev3")
        s_bnd3 = S("s_bnd3")
        s_mixk = S("s_mixk")
        s_mixr = S("s_mixr")
        s_kfmm = S("s_kfmm")
        s_kact = S("s_kact")
        s_rfmm = S("s_rfmm")
        s_fe = S("s_fe")
        s_sigf = S("s_sigf")
        s_vomm = S("s_vomm")
        s_pr = S("s_pr")
        s_sq3 = S("s_sq3")
        s_var3 = S("s_var3")
        s_rsq3 = S("s_rsq3")
        s_ow = S("s_ow")
        s_pay1 = S("s_pay1")
        s_ccd = S("s_ccd")
        s_ccc = S("s_ccc")
        s_ikv = S("s_ikv")
        s_dx1 = S("s_dx1")
        s_dx2 = S("s_dx2")
        s_dx3 = S("s_dx3")
        s_pay2 = S("s_pay2")
        s_rowmm = S("s_rowmm")
        s_ycol = S("s_ycol")
        s_misc = S("s_misc")
        s_misc2 = S("s_misc2")

        blk = ctx.enter_context(nc.Block())

        ph0 = ExitStack()
        iot_r = ph0.enter_context(nc.sbuf_tensor("iot_r", [128, 128], F32))
        iot_c = ph0.enter_context(nc.sbuf_tensor("iot_c", [128, 128], F32))
        if True:

            @blk.sync
            def _(sync):
                sync.dma_start(out=tm_c[:], in_=tm_d).then_inc(s_misc, 16)
                sync.dma_start(out=mur_c[:], in_=mur_d).then_inc(s_misc, 16)
                sync.dma_start(out=muk_c[:], in_=muk_d).then_inc(s_misc, 16)
                sync.dma_start(out=td_c[:], in_=td_d).then_inc(s_misc, 16)
                sync.dma_start(out=tf_c[:], in_=tf_d).then_inc(s_misc, 16)
                if exchange:
                    sync.dma_start(out=flagc[:], in_=flagc_d).then_inc(s_misc2, 16)
                    sync.dma_start(out=flagu[:], in_=flagu_d).then_inc(s_misc2, 16)

            @blk.gpsimd
            def _(g):
                g.iota(iot_r[:], [[0, 128]], channel_multiplier=1,
                       allow_small_or_imprecise_dtypes=True)
                g.iota(iot_c[:], [[1, 128]], channel_multiplier=0,
                       allow_small_or_imprecise_dtypes=True).then_inc(s_init, 1)

            @blk.vector
            def _(v):
                for col, val in [(CB_60, 60.0), (CB_M1, -1.0), (CB_1, 1.0),
                                 (CB_EPS16, 1e-16), (CB_LNEPS, 1e-5),
                                 (CB_M05, -0.5), (CB_M512, -512.0)]:
                    v.memset(cpool[:, col:col + 1], val)
                v.memset(carry_kv[:], 0.0)
                v.memset(carry_k[:], 0.0)
                v.memset(i_kv[:], 0.0)
                v.memset(i_k[:], 0.0)
                v.wait_ge(s_misc, 16 * 5)
                v.tensor_scalar(mur1_c[:], mur_c[:], -1.0, None, Alu.add)
                v.tensor_scalar(muk1_c[:], muk_c[:], -1.0, None, Alu.add)
                v.drain()
                v.tensor_scalar(mur1n_c[:], mur1_c[:], -1.0, None, Alu.mult)
                v.tensor_scalar(muk1n_c[:], muk1_c[:], -1.0, None, Alu.mult)
                v.wait_ge(s_init, 1)
                v.tensor_tensor(ident[:], iot_r[:], iot_c[:],
                                Alu.is_equal).then_inc(s_init, 1)

            @blk.scalar
            def _(sc):
                sc.wait_ge(s_misc, 16 * 5)
                sc.wait_ge(s_init, 2)
                sc.activation(dec_c[:], td_c[:], Act.Exp)
                sc.drain()
                sc.activation(wdec_c[:], dec_c[:], Act.Exp,
                              scale=cpool[:, CB_M1:CB_M1 + 1])
                sc.activation(w512_c[:], dec_c[:], Act.Exp,
                              scale=cpool[:, CB_M512:CB_M512 + 1])
                sc.activation(etf_c[:], tf_c[:], Act.Exp).then_inc(s_init, 1)


        @blk.gpsimd
        def _(g):
            g.wait_ge(s_init, 3)
            g.iota(sq_junk[:, 0:TILE], [[1, TILE]], channel_multiplier=0,
                   allow_small_or_imprecise_dtypes=True).then_inc(s_dx1, 1)

        @blk.vector
        def _(v):
            v.wait_ge(s_dx1, 1)
            v.wait_ge(s_init, 3)
            for j in range(NCH):
                i = v.tensor_scalar(dexpt[:, j, :], sq_junk[:, 0:TILE],
                                    dec_c[:, j:j + 1], None, Alu.mult)
            i.then_inc(s_dx2, 1)

        @blk.scalar
        def _(sc):
            sc.wait_ge(s_dx2, 1)
            for j in range(NCH):
                i = sc.activation(dexpt[:, j, :],
                                  dexpt[:, j, :].bitcast(F32), Act.Exp,
                                  scale=cpool[:, CB_M1:CB_M1 + 1])
            i.then_inc(s_dx3, 1)

        ph0.close()
        ph1 = ExitStack()
        x_tm0 = ph1.enter_context(nc.sbuf_tensor("x_tm0", [128, C], F32R))
        x_tm1 = ph1.enter_context(nc.sbuf_tensor("x_tm1", [128, C], F32R))
        x_fm = ph1.enter_context(nc.sbuf_tensor("x_fm", [128, NCH, TILE + 1], F32))
        d_full = ph1.enter_context(nc.sbuf_tensor("d_full", [128, NCH, TILE], F32))
        xm_fm = ph1.enter_context(nc.sbuf_tensor("xm_fm", [128, NCH, TILE], F32R))
        k_sb0 = ph1.enter_context(nc.sbuf_tensor("k_sb0", [128, TILE], F32))
        k_sb1 = ph1.enter_context(nc.sbuf_tensor("k_sb1", [128, TILE], F32))
        k_sb2 = ph1.enter_context(nc.sbuf_tensor("k_sb2", [128, TILE], F32))
        kv_sb0 = ph1.enter_context(nc.sbuf_tensor("kv_sb0", [128, TILE], F32))
        kv_sb1 = ph1.enter_context(nc.sbuf_tensor("kv_sb1", [128, TILE], F32))
        kv_sb2 = ph1.enter_context(nc.sbuf_tensor("kv_sb2", [128, TILE], F32))
        s_buf0 = ph1.enter_context(nc.sbuf_tensor("s_buf0", [128, TILE + 1], F32))
        s_buf1 = ph1.enter_context(nc.sbuf_tensor("s_buf1", [128, TILE + 1], F32))
        s_buf2 = ph1.enter_context(nc.sbuf_tensor("s_buf2", [128, TILE + 1], F32))
        sk_buf0 = ph1.enter_context(nc.sbuf_tensor("sk_buf0", [128, TILE + 1], F32))
        sk_buf1 = ph1.enter_context(nc.sbuf_tensor("sk_buf1", [128, TILE + 1], F32))
        sk_buf2 = ph1.enter_context(nc.sbuf_tensor("sk_buf2", [128, TILE + 1], F32))
        wkv_sb0 = ph1.enter_context(nc.sbuf_tensor("wkv_sb0", [128, TILE], F32))
        wkv_sb1 = ph1.enter_context(nc.sbuf_tensor("wkv_sb1", [128, TILE], F32))
        sumk_sb0 = ph1.enter_context(nc.sbuf_tensor("sumk_sb0", [128, TILE], F32))
        sumk_sb1 = ph1.enter_context(nc.sbuf_tensor("sumk_sb1", [128, TILE], F32))
        if True:
            x_tm = [x_tm0, x_tm1]
            k_sb, kv_sb = [k_sb0, k_sb1, k_sb2], [kv_sb0, kv_sb1, kv_sb2]
            s_bufs, sk_bufs = [s_buf0, s_buf1, s_buf2], [sk_buf0, sk_buf1, sk_buf2]
            wkv_sb, sumk_sb = [wkv_sb0, wkv_sb1], [sumk_sb0, sumk_sb1]

            @blk.gpsimd
            def _(g):
                g.wait_ge(s_init, 3)
                g.nop().then_inc(s_init, 1)
                g.dma_start(out=x_fm[:, :, 0:1],
                            in_=xprev_d.rearrange("p (a b) -> p a b", b=1),
                            allow_slow_non_contiguous=True
                            ).then_inc(s_bndd, 16)


            @blk.scalar
            def _(sc):
                sc.dma_start(out=wslotA[:], in_=w_d["WkT"].rearrange(
                    "(a p) c -> p a c", p=128)).then_inc(s_wl, 16)
                sc.dma_start(out=wslotB[:], in_=w_d["WvT"].rearrange(
                    "(a p) c -> p a c", p=128)).then_inc(s_wl, 16)
                sc.dma_start(out=wslotC[:], in_=w_d["WoT"].rearrange(
                    "(a p) c -> p a c", p=128)).then_inc(s_wlo, 16)

            @blk.sync
            def _(sync):
                sync.wait_ge(s_init, 3)
                for ti in range(NT):
                    for sl in range(NSL):
                        gs = ti * NSL + sl
                        if gs >= 2:
                            sync.wait_ge(s_T, gs - 1)
                        sync.dma_start(
                            out=x_tm[gs % 2][:],
                            in_=xr_view[:, gs, :],
                        ).then_inc(s_xld[gs % 2], 16)

            @blk.tensor
            def _(t):
                t.wait_ge(s_init, 2)
                for ti in range(NT):
                    for sl in range(NSL):
                        gs = ti * NSL + sl
                        t.wait_ge(s_xld[gs % 2], 16 * (gs // 2 + 1))
                        if gs >= 1:
                            t.wait_ge(s_ev, gs)
                        if ti >= 1 and sl == 0:
                            t.wait_ge(s_k, ti * NCH)
                        for j in range(NCH):
                            dst = psA[0] if j < 4 else psA[1]
                            i = t.transpose(
                                dst[:].bitcast(F32R)[:, (j % 4) * 128:(j % 4 + 1) * 128],
                                x_tm[gs % 2][:, j * 128:(j + 1) * 128],
                                ident[:])
                        i.then_inc(s_T, 1)
                    t.wait_ge(s_xm, ti + 1)
                    for j in range(NCH):
                        g = ti * NCH + j
                        if g >= 3:
                            t.wait_ge(s_k, g - 2)
                        pk = psK3[g % 3]
                        for c in range(NCH):
                            i = t.matmul(pk,
                                         wslotA[:, c, j * 128:(j + 1) * 128],
                                         xm_fm[:, c, :],
                                         start=(c == 0), stop=(c == NCH - 1))
                        i.then_inc(s_kmm, 1)
                        if g >= 3:
                            t.wait_ge(s_kv, g - 2)
                        pv = psV3[g % 3]
                        for c in range(NCH):
                            i = t.matmul(pv,
                                         wslotB[:, c, j * 128:(j + 1) * 128],
                                         xm_fm[:, c, :],
                                         start=(c == 0), stop=(c == NCH - 1))
                        i.then_inc(s_vmm, 1)

            @blk.scalar
            def _(sc):
                sc.wait_ge(s_wl, 32)
                for ti in range(NT):
                    for sl in range(NSL):
                        gs = ti * NSL + sl
                        sc.wait_ge(s_T, gs + 1)
                        if ti >= 1 and sl == 0:
                            sc.wait_ge(s_xm, ti)
                        if sl == NSL - 1 and ti >= 1:
                            sc.wait_ge(s_bnd, ti)
                        sc.copy(x_fm[:, 0:4, 1 + sl * 128:1 + (sl + 1) * 128],
                                psA[0][:].bitcast(F32R).rearrange(
                                    "p (a b) -> p a b", a=4))
                        sc.copy(x_fm[:, 4:8, 1 + sl * 128:1 + (sl + 1) * 128],
                                psA[1][:].bitcast(F32R).rearrange(
                                    "p (a b) -> p a b", a=4)).then_inc(s_ev, 1)
                    for j in range(NCH):
                        g = ti * NCH + j
                        sc.wait_ge(s_kmm, g + 1)
                        if g >= 3:
                            sc.wait_ge(s_scan, g - 2)
                        kb = k_sb[g % 3]
                        sc.activation(kb[:], psK3[g % 3], Act.Relu,
                                      bias=cpool[:, CB_60:CB_60 + 1],
                                      scale=cpool[:, CB_M1:CB_M1 + 1])
                        sc.drain()
                        sc.activation(kb[:], kb[:], Act.Exp,
                                      bias=cpool[:, CB_60:CB_60 + 1],
                                      scale=cpool[:, CB_M1:CB_M1 + 1]
                                      ).then_inc(s_k, 1)

            @blk.vector
            def _(v):
                v.wait_ge(s_init, 4)
                for ti in range(NT):
                    v.wait_ge(s_ev, ti * NSL + NSL)
                    if ti == 0:
                        v.wait_ge(s_bndd, 16)
                    else:
                        v.wait_ge(s_stx, 16 * ti)
                    v.drain()
                    v.tensor_tensor(d_full[:], x_fm[:, :, 1:TILE + 1],
                                    x_fm[:, :, 0:TILE], Alu.subtract)
                    v.drain()
                    for j in range(NCH):
                        i = v.scalar_tensor_tensor(
                            xm_fm[:, j, :], d_full[:, j, :], tm_c[:, j:j + 1],
                            x_fm[:, j, 0:TILE], Alu.mult, Alu.add)
                    i.then_inc(s_xm, 1)
                    v.drain()
                    v.tensor_copy(x_fm[:, :, 0:1],
                                  x_fm[:, :, TILE:TILE + 1]).then_inc(s_bnd, 1)
                    g0 = ti * NCH
                    v.wait_ge(s_k, g0 + 1)
                    v.wait_ge(s_vmm, g0 + 1)

                    v.tensor_tensor(kv_sb[g0 % 3][:], k_sb[g0 % 3][:],
                                    psV3[g0 % 3], Alu.mult).then_inc(s_kv, 1)
                    v.tensor_copy(s_bufs[g0 % 3][:, 0:1], carry_kv[:, 0:1])
                    v.tensor_copy(sk_bufs[g0 % 3][:, 0:1], carry_k[:, 0:1])
                    for j in range(NCH + 1):
                        g = ti * NCH + j
                        v.drain()
                        if j < NCH:
                            wj = wdec_c[:, j:j + 1].broadcast_to([128, TILE])
                            v.tensor_tensor_scan(s_bufs[g % 3][:, 1:TILE + 1],
                                                 wj, kv_sb[g % 3][:],
                                                 s_bufs[g % 3][:, 0:1],
                                                 Alu.mult, Alu.add)
                            i = v.tensor_tensor_scan(sk_bufs[g % 3][:, 1:TILE + 1],
                                                 wj, k_sb[g % 3][:],
                                                 sk_bufs[g % 3][:, 0:1],
                                                 Alu.mult, Alu.add)
                        if j >= 1:
                            jj, gg = j - 1, g - 1
                            if gg >= 2:
                                v.wait_ge(s_stw[gg % 2], 32 * (gg // 2))
                            v.scalar_tensor_tensor(
                                wkv_sb[gg % 2][:], kv_sb[gg % 3][:],
                                etf_c[:, jj:jj + 1],
                                s_bufs[gg % 3][:, 0:TILE], Alu.mult, Alu.add)
                            v.scalar_tensor_tensor(
                                sumk_sb[gg % 2][:], k_sb[gg % 3][:],
                                etf_c[:, jj:jj + 1],
                                sk_bufs[gg % 3][:, 0:TILE], Alu.mult, Alu.add)
                            v.tensor_copy(carry_kv[:, jj:jj + 1],
                                          s_bufs[gg % 3][:, TILE:TILE + 1])
                            v.tensor_copy(carry_k[:, jj:jj + 1],
                                          sk_bufs[gg % 3][:, TILE:TILE + 1]
                                          ).then_inc(s_scan, 1)
                        if j + 1 < NCH:
                            gn = g + 1
                            v.wait_ge(s_k, gn + 1)
                            v.wait_ge(s_vmm, gn + 1)

                            v.tensor_tensor(kv_sb[gn % 3][:], k_sb[gn % 3][:],
                                            psV3[gn % 3], Alu.mult
                                            ).then_inc(s_kv, 1)
                            v.tensor_copy(s_bufs[gn % 3][:, 0:1],
                                          carry_kv[:, j + 1:j + 2])
                            v.tensor_copy(sk_bufs[gn % 3][:, 0:1],
                                          carry_k[:, j + 1:j + 2])

            @blk.gpsimd
            def _(g):
                for ti in range(NT):
                    g.wait_ge(s_xm, ti + 1)
                    g.dma_start(out=xm_d[ti].rearrange("a p t -> p a t"),
                                in_=xm_fm[:]).then_inc(s_stx, 16)
                    for j in range(NCH):
                        gg = ti * NCH + j
                        g.wait_ge(s_scan, gg + 1)
                        g.dma_start(out=wkv_d[ti, j], in_=wkv_sb[gg % 2][:]
                                    ).then_inc(s_stw[gg % 2], 16)
                        g.dma_start(out=sumk_d[ti, j], in_=sumk_sb[gg % 2][:]
                                    ).then_inc(s_stw[gg % 2], 16)

            if exchange:
                @blk.vector
                def _(v):
                    v.wait_ge(s_misc2, 32)
                    v.drain()
                    v.tensor_scalar(pay1[:, 0:NCH], carry_kv[:],
                                    flagc[:, 0:1], None, Alu.mult)
                    v.tensor_scalar(pay1[:, NCH:2 * NCH], carry_k[:],
                                    flagc[:, 0:1], None, Alu.mult
                                    ).then_inc(s_pay1, 1)

                @blk.gpsimd
                def _(g):
                    g.wait_ge(s_pay1, 1)
                    g.dma_start(out=cc1_in.ap(), in_=pay1[:]).then_inc(s_ccd, 16)
                    g.wait_ge(s_ccd, 16)
                    g.collective_compute(
                        "AllReduce", Alu.add,
                        replica_groups=[[0, 1], [2, 3], [4, 5], [6, 7]],
                        ins=[cc1_in.ap()], outs=[cc1_out.ap()],
                    ).then_inc(s_ccc, 1)
                    g.wait_ge(s_ccc, 1)
                    g.dma_start(out=rcv1[:], in_=cc1_out.ap()).then_inc(s_ccd, 16)

                @blk.vector
                def _(v):
                    v.wait_ge(s_ccd, 32)
                    v.tensor_scalar(i_kv[:], rcv1[:, 0:NCH], flagu[:, 0:1],
                                    None, Alu.mult)
                    v.tensor_scalar(i_k[:], rcv1[:, NCH:2 * NCH], flagu[:, 0:1],
                                    None, Alu.mult).then_inc(s_ikv, 1)

        ph1.close()
        ph2a = ExitStack()
        xm2 = [ph2a.enter_context(nc.sbuf_tensor(f"xm2_{i}", [128, NCH, TILE], F32R)) for i in range(2)]
        wkv_in = [ph2a.enter_context(nc.sbuf_tensor(f"wkv_in{i}", [128, TILE], F32)) for i in range(3)]
        sumk_in = [ph2a.enter_context(nc.sbuf_tensor(f"sumk_in{i}", [128, TILE], F32)) for i in range(3)]
        e_sb = [ph2a.enter_context(nc.sbuf_tensor(f"e_sb{i}", [128, TILE], F32)) for i in range(3)]
        rwkv = ph2a.enter_context(nc.sbuf_tensor("rwkv", [128, NCH, TILE], F32R))
        y1p = [ph2a.enter_context(nc.sbuf_tensor(f"y1p{i}", [128, C], F32R)) for i in range(4)]
        if True:

            @blk.scalar
            def _(sc):
                sc.wait_ge(s_kmm, NT * NCH)
                sc.dma_start(out=wslotA[:], in_=w_d["WrT"].rearrange(
                    "(a p) c -> p a c", p=128)).then_inc(s_wlr, 16)
                sc.wait_ge(s_vmm, NT * NCH)
                sc.dma_start(out=wslotB[:], in_=w_d["fWkT"].rearrange(
                    "(a p) c -> p a c", p=128)).then_inc(s_wfk, 16)

            @blk.sync
            def _(sync):
                sync.wait_ge(s_stx, 16 * NT)
                sync.wait_ge(s_stw[0], 32 * 4 * NT)
                sync.wait_ge(s_stw[1], 32 * 4 * NT)
                def xm2_load(sync, ti):
                    if ti >= 2:
                        sync.wait_ge(s_rmm, (ti - 1) * NCH)
                    sync.dma_start(out=xm2[ti % 2][:],
                                   in_=xm_d[ti].rearrange("a p t -> p a t")
                                   ).then_inc(s_l2m[ti % 2], 16)

                xm2_load(sync, 0)
                xm2_load(sync, 1)
                for ti in range(NT):
                    for j in range(NCH):
                        g = ti * NCH + j
                        if g >= 3:
                            sync.wait_ge(s_rw, g - 2)
                        sync.dma_start(out=wkv_in[g % 3][:], in_=wkv_d[ti, j]
                                       ).then_inc(s_l2w[g % 3], 16)
                        sync.dma_start(out=sumk_in[g % 3][:], in_=sumk_d[ti, j]
                                       ).then_inc(s_l2w[g % 3], 16)
                    for sl in range(NSL):
                        gs = ti * NSL + sl
                        if gs >= 4:
                            sync.wait_ge(s_sty[gs % 4], 16 * (gs // 4))
                        sync.dma_start(out=y1p[gs % 4][:],
                                       in_=xr_view[:, gs, :]
                                       ).then_inc(s_l2x[gs % 4], 16)
                    if ti + 2 < NT:
                        xm2_load(sync, ti + 2)

            @blk.tensor
            def _(t):
                t.wait_ge(s_wlr, 16)
                t.wait_ge(s_wlo, 16)

                def r_proj(t, ti, j0, j1):
                    if j0 == 0:
                        t.wait_ge(s_l2m[ti % 2], 16 * (ti // 2 + 1))
                    for j in range(j0, j1):
                        g = ti * NCH + j
                        if g >= 2:
                            t.wait_ge(s_e, g - 1)
                        for c in range(NCH):
                            i = t.matmul(psA[j % 2][:],
                                         wslotA[:, c, j * 128:(j + 1) * 128],
                                         xm2[ti % 2][:, c, :],
                                         start=(c == 0), stop=(c == NCH - 1))
                        i.then_inc(s_rmm, 1)

                def wo_proj(t, ti):
                    t.wait_ge(s_rw, ti * NCH + NCH)
                    for sl in range(NSL):
                        gs = ti * NSL + sl
                        if gs >= 2:
                            t.wait_ge(s_y1p, gs - 1)
                        for j in range(NCH):
                            for h in range(2):
                                i = t.matmul(
                                    psW[sl % 2][:, h * 512:(h + 1) * 512],
                                    rwkv[:, j, sl * 128:(sl + 1) * 128],
                                    wslotC[:, j, h * 512:(h + 1) * 512],
                                    start=(j == 0), stop=(j == NCH - 1))
                        i.then_inc(s_omm, 1)

                for ti in range(NT):
                    r_proj(t, ti, 0, NCH)
                    wo_proj(t, ti)

            @blk.scalar
            def _(sc):
                for ti in range(NT):
                    for j in range(NCH):
                        g = ti * NCH + j
                        sc.wait_ge(s_rmm, g + 1)
                        if g >= 3:
                            sc.wait_ge(s_rw, g - 2)
                        sc.activation(e_sb[g % 3][:], psA[j % 2][:], Act.Exp,
                                      scale=cpool[:, CB_M1:CB_M1 + 1]
                                      ).then_inc(s_e, 1)
                        sc.wait_ge(s_den, g + 1)
                        sc.drain()
                        sc.activation(e_sb[g % 3][:], e_sb[g % 3][:],
                                      Act.Ln,
                                      bias=cpool[:, CB_EPS16:CB_EPS16 + 1])
                        sc.drain()
                        sc.activation(e_sb[g % 3][:], e_sb[g % 3][:],
                                      Act.Exp, scale=cpool[:, CB_M1:CB_M1 + 1]
                                      ).then_inc(s_inv, 1)
                    for sl in range(NSL):
                        gs = ti * NSL + sl
                        sc.wait_ge(s_y1p, gs + 1)
                        sc.drain()
                        sc.activation(sq_junk[:],
                                      y1p[gs % 4][:, 0:512].bitcast(F32),
                                      Act.Square,
                                      accum_out=sumsq[:, 2 * sl:2 * sl + 1])
                        sc.drain()
                        sc.activation(sq_junk[:],
                                      y1p[gs % 4][:, 512:C].bitcast(F32),
                                      Act.Square,
                                      accum_out=sumsq[:, 2 * sl + 1:2 * sl + 2]
                                      ).then_inc(s_sq, 1)
                    sc.wait_ge(s_var, ti + 1)
                    sc.activation(is_col[:], v_col[:], Act.Ln,
                                  bias=cpool[:, CB_LNEPS:CB_LNEPS + 1])
                    sc.drain()
                    sc.activation(is_col[:], is_col[:], Act.Exp,
                                  scale=cpool[:, CB_M05:CB_M05 + 1]
                                  ).then_inc(s_rsq, 1)

            @blk.vector
            def _(v):
                if exchange:
                    v.wait_ge(s_ikv, 1)
                v.wait_ge(s_dx3, 1)
                for ti in range(NT):
                    if ti >= 1:
                        v.wait_ge(s_omm, ti * NSL)
                    for j in range(NCH):
                        g = ti * NCH + j
                        v.wait_ge(s_l2w[g % 3], 32 * (g // 3 + 1))
                        v.wait_ge(s_e, g + 1)
                        if g >= 3:
                            v.wait_ge(s_inv, g - 2)
                        if exchange:
                            v.scalar_tensor_tensor(
                                wkv_in[g % 3][:], dexpt[:, j, :].bitcast(F32),
                                i_kv[:, j:j + 1], wkv_in[g % 3][:],
                                Alu.mult, Alu.add)
                            v.scalar_tensor_tensor(
                                sumk_in[g % 3][:], dexpt[:, j, :].bitcast(F32),
                                i_k[:, j:j + 1], sumk_in[g % 3][:],
                                Alu.mult, Alu.add)
                            v.drain()
                        v.scalar_tensor_tensor(
                            e_sb[g % 3][:], e_sb[g % 3][:], 1.0,
                            sumk_in[g % 3][:], Alu.add, Alu.mult
                            ).then_inc(s_den, 1)
                        v.wait_ge(s_inv, g + 1)
                        v.tensor_tensor(rwkv[:, j, :], wkv_in[g % 3][:],
                                        e_sb[g % 3][:], Alu.mult
                                        ).then_inc(s_rw, 1)

                    if exchange:
                        v.tensor_tensor(i_kv[:], i_kv[:], w512_c[:], Alu.mult)
                        v.tensor_tensor(i_k[:], i_k[:], w512_c[:], Alu.mult)
                        v.drain()
                    for sl in range(NSL):
                        gs = ti * NSL + sl
                        v.wait_ge(s_omm, gs + 1)
                        v.wait_ge(s_l2x[gs % 4], 16 * (gs // 4 + 1))
                        v.scalar_tensor_tensor(
                            y1p[gs % 4][:], psW[sl % 2][:], 1.0,
                            y1p[gs % 4][:].bitcast(F32),
                            Alu.mult, Alu.add,
                            accum_out=sums[:, sl:sl + 1]).then_inc(s_y1p, 1)
                    v.wait_ge(s_sq, ti * NSL + NSL)
                    v.drain()
                    v.tensor_scalar(m_col[:], sums[:], 1.0 / C, None, Alu.mult)
                    v.tensor_tensor(v_col[:], sumsq[:].rearrange(
                        "p (a b) -> p a b", b=2)[:, :, 0],
                        sumsq[:].rearrange("p (a b) -> p a b", b=2)[:, :, 1],
                        Alu.add)
                    v.drain()
                    v.tensor_tensor(b_col[:], m_col[:], m_col[:], Alu.mult)
                    v.drain()
                    v.scalar_tensor_tensor(v_col[:], v_col[:], 1.0 / C,
                                           b_col[:], Alu.mult, Alu.subtract
                                           ).then_inc(s_var, 1)
                    v.wait_ge(s_rsq, ti + 1)
                    v.scalar_tensor_tensor(b_col[:], m_col[:], -1.0,
                                           is_col[:], Alu.mult, Alu.mult)
                    v.drain()
                    for sl in range(NSL):
                        gs = ti * NSL + sl
                        v.tensor_scalar(y1p[gs % 4][:],
                                        y1p[gs % 4][:].bitcast(F32),
                                        is_col[:, sl:sl + 1],
                                        b_col[:, sl:sl + 1],
                                        Alu.mult, Alu.add).then_inc(s_y1w, 1)


            @blk.gpsimd
            def _(g):
                for ti in range(NT):
                    for sl in range(NSL):
                        gs = ti * NSL + sl
                        g.wait_ge(s_y1w, gs + 1)
                        g.dma_start(out=y1_d[ti, sl], in_=y1p[gs % 4][:]
                                    ).then_inc(s_sty[gs % 4], 16)

            if exchange:
                LASTS = (NT * NSL - 1) % 4

                @blk.vector
                def _(v):
                    v.wait_ge(s_y1w, NT * NSL)
                    v.drain()
                    v.tensor_scalar(pay2[:], y1p[LASTS][:].bitcast(F32),
                                    flagc[:, 0:1], None, Alu.mult
                                    ).then_inc(s_pay2, 1)

                @blk.gpsimd
                def _(g):
                    g.wait_ge(s_pay2, 1)
                    g.dma_start(out=cc2_in.ap(), in_=pay2[127:128, :]
                                ).then_inc(s_ccd, 16)
                    g.wait_ge(s_ccd, 48)
                    g.collective_compute(
                        "AllReduce", Alu.add,
                        replica_groups=[[0, 1], [2, 3], [4, 5], [6, 7]],
                        ins=[cc2_in.ap()], outs=[cc2_out.ap()],
                    ).then_inc(s_ccc, 1)
                    g.wait_ge(s_ccc, 2)

                @blk.gpsimd
                def _(g):
                    g.dma_start(out=rcv2fm[:],
                                in_=cc2_out.ap().rearrange(
                                    "o (a p) -> (o p) a", p=128),
                                allow_slow_non_contiguous=True
                                ).then_inc(s_ccd, 16)

                @blk.vector
                def _(v):
                    v.wait_ge(s_ccd, 64)
                    v.tensor_scalar(ycol0[:], rcv2fm[:],
                                    flagu[:, 0:1], None, Alu.mult
                                    ).then_inc(s_ycol, 1)

        ph2a.close()
        ph2b = ExitStack()
        d_buf = [ph2b.enter_context(nc.sbuf_tensor(f"d_buf{i}", [128, TILE], F32))
                 for i in range(2)]
        y1t = [ph2b.enter_context(nc.sbuf_tensor(f"y1t{i}", [128, C], F32R))
               for i in range(2)]
        y1r = [ph2b.enter_context(nc.sbuf_tensor(f"y1r{i}", [128, C], F32R))
               for i in range(2)]
        y1fm = ph2b.enter_context(nc.sbuf_tensor("y1fm", [128, NCH, TILE + 1], F32))
        ybcol = ph2b.enter_context(nc.sbuf_tensor("ybcol", [128, NCH], F32))
        xr_fm = ph2b.enter_context(nc.sbuf_tensor("xr_fm", [128, NCH, TILE], F32R))
        xk_fm = dexpt
        kact = ph2b.enter_context(nc.sbuf_tensor("kact", [128, NCH, TILE], F32R))
        sigf = [ph2b.enter_context(nc.sbuf_tensor(f"sigf{i}", [128, C], F32))
                for i in range(2)]
        y2p = [ph2b.enter_context(nc.sbuf_tensor(f"y2p{i}", [128, C], F32))
               for i in range(2)]
        if True:
            @blk.sync
            def _(sync):
                for _q in range(4):
                    sync.wait_ge(s_sty[_q], 16 * NT)
                sync.wait_ge(s_rmm, NT * NCH)
                sync.wait_ge(s_y1w, NT * NSL)

                def y1t_loads(sync, ti):
                    for sl in range(NSL):
                        gs = ti * NSL + sl
                        if gs >= 2:
                            sync.wait_ge(s_T3, gs - 1)
                        sync.dma_start(out=y1t[gs % 2][:], in_=y1_d[ti, sl]
                                       ).then_inc(s_l3t[gs % 2], 16)

                y1t_loads(sync, 0)
                sync.dma_start(out=wslotA[:], in_=w_d["fWrT"].rearrange(
                    "(a p) c -> p a c", p=128)).then_inc(s_wfr, 16)
                sync.dma_start(out=wslotC[:], in_=w_d["fWvT"].rearrange(
                    "(a p) c -> p a c", p=128)).then_inc(s_wfv, 16)
                for ti in range(NT):
                    if ti + 1 < NT:
                        y1t_loads(sync, ti + 1)
                    for sl in range(NSL):
                        gs = ti * NSL + sl
                        if gs >= 2:
                            sync.wait_ge(s_pr, gs - 1)
                        sync.dma_start(out=y1r[gs % 2][:], in_=y1_d[ti, sl]
                                       ).then_inc(s_l3r[gs % 2], 16)

            @blk.vector
            def _(v):
                v.memset(ybcol[:], 0.0).then_inc(s_bnd3, 1)
                v.drain()

            @blk.tensor
            def _(t):
                t.wait_ge(s_wfk, 16)

                def transp(t, k, sl):
                    gs = k * NSL + sl
                    t.wait_ge(s_l3t[gs % 2], 16 * (gs // 2 + 1))
                    if gs >= 1:
                        t.wait_ge(s_ev3, gs)
                    for j in range(NCH):
                        dst = psA[0] if j < 4 else psA[1]
                        i = t.transpose(
                            dst[:].bitcast(F32R)[:, (j % 4) * 128:(j % 4 + 1) * 128],
                            y1t[gs % 2][:, j * 128:(j + 1) * 128],
                            ident[:])
                    i.then_inc(s_T3, 1)

                def rf_mm(t, ti, sl):
                    gs = ti * NSL + sl
                    if gs == 0:
                        t.wait_ge(s_wfr, 16)
                    if gs >= 2:
                        t.wait_ge(s_pr, gs - 1)
                    for c in range(NCH):
                        for h in range(2):
                            i = t.matmul(
                                psW[sl % 2][:, h * 512:(h + 1) * 512],
                                xr_fm[:, c, sl * 128:(sl + 1) * 128],
                                wslotA[:, c, h * 512:(h + 1) * 512],
                                start=(c == 0), stop=(c == NCH - 1))
                    i.then_inc(s_rfmm, 1)

                def vo_mm(t, ti, sl):
                    gs = ti * NSL + sl
                    t.wait_ge(s_kact, ti * NCH + NCH)
                    if gs == 0:
                        t.wait_ge(s_wfv, 16)
                    t.wait_ge(s_fe, gs + 1)
                    for j in range(NCH):
                        for h in range(2):
                            i = t.matmul(
                                psW[sl % 2][:, h * 512:(h + 1) * 512],
                                kact[:, j, sl * 128:(sl + 1) * 128],
                                wslotC[:, j, h * 512:(h + 1) * 512],
                                start=(j == 0), stop=(j == NCH - 1))
                    i.then_inc(s_vomm, 1)

                for sl in range(NSL):
                    transp(t, 0, sl)
                for ti in range(NT):
                    nxt = ti + 1 < NT
                    if nxt:
                        transp(t, ti + 1, 0)
                    t.wait_ge(s_mixk, ti * NCH + NCH)
                    for j in range(NCH):
                        g = ti * NCH + j
                        if g >= 2:
                            t.wait_ge(s_kact, g - 1)
                        for c in range(NCH):
                            i = t.matmul(psB[j % 2][:],
                                         wslotB[:, c, j * 128:(j + 1) * 128],
                                         xk_fm[:, c, :],
                                         start=(c == 0), stop=(c == NCH - 1))
                        i.then_inc(s_kfmm, 1)
                    t.wait_ge(s_mixr, ti * NCH + NCH)
                    for sl in range(NSL):
                        if nxt and sl + 1 < NSL:
                            transp(t, ti + 1, sl + 1)
                        rf_mm(t, ti, sl)
                        vo_mm(t, ti, sl)

            @blk.scalar
            def _(sc):
                def evac(sc, k, sl):
                    gs = k * NSL + sl
                    sc.wait_ge(s_T3, gs + 1)
                    if k >= 1 and sl == 0:
                        sc.wait_ge(s_mixr, k * NCH)
                    if sl == NSL - 1:
                        sc.wait_ge(s_bnd3, k + 1)
                    sc.copy(y1fm[:, 0:4, 1 + sl * 128:1 + (sl + 1) * 128],
                            psA[0][:].bitcast(F32R).rearrange(
                                "p (a b) -> p a b", a=4))
                    sc.copy(y1fm[:, 4:8, 1 + sl * 128:1 + (sl + 1) * 128],
                            psA[1][:].bitcast(F32R).rearrange(
                                "p (a b) -> p a b", a=4)).then_inc(s_ev3, 1)

                def sig_block(sc, ti, sl):
                    gs = ti * NSL + sl
                    sc.wait_ge(s_rfmm, gs + 1)
                    sf = sigf[sl % 2]
                    if gs >= 2:
                        sc.wait_ge(s_pr, gs - 1)
                    sc.activation(sf[:], psW[sl % 2][:], Act.Exp,
                                  scale=cpool[:, CB_M1:CB_M1 + 1]
                                  ).then_inc(s_fe, 1)
                    sc.drain()
                    sc.activation(sf[:], sf[:], Act.Ln,
                                  bias=cpool[:, CB_1:CB_1 + 1])
                    sc.drain()
                    sc.activation(sf[:], sf[:], Act.Exp,
                                  scale=cpool[:, CB_M1:CB_M1 + 1]
                                  ).then_inc(s_sigf, 1)

                def sq_rsq(sc, ti, sl):
                    gs = ti * NSL + sl
                    sc.wait_ge(s_pr, gs + 1)
                    sc.drain()
                    sc.activation(sq_junk[:], y2p[sl % 2][:, 0:512],
                                  Act.Square,
                                  accum_out=sumsq[:, 2 * sl:2 * sl + 1])
                    sc.drain()
                    sc.activation(sq_junk[:], y2p[sl % 2][:, 512:C],
                                  Act.Square,
                                  accum_out=sumsq[:, 2 * sl + 1:2 * sl + 2]
                                  ).then_inc(s_sq3, 1)
                    sc.wait_ge(s_var3, gs + 1)
                    sc.activation(is_col[:, sl:sl + 1], v_col[:, sl:sl + 1],
                                  Act.Ln,
                                  bias=cpool[:, CB_LNEPS:CB_LNEPS + 1])
                    sc.drain()
                    sc.activation(is_col[:, sl:sl + 1], is_col[:, sl:sl + 1],
                                  Act.Exp, scale=cpool[:, CB_M05:CB_M05 + 1]
                                  ).then_inc(s_rsq3, 1)

                for sl in range(NSL):
                    evac(sc, 0, sl)
                for ti in range(NT):
                    nxt = ti + 1 < NT
                    for j in range(NCH):
                        g = ti * NCH + j
                        sc.wait_ge(s_kfmm, g + 1)
                        sc.drain()
                        sc.activation(sq_junk[:], psB[j % 2][:], Act.Relu)
                        sc.drain()
                        sc.activation(kact[:, j, :], sq_junk[:], Act.Square
                                      ).then_inc(s_kact, 1)
                    for sl in range(NSL):
                        if nxt:
                            evac(sc, ti + 1, sl)
                        sig_block(sc, ti, sl)
                        if sl >= 1:
                            sq_rsq(sc, ti, sl - 1)
                    sq_rsq(sc, ti, NSL - 1)

            @blk.vector
            def _(v):
                v.memset(ybcol[:], 0.0).then_inc(s_bnd3, 1)
                v.drain()

                def xk_pass(v, k):
                    pf = (k == 0 and exchange)
                    if k >= 1:
                        v.wait_ge(s_kfmm, k * NCH)
                    if k == 0:
                        v.tensor_copy(y1fm[:, :, 0:1],
                                      ybcol[:].rearrange("p (a b) -> p a b", b=1))
                    v.wait_ge(s_ev3, k * NSL + NSL)
                    for j in range(NCH + 1):
                        v.drain()
                        if j < NCH:
                            v.tensor_tensor(d_buf[j % 2][:],
                                            y1fm[:, j, 1:TILE + 1],
                                            y1fm[:, j, 0:TILE], Alu.subtract)
                        if j >= 1:
                            jj = j - 1
                            i = v.scalar_tensor_tensor(
                                xk_fm[:, jj, :], d_buf[jj % 2][:],
                                muk1_c[:, jj:jj + 1],
                                y1fm[:, jj, 1:TILE + 1], Alu.mult, Alu.add)
                            if not pf:
                                i.then_inc(s_mixk, 1)
                    v.drain()
                    if pf:
                        v.wait_ge(s_ycol, 1)
                        v.drain()
                        for j in range(NCH):
                            v.scalar_tensor_tensor(
                                xk_fm[:, j, 0:1], ycol0[:, j:j + 1],
                                muk1n_c[:, j:j + 1],
                                xk_fm[:, j, 0:1], Alu.mult, Alu.add
                                ).then_inc(s_mixk, 1)
                        v.drain()

                def xr_pass(v, k):
                    pf = (k == 0 and exchange)
                    if k >= 1:
                        v.wait_ge(s_rfmm, k * NSL)
                    for j in range(NCH + 1):
                        v.drain()
                        if j < NCH:
                            v.tensor_tensor(d_buf[j % 2][:],
                                            y1fm[:, j, 1:TILE + 1],
                                            y1fm[:, j, 0:TILE], Alu.subtract)
                        if j >= 1:
                            jj = j - 1
                            i = v.scalar_tensor_tensor(
                                xr_fm[:, jj, :], d_buf[jj % 2][:],
                                mur1_c[:, jj:jj + 1],
                                y1fm[:, jj, 1:TILE + 1], Alu.mult, Alu.add)
                            if not pf:
                                i.then_inc(s_mixr, 1)
                    v.tensor_copy(y1fm[:, :, 0:1],
                                  y1fm[:, :, TILE:TILE + 1]).then_inc(s_bnd3, 1)
                    v.drain()
                    if pf:
                        v.wait_ge(s_ycol, 1)
                        v.drain()
                        for j in range(NCH):
                            v.scalar_tensor_tensor(
                                xr_fm[:, j, 0:1], ycol0[:, j:j + 1],
                                mur1n_c[:, j:j + 1],
                                xr_fm[:, j, 0:1], Alu.mult, Alu.add
                                ).then_inc(s_mixr, 1)
                        v.drain()

                def prod_ln(v, ti, sl):
                    gs = ti * NSL + sl
                    v.wait_ge(s_vomm, gs + 1)
                    v.wait_ge(s_sigf, gs + 1)
                    v.wait_ge(s_l3r[gs % 2], 16 * (gs // 2 + 1))
                    if gs >= 2:
                        v.wait_ge(s_sto[gs % 2], 16 * (gs // 2))
                    v.tensor_tensor(y2p[sl % 2][:], psW[sl % 2][:],
                                    sigf[sl % 2][:], Alu.mult)
                    v.drain()
                    v.scalar_tensor_tensor(
                        y2p[sl % 2][:], y2p[sl % 2][:], 1.0,
                        y1r[gs % 2][:].bitcast(F32),
                        Alu.mult, Alu.add,
                        accum_out=sums[:, sl:sl + 1]).then_inc(s_pr, 1)
                    v.wait_ge(s_sq3, gs + 1)
                    v.drain()
                    v.tensor_scalar(m_col[:, sl:sl + 1], sums[:, sl:sl + 1],
                                    1.0 / C, None, Alu.mult)
                    v.tensor_tensor(v_col[:, sl:sl + 1],
                                    sumsq[:, 2 * sl:2 * sl + 1],
                                    sumsq[:, 2 * sl + 1:2 * sl + 2], Alu.add)
                    v.drain()
                    v.tensor_tensor(b_col[:, sl:sl + 1], m_col[:, sl:sl + 1],
                                    m_col[:, sl:sl + 1], Alu.mult)
                    v.drain()
                    v.scalar_tensor_tensor(v_col[:, sl:sl + 1],
                                           v_col[:, sl:sl + 1], 1.0 / C,
                                           b_col[:, sl:sl + 1],
                                           Alu.mult, Alu.subtract
                                           ).then_inc(s_var3, 1)
                    v.wait_ge(s_rsq3, gs + 1)
                    v.scalar_tensor_tensor(b_col[:, sl:sl + 1],
                                           m_col[:, sl:sl + 1], -1.0,
                                           is_col[:, sl:sl + 1],
                                           Alu.mult, Alu.mult)
                    v.drain()
                    v.tensor_scalar(y2p[sl % 2][:], y2p[sl % 2][:],
                                    is_col[:, sl:sl + 1],
                                    b_col[:, sl:sl + 1],
                                    Alu.mult, Alu.add).then_inc(s_ow, 1)

                xk_pass(v, 0)
                xr_pass(v, 0)
                for ti in range(NT):
                    nxt = ti + 1 < NT
                    prod_ln(v, ti, 0)
                    prod_ln(v, ti, 1)
                    if nxt:
                        xk_pass(v, ti + 1)
                    prod_ln(v, ti, 2)
                    prod_ln(v, ti, 3)
                    if nxt:
                        xr_pass(v, ti + 1)

            @blk.gpsimd
            def _(g):
                for ti in range(NT):
                    for sl in range(NSL):
                        gs = ti * NSL + sl
                        g.wait_ge(s_ow, gs + 1)
                        g.dma_start(out=out_d[gs * 128:(gs + 1) * 128, :],
                                    in_=y2p[sl % 2][:]).then_inc(s_sto[gs % 2], 16)
                g.wait_ge(s_sto[0], 16 * 2 * NT)
                g.wait_ge(s_sto[1], 16 * 2 * NT)
        ph2b.close()
    return nc


_PROG_CACHE = {}


def _get_prog(T_c, exchange):
    key = (T_c, exchange)
    if key not in _PROG_CACHE:
        _PROG_CACHE[key] = build(T_c, exchange)
    return _PROG_CACHE[key]


def _fm(vec):
    return np.ascontiguousarray(np.asarray(vec, np.float32).reshape(NCH, 128).T)


def kernel(L0_attn_x, L0_attn_time_decay, L0_attn_time_first, L0_attn_time_mix,
           L0_attn_Wk, L0_attn_Wv, L0_attn_Wr, L0_attn_Wo,
           L0_ffn_W_r, L0_ffn_W_k, L0_ffn_W_v, L0_ffn_mu_r, L0_ffn_mu_k):
    x = np.ascontiguousarray(np.asarray(L0_attn_x, np.float32))
    B, T, _ = x.shape
    wT = {"WkT": L0_attn_Wk, "WvT": L0_attn_Wv, "WrT": L0_attn_Wr,
          "WoT": L0_attn_Wo, "fWrT": L0_ffn_W_r, "fWkT": L0_ffn_W_k,
          "fWvT": L0_ffn_W_v}
    wT = {k: np.ascontiguousarray(np.asarray(v, np.float32).T)
          for k, v in wT.items()}
    base = {
        "tm_fm": _fm(np.asarray(L0_attn_time_mix, np.float32).reshape(C)),
        "mur_fm": _fm(L0_ffn_mu_r),
        "muk_fm": _fm(L0_ffn_mu_k),
        "td_fm": _fm(np.repeat(np.asarray(L0_attn_time_decay, np.float32)[:, 0], 64)),
        "tf_fm": _fm(np.repeat(np.asarray(L0_attn_time_first, np.float32)[:, 0], 64)),
    }
    base.update(wT)
    zrow = np.zeros((128, NCH), np.float32)

    if not EXCHANGE:
        nc = _get_prog(T, False)
        in_maps = [dict(base, x=x[core % B], xprev_fm=zrow) for core in range(8)]
        res = run_bass_kernel_spmd(nc, in_maps, core_ids=list(range(8)))
        return np.stack([res.results[b]["out"] for b in range(B)], axis=0)
    else:
        T_c = T // 2
        nc = _get_prog(T_c, True)
        one = np.ones((128, 1), np.float32)
        zero = np.zeros((128, 1), np.float32)
        in_maps = []
        for core in range(8):
            b, half = core // 2, core % 2
            t0 = half * T_c
            in_maps.append(dict(
                base,
                x=np.ascontiguousarray(x[b, t0:t0 + T_c]),
                xprev_fm=(zrow if half == 0 else _fm(x[b, t0 - 1])),
                flagc=(one if half == 0 else zero),
                flagu=(zero if half == 0 else one)))
        res = run_bass_kernel_spmd(nc, in_maps, core_ids=list(range(8)))
        return np.stack(
            [np.concatenate([res.results[2 * b]["out"],
                             res.results[2 * b + 1]["out"]], axis=0)
             for b in range(B)], axis=0)


# version 19
# speedup vs baseline: 1.1286x; 1.0160x over previous
import numpy as np
import concourse.bass as bass
import concourse.mybir as mybir
from concourse.bass_utils import run_bass_kernel_spmd

F32 = mybir.dt.float32
F32R = mybir.dt.float32r
Act = mybir.ActivationFunctionType
Alu = mybir.AluOpType

C = 1024
NCH = C // 128
TILE = 512
NSL = TILE // 128

EXCHANGE = True


def build(T_c, exchange):
    NT = T_c // TILE
    nc = bass.Bass("TRN2", target_bir_lowering=False,
                   **({"num_devices": 8} if exchange else {}))

    x_d = nc.dram_tensor("x", [T_c, C], F32R, kind="ExternalInput").ap()
    xprev_d = nc.dram_tensor("xprev_fm", [128, NCH], F32, kind="ExternalInput").ap()
    w_d = {n: nc.dram_tensor(n, [C, C], F32R, kind="ExternalInput").ap()
           for n in ["WkT", "WvT", "WrT", "WoT", "fWrT", "fWkT", "fWvT"]}
    tm_d = nc.dram_tensor("tm_fm", [128, NCH], F32, kind="ExternalInput").ap()
    mur_d = nc.dram_tensor("mur_fm", [128, NCH], F32, kind="ExternalInput").ap()
    muk_d = nc.dram_tensor("muk_fm", [128, NCH], F32, kind="ExternalInput").ap()
    td_d = nc.dram_tensor("td_fm", [128, NCH], F32, kind="ExternalInput").ap()
    tf_d = nc.dram_tensor("tf_fm", [128, NCH], F32, kind="ExternalInput").ap()
    out_d = nc.dram_tensor("out", [T_c, C], F32, kind="ExternalOutput").ap()

    xm_d = nc.dram_tensor("xm_s", [NT, NCH, 128, TILE], F32R, kind="Internal").ap()
    wkv_d = nc.dram_tensor("wkv_s", [NT, NCH, 128, TILE], F32, kind="Internal").ap()
    sumk_d = nc.dram_tensor("sumk_s", [NT, NCH, 128, TILE], F32, kind="Internal").ap()
    y1_d = nc.dram_tensor("y1_s", [NT, NSL, 128, C], F32R, kind="Internal").ap()

    if exchange:
        flagc_d = nc.dram_tensor("flagc", [128, 1], F32, kind="ExternalInput").ap()
        flagu_d = nc.dram_tensor("flagu", [128, 1], F32, kind="ExternalInput").ap()
        cc1_in = nc.dram_tensor("cc1_in", [128, 2 * NCH], F32, kind="Internal")
        cc1_out = nc.dram_tensor("cc1_out", [128, 2 * NCH], F32, kind="Internal")
        cc2_in = nc.dram_tensor("cc2_in", [1, C], F32, kind="Internal")
        cc2_out = nc.dram_tensor("cc2_out", [1, C], F32, kind="Internal")

    xr_view = x_d.rearrange("(a p) c -> p a c", p=128)

    def st1(ti, j):
        return 272 * ti + 16 + 32 * (j + 1)
    ST1_ALL = 272 * NT
    ST2_ALL = ST1_ALL + 16 * NT * NSL
    def ld2(ti, part):
        return 16 * (ti * (NSL + 1 + 2 * NCH) + part)

    from contextlib import ExitStack
    with ExitStack() as ctx:
        sbuf = lambda *a: ctx.enter_context(nc.sbuf_tensor(*a))

        cpool = sbuf("cpool", [128, 8], F32)
        CB_60, CB_M1, CB_1, CB_EPS16, CB_LNEPS, CB_M05, CB_M512 = range(7)
        ident = sbuf("ident", [128, 128], F32R)
        tm_c = sbuf("tm_c", [128, NCH], F32)
        mur_c = sbuf("mur_c", [128, NCH], F32)
        muk_c = sbuf("muk_c", [128, NCH], F32)
        mur1_c = sbuf("mur1_c", [128, NCH], F32)
        muk1_c = sbuf("muk1_c", [128, NCH], F32)
        td_c = sbuf("td_c", [128, NCH], F32)
        tf_c = sbuf("tf_c", [128, NCH], F32)
        dec_c = sbuf("dec_c", [128, NCH], F32)
        wdec_c = sbuf("wdec_c", [128, NCH], F32)
        etf_c = sbuf("etf_c", [128, NCH], F32)
        w512_c = sbuf("w512_c", [128, NCH], F32)
        dexpt = sbuf("dexpt", [128, NCH, TILE], F32R)
        carry_kv = sbuf("carry_kv", [128, NCH], F32)
        carry_k = sbuf("carry_k", [128, NCH], F32)
        i_kv = sbuf("i_kv", [128, NCH], F32)
        i_k = sbuf("i_k", [128, NCH], F32)
        sq_junk = sbuf("sq_junk", [128, 512], F32)
        sums = sbuf("sums", [128, NSL], F32)
        sumsq = sbuf("sumsq", [128, 2 * NSL], F32)
        m_col = sbuf("m_col", [128, NSL], F32)
        v_col = sbuf("v_col", [128, NSL], F32)
        is_col = sbuf("is_col", [128, NSL], F32)
        b_col = sbuf("b_col", [128, NSL], F32)
        wslotA = sbuf("wslotA", [128, NCH, C], F32R)
        wslotB = sbuf("wslotB", [128, NCH, C], F32R)
        wslotC = sbuf("wslotC", [128, NCH, C], F32R)
        if exchange:
            flagc = sbuf("flagc_s", [128, 1], F32)
            flagu = sbuf("flagu_s", [128, 1], F32)
            pay1 = sbuf("pay1", [128, 2 * NCH], F32)
            rcv1 = sbuf("rcv1", [128, 2 * NCH], F32)
            pay2 = sbuf("pay2", [128, C], F32)
            rcv2fm = sbuf("rcv2fm", [128, NCH], F32)
            ycol0 = sbuf("ycol0", [128, NCH], F32)

        psA = [ctx.enter_context(nc.psum_tensor(f"psA{i}", [128, 512], F32))
               for i in range(2)]
        psB = [ctx.enter_context(nc.psum_tensor(f"psB{i}", [128, 512], F32))
               for i in range(2)]
        psW = [ctx.enter_context(nc.psum_tensor(f"psW{i}", [128, C], F32))
               for i in range(2)]
        psK3 = [psA[0][:], psA[1][:], psW[0][:, 0:512]]
        psV3 = [psB[0][:], psB[1][:], psW[0][:, 512:C]]

        S = lambda name: ctx.enter_context(nc.semaphore(name))
        s_init = S("s_init")
        s_wl = S("s_wl")
        s_wlo = S("s_wlo")
        s_wlr = S("s_wlr")
        s_wfk = S("s_wfk")
        s_wfr = S("s_wfr")
        s_wfv = S("s_wfv")
        s_xld = [S("s_xld0"), S("s_xld1")]
        s_T = S("s_T")
        s_ev = S("s_ev")
        s_bnd = S("s_bnd")
        s_bndd = S("s_bndd")
        s_xm = S("s_xm")
        s_kmm = S("s_kmm")
        s_k = S("s_k")
        s_vmm = S("s_vmm")
        s_kv = S("s_kv")
        s_scan = S("s_scan")
        s_tts = S("s_tts")
        s_gw = S("s_gw")
        s_stx = S("s_stx")
        s_stw = [S("s_stw0"), S("s_stw1")]
        s_sty = [S("s_sty0"), S("s_sty1"), S("s_sty2"), S("s_sty3")]
        s_sto = [S("s_sto0"), S("s_sto1")]
        s_l2x = [S("s_l2x0"), S("s_l2x1"), S("s_l2x2"), S("s_l2x3")]
        s_l2m = [S("s_l2m0"), S("s_l2m1")]
        s_l2w = [S("s_l2w0"), S("s_l2w1"), S("s_l2w2")]
        s_rmm = S("s_rmm")
        s_e = S("s_e")
        s_den = S("s_den")
        s_inv = S("s_inv")
        s_rw = S("s_rw")
        s_omm = S("s_omm")
        s_y1p = S("s_y1p")
        s_sq = S("s_sq")
        s_var = S("s_var")
        s_rsq = S("s_rsq")
        s_y1w = S("s_y1w")
        s_l3t = [S("s_l3t0"), S("s_l3t1")]
        s_l3r = [S("s_l3r0"), S("s_l3r1")]
        s_T3 = S("s_T3")
        s_ev3 = S("s_ev3")
        s_bnd3 = S("s_bnd3")
        s_mixk = S("s_mixk")
        s_mixr = S("s_mixr")
        s_kfmm = S("s_kfmm")
        s_kact = S("s_kact")
        s_rfmm = S("s_rfmm")
        s_fe = S("s_fe")
        s_sigf = S("s_sigf")
        s_vomm = S("s_vomm")
        s_pr = S("s_pr")
        s_sq3 = S("s_sq3")
        s_var3 = S("s_var3")
        s_rsq3 = S("s_rsq3")
        s_ow = S("s_ow")
        s_pay1 = S("s_pay1")
        s_ccd = S("s_ccd")
        s_ccc = S("s_ccc")
        s_ikv = S("s_ikv")
        s_dx1 = S("s_dx1")
        s_dx2 = S("s_dx2")
        s_dx3 = S("s_dx3")
        s_pay2 = S("s_pay2")
        s_rowmm = S("s_rowmm")
        s_ycol = S("s_ycol")
        s_misc = S("s_misc")
        s_misc2 = S("s_misc2")

        blk = ctx.enter_context(nc.Block())

        ph0 = ExitStack()
        iot_r = ph0.enter_context(nc.sbuf_tensor("iot_r", [128, 128], F32))
        iot_c = ph0.enter_context(nc.sbuf_tensor("iot_c", [128, 128], F32))
        if True:

            @blk.sync
            def _(sync):
                sync.dma_start(out=tm_c[:], in_=tm_d).then_inc(s_misc, 16)
                sync.dma_start(out=mur_c[:], in_=mur_d).then_inc(s_misc, 16)
                sync.dma_start(out=muk_c[:], in_=muk_d).then_inc(s_misc, 16)
                sync.dma_start(out=td_c[:], in_=td_d).then_inc(s_misc, 16)
                sync.dma_start(out=tf_c[:], in_=tf_d).then_inc(s_misc, 16)
                if exchange:
                    sync.dma_start(out=flagc[:], in_=flagc_d).then_inc(s_misc2, 16)
                    sync.dma_start(out=flagu[:], in_=flagu_d).then_inc(s_misc2, 16)

            @blk.gpsimd
            def _(g):
                g.iota(iot_r[:], [[0, 128]], channel_multiplier=1,
                       allow_small_or_imprecise_dtypes=True)
                g.iota(iot_c[:], [[1, 128]], channel_multiplier=0,
                       allow_small_or_imprecise_dtypes=True).then_inc(s_init, 1)

            @blk.vector
            def _(v):
                for col, val in [(CB_60, 60.0), (CB_M1, -1.0), (CB_1, 1.0),
                                 (CB_EPS16, 1e-16), (CB_LNEPS, 1e-5),
                                 (CB_M05, -0.5), (CB_M512, -512.0)]:
                    v.memset(cpool[:, col:col + 1], val)
                v.memset(carry_kv[:], 0.0)
                v.memset(carry_k[:], 0.0)
                v.memset(i_kv[:], 0.0)
                v.memset(i_k[:], 0.0)
                v.wait_ge(s_misc, 16 * 5)
                v.tensor_scalar(mur1_c[:], mur_c[:], -1.0, None, Alu.add)
                v.tensor_scalar(muk1_c[:], muk_c[:], -1.0, None, Alu.add)
                v.wait_ge(s_init, 1)
                v.tensor_tensor(ident[:], iot_r[:], iot_c[:],
                                Alu.is_equal).then_inc(s_init, 1)

            @blk.scalar
            def _(sc):
                sc.wait_ge(s_misc, 16 * 5)
                sc.wait_ge(s_init, 2)
                sc.activation(dec_c[:], td_c[:], Act.Exp)
                sc.drain()
                sc.activation(wdec_c[:], dec_c[:], Act.Exp,
                              scale=cpool[:, CB_M1:CB_M1 + 1])
                sc.activation(w512_c[:], dec_c[:], Act.Exp,
                              scale=cpool[:, CB_M512:CB_M512 + 1])
                sc.activation(etf_c[:], tf_c[:], Act.Exp).then_inc(s_init, 1)


        @blk.gpsimd
        def _(g):
            g.wait_ge(s_init, 3)
            g.iota(sq_junk[:, 0:TILE], [[1, TILE]], channel_multiplier=0,
                   allow_small_or_imprecise_dtypes=True).then_inc(s_dx1, 1)

        @blk.vector
        def _(v):
            v.wait_ge(s_dx1, 1)
            v.wait_ge(s_init, 3)
            for j in range(NCH):
                i = v.tensor_scalar(dexpt[:, j, :], sq_junk[:, 0:TILE],
                                    dec_c[:, j:j + 1], None, Alu.mult)
            i.then_inc(s_dx2, 1)

        @blk.scalar
        def _(sc):
            sc.wait_ge(s_dx2, 1)
            for j in range(NCH):
                i = sc.activation(dexpt[:, j, :],
                                  dexpt[:, j, :].bitcast(F32), Act.Exp,
                                  scale=cpool[:, CB_M1:CB_M1 + 1])
            i.then_inc(s_dx3, 1)

        ph0.close()
        ph1 = ExitStack()
        x_tm0 = ph1.enter_context(nc.sbuf_tensor("x_tm0", [128, C], F32R))
        x_tm1 = ph1.enter_context(nc.sbuf_tensor("x_tm1", [128, C], F32R))
        x_fm = ph1.enter_context(nc.sbuf_tensor("x_fm", [128, NCH, TILE + 1], F32))
        d_full = ph1.enter_context(nc.sbuf_tensor("d_full", [128, NCH, TILE], F32))
        xm_fm = ph1.enter_context(nc.sbuf_tensor("xm_fm", [128, NCH, TILE], F32R))
        k_sb0 = ph1.enter_context(nc.sbuf_tensor("k_sb0", [128, TILE], F32))
        k_sb1 = ph1.enter_context(nc.sbuf_tensor("k_sb1", [128, TILE], F32))
        k_sb2 = ph1.enter_context(nc.sbuf_tensor("k_sb2", [128, TILE], F32))
        kv_sb0 = ph1.enter_context(nc.sbuf_tensor("kv_sb0", [128, TILE], F32))
        kv_sb1 = ph1.enter_context(nc.sbuf_tensor("kv_sb1", [128, TILE], F32))
        kv_sb2 = ph1.enter_context(nc.sbuf_tensor("kv_sb2", [128, TILE], F32))
        s_buf0 = ph1.enter_context(nc.sbuf_tensor("s_buf0", [128, TILE + 1], F32))
        s_buf1 = ph1.enter_context(nc.sbuf_tensor("s_buf1", [128, TILE + 1], F32))
        s_buf2 = ph1.enter_context(nc.sbuf_tensor("s_buf2", [128, TILE + 1], F32))
        sk_buf0 = ph1.enter_context(nc.sbuf_tensor("sk_buf0", [128, TILE + 1], F32))
        sk_buf1 = ph1.enter_context(nc.sbuf_tensor("sk_buf1", [128, TILE + 1], F32))
        sk_buf2 = ph1.enter_context(nc.sbuf_tensor("sk_buf2", [128, TILE + 1], F32))
        wkv_sb0 = ph1.enter_context(nc.sbuf_tensor("wkv_sb0", [128, TILE], F32))
        wkv_sb1 = ph1.enter_context(nc.sbuf_tensor("wkv_sb1", [128, TILE], F32))
        sumk_sb0 = ph1.enter_context(nc.sbuf_tensor("sumk_sb0", [128, TILE], F32))
        sumk_sb1 = ph1.enter_context(nc.sbuf_tensor("sumk_sb1", [128, TILE], F32))
        if True:
            x_tm = [x_tm0, x_tm1]
            k_sb, kv_sb = [k_sb0, k_sb1, k_sb2], [kv_sb0, kv_sb1, kv_sb2]
            s_bufs, sk_bufs = [s_buf0, s_buf1, s_buf2], [sk_buf0, sk_buf1, sk_buf2]
            wkv_sb, sumk_sb = [wkv_sb0, wkv_sb1], [sumk_sb0, sumk_sb1]

            @blk.gpsimd
            def _(g):
                g.wait_ge(s_init, 3)
                g.nop().then_inc(s_init, 1)
                g.dma_start(out=x_fm[:, :, 0:1],
                            in_=xprev_d.rearrange("p (a b) -> p a b", b=1),
                            allow_slow_non_contiguous=True
                            ).then_inc(s_bndd, 16)


            @blk.scalar
            def _(sc):
                sc.dma_start(out=wslotA[:], in_=w_d["WkT"].rearrange(
                    "(a p) c -> p a c", p=128)).then_inc(s_wl, 16)
                sc.dma_start(out=wslotB[:], in_=w_d["WvT"].rearrange(
                    "(a p) c -> p a c", p=128)).then_inc(s_wl, 16)
                sc.dma_start(out=wslotC[:], in_=w_d["WoT"].rearrange(
                    "(a p) c -> p a c", p=128)).then_inc(s_wlo, 16)

            @blk.sync
            def _(sync):
                sync.wait_ge(s_init, 3)
                for ti in range(NT):
                    for sl in range(NSL):
                        gs = ti * NSL + sl
                        if gs >= 2:
                            sync.wait_ge(s_T, gs - 1)
                        sync.dma_start(
                            out=x_tm[gs % 2][:],
                            in_=xr_view[:, gs, :],
                        ).then_inc(s_xld[gs % 2], 16)

            @blk.tensor
            def _(t):
                t.wait_ge(s_init, 2)
                for ti in range(NT):
                    for sl in range(NSL):
                        gs = ti * NSL + sl
                        t.wait_ge(s_xld[gs % 2], 16 * (gs // 2 + 1))
                        if gs >= 1:
                            t.wait_ge(s_ev, gs)
                        if ti >= 1 and sl == 0:
                            t.wait_ge(s_k, ti * NCH)
                        for j in range(NCH):
                            dst = psA[0] if j < 4 else psA[1]
                            i = t.transpose(
                                dst[:].bitcast(F32R)[:, (j % 4) * 128:(j % 4 + 1) * 128],
                                x_tm[gs % 2][:, j * 128:(j + 1) * 128],
                                ident[:])
                        i.then_inc(s_T, 1)
                    if ti == 0:
                        t.wait_ge(s_wl, 32)
                    t.wait_ge(s_xm, ti + 1)
                    for j in range(NCH):
                        g = ti * NCH + j
                        if g >= 3:
                            t.wait_ge(s_k, g - 2)
                        pk = psK3[g % 3]
                        for c in range(NCH):
                            i = t.matmul(pk,
                                         wslotA[:, c, j * 128:(j + 1) * 128],
                                         xm_fm[:, c, :],
                                         start=(c == 0), stop=(c == NCH - 1))
                        i.then_inc(s_kmm, 1)
                        if g >= 3:
                            t.wait_ge(s_kv, g - 2)
                        pv = psV3[g % 3]
                        for c in range(NCH):
                            i = t.matmul(pv,
                                         wslotB[:, c, j * 128:(j + 1) * 128],
                                         xm_fm[:, c, :],
                                         start=(c == 0), stop=(c == NCH - 1))
                        i.then_inc(s_vmm, 1)

            @blk.scalar
            def _(sc):
                for ti in range(NT):
                    for sl in range(NSL):
                        gs = ti * NSL + sl
                        sc.wait_ge(s_T, gs + 1)
                        if ti >= 1 and sl == 0:
                            sc.wait_ge(s_xm, ti)
                        if sl == NSL - 1 and ti >= 1:
                            sc.wait_ge(s_bnd, ti)
                        sc.copy(x_fm[:, 0:4, 1 + sl * 128:1 + (sl + 1) * 128],
                                psA[0][:].bitcast(F32R).rearrange(
                                    "p (a b) -> p a b", a=4))
                        sc.copy(x_fm[:, 4:8, 1 + sl * 128:1 + (sl + 1) * 128],
                                psA[1][:].bitcast(F32R).rearrange(
                                    "p (a b) -> p a b", a=4)).then_inc(s_ev, 1)
                    for j in range(NCH):
                        g = ti * NCH + j
                        sc.wait_ge(s_kmm, g + 1)
                        if g >= 3:
                            sc.wait_ge(s_scan, g - 2)
                        kb = k_sb[g % 3]
                        sc.activation(kb[:], psK3[g % 3], Act.Relu,
                                      bias=cpool[:, CB_60:CB_60 + 1],
                                      scale=cpool[:, CB_M1:CB_M1 + 1])
                        sc.drain()
                        sc.activation(kb[:], kb[:], Act.Exp,
                                      bias=cpool[:, CB_60:CB_60 + 1],
                                      scale=cpool[:, CB_M1:CB_M1 + 1]
                                      ).then_inc(s_k, 1)

            @blk.vector
            def _(v):
                v.wait_ge(s_init, 4)
                for ti in range(NT):
                    v.wait_ge(s_ev, ti * NSL + NSL)
                    if ti == 0:
                        v.wait_ge(s_bndd, 16)
                    else:
                        v.wait_ge(s_stx, 16 * ti)
                    v.drain()
                    v.tensor_tensor(d_full[:], x_fm[:, :, 1:TILE + 1],
                                    x_fm[:, :, 0:TILE], Alu.subtract)
                    v.drain()
                    for j in range(NCH):
                        i = v.scalar_tensor_tensor(
                            xm_fm[:, j, :], d_full[:, j, :], tm_c[:, j:j + 1],
                            x_fm[:, j, 0:TILE], Alu.mult, Alu.add)
                    i.then_inc(s_xm, 1)
                    v.drain()
                    v.tensor_copy(x_fm[:, :, 0:1],
                                  x_fm[:, :, TILE:TILE + 1]).then_inc(s_bnd, 1)
                    g0 = ti * NCH
                    v.wait_ge(s_k, g0 + 1)
                    v.wait_ge(s_vmm, g0 + 1)

                    v.tensor_tensor(kv_sb[g0 % 3][:], k_sb[g0 % 3][:],
                                    psV3[g0 % 3], Alu.mult).then_inc(s_kv, 1)
                    v.tensor_copy(s_bufs[g0 % 3][:, 0:1], carry_kv[:, 0:1])
                    v.tensor_copy(sk_bufs[g0 % 3][:, 0:1], carry_k[:, 0:1])
                    for j in range(NCH + 1):
                        g = ti * NCH + j
                        v.drain()
                        if j < NCH:
                            wj = wdec_c[:, j:j + 1].broadcast_to([128, TILE])
                            v.tensor_tensor_scan(s_bufs[g % 3][:, 1:TILE + 1],
                                                 wj, kv_sb[g % 3][:],
                                                 s_bufs[g % 3][:, 0:1],
                                                 Alu.mult, Alu.add)
                            i = v.tensor_tensor_scan(sk_bufs[g % 3][:, 1:TILE + 1],
                                                 wj, k_sb[g % 3][:],
                                                 sk_bufs[g % 3][:, 0:1],
                                                 Alu.mult, Alu.add)
                        if j >= 1:
                            jj, gg = j - 1, g - 1
                            if gg >= 2:
                                v.wait_ge(s_stw[gg % 2], 32 * (gg // 2))
                            v.scalar_tensor_tensor(
                                wkv_sb[gg % 2][:], kv_sb[gg % 3][:],
                                etf_c[:, jj:jj + 1],
                                s_bufs[gg % 3][:, 0:TILE], Alu.mult, Alu.add)
                            v.scalar_tensor_tensor(
                                sumk_sb[gg % 2][:], k_sb[gg % 3][:],
                                etf_c[:, jj:jj + 1],
                                sk_bufs[gg % 3][:, 0:TILE], Alu.mult, Alu.add)
                            v.tensor_copy(carry_kv[:, jj:jj + 1],
                                          s_bufs[gg % 3][:, TILE:TILE + 1])
                            v.tensor_copy(carry_k[:, jj:jj + 1],
                                          sk_bufs[gg % 3][:, TILE:TILE + 1]
                                          ).then_inc(s_scan, 1)
                        if j + 1 < NCH:
                            gn = g + 1
                            v.wait_ge(s_k, gn + 1)
                            v.wait_ge(s_vmm, gn + 1)

                            v.tensor_tensor(kv_sb[gn % 3][:], k_sb[gn % 3][:],
                                            psV3[gn % 3], Alu.mult
                                            ).then_inc(s_kv, 1)
                            v.tensor_copy(s_bufs[gn % 3][:, 0:1],
                                          carry_kv[:, j + 1:j + 2])
                            v.tensor_copy(sk_bufs[gn % 3][:, 0:1],
                                          carry_k[:, j + 1:j + 2])

            @blk.gpsimd
            def _(g):
                for ti in range(NT):
                    g.wait_ge(s_xm, ti + 1)
                    g.dma_start(out=xm_d[ti].rearrange("a p t -> p a t"),
                                in_=xm_fm[:]).then_inc(s_stx, 16)
                    for j in range(NCH):
                        gg = ti * NCH + j
                        g.wait_ge(s_scan, gg + 1)
                        g.dma_start(out=wkv_d[ti, j], in_=wkv_sb[gg % 2][:]
                                    ).then_inc(s_stw[gg % 2], 16)
                        g.dma_start(out=sumk_d[ti, j], in_=sumk_sb[gg % 2][:]
                                    ).then_inc(s_stw[gg % 2], 16)

            if exchange:
                @blk.vector
                def _(v):
                    v.wait_ge(s_misc2, 32)
                    v.drain()
                    v.tensor_scalar(pay1[:, 0:NCH], carry_kv[:],
                                    flagc[:, 0:1], None, Alu.mult)
                    v.tensor_scalar(pay1[:, NCH:2 * NCH], carry_k[:],
                                    flagc[:, 0:1], None, Alu.mult
                                    ).then_inc(s_pay1, 1)

                @blk.gpsimd
                def _(g):
                    g.wait_ge(s_pay1, 1)
                    g.dma_start(out=cc1_in.ap(), in_=pay1[:]).then_inc(s_ccd, 16)
                    g.wait_ge(s_ccd, 16)
                    g.collective_compute(
                        "AllReduce", Alu.add,
                        replica_groups=[[0, 1], [2, 3], [4, 5], [6, 7]],
                        ins=[cc1_in.ap()], outs=[cc1_out.ap()],
                    ).then_inc(s_ccc, 1)
                    g.wait_ge(s_ccc, 1)
                    g.dma_start(out=rcv1[:], in_=cc1_out.ap()).then_inc(s_ccd, 16)

                @blk.vector
                def _(v):
                    v.wait_ge(s_ccd, 32)
                    v.tensor_scalar(i_kv[:], rcv1[:, 0:NCH], flagu[:, 0:1],
                                    None, Alu.mult)
                    v.tensor_scalar(i_k[:], rcv1[:, NCH:2 * NCH], flagu[:, 0:1],
                                    None, Alu.mult).then_inc(s_ikv, 1)

        ph1.close()
        ph2a = ExitStack()
        xm2 = [ph2a.enter_context(nc.sbuf_tensor(f"xm2_{i}", [128, NCH, TILE], F32R)) for i in range(2)]
        wkv_in = [ph2a.enter_context(nc.sbuf_tensor(f"wkv_in{i}", [128, TILE], F32)) for i in range(3)]
        sumk_in = [ph2a.enter_context(nc.sbuf_tensor(f"sumk_in{i}", [128, TILE], F32)) for i in range(3)]
        e_sb = [ph2a.enter_context(nc.sbuf_tensor(f"e_sb{i}", [128, TILE], F32)) for i in range(3)]
        rwkv = ph2a.enter_context(nc.sbuf_tensor("rwkv", [128, NCH, TILE], F32R))
        y1p = [ph2a.enter_context(nc.sbuf_tensor(f"y1p{i}", [128, C], F32R)) for i in range(4)]
        if True:

            @blk.scalar
            def _(sc):
                sc.wait_ge(s_kmm, NT * NCH)
                sc.dma_start(out=wslotA[:], in_=w_d["WrT"].rearrange(
                    "(a p) c -> p a c", p=128)).then_inc(s_wlr, 16)
                sc.wait_ge(s_vmm, NT * NCH)
                sc.dma_start(out=wslotB[:], in_=w_d["fWkT"].rearrange(
                    "(a p) c -> p a c", p=128)).then_inc(s_wfk, 16)

            @blk.sync
            def _(sync):
                sync.wait_ge(s_stx, 16 * NT)
                sync.wait_ge(s_stw[0], 32 * 4 * NT)
                sync.wait_ge(s_stw[1], 32 * 4 * NT)
                def xm2_load(sync, ti):
                    if ti >= 2:
                        sync.wait_ge(s_rmm, (ti - 1) * NCH)
                    sync.dma_start(out=xm2[ti % 2][:],
                                   in_=xm_d[ti].rearrange("a p t -> p a t")
                                   ).then_inc(s_l2m[ti % 2], 16)

                xm2_load(sync, 0)
                xm2_load(sync, 1)
                for ti in range(NT):
                    for j in range(NCH):
                        g = ti * NCH + j
                        if g >= 3:
                            sync.wait_ge(s_rw, g - 2)
                        sync.dma_start(out=wkv_in[g % 3][:], in_=wkv_d[ti, j]
                                       ).then_inc(s_l2w[g % 3], 16)
                        sync.dma_start(out=sumk_in[g % 3][:], in_=sumk_d[ti, j]
                                       ).then_inc(s_l2w[g % 3], 16)
                    for sl in range(NSL):
                        gs = ti * NSL + sl
                        if gs >= 4:
                            sync.wait_ge(s_sty[gs % 4], 16 * (gs // 4))
                        sync.dma_start(out=y1p[gs % 4][:],
                                       in_=xr_view[:, gs, :]
                                       ).then_inc(s_l2x[gs % 4], 16)
                    if ti + 2 < NT:
                        xm2_load(sync, ti + 2)

            @blk.tensor
            def _(t):
                t.wait_ge(s_wlr, 16)
                t.wait_ge(s_wlo, 16)

                def r_proj(t, ti, j0, j1):
                    if j0 == 0:
                        t.wait_ge(s_l2m[ti % 2], 16 * (ti // 2 + 1))
                    for j in range(j0, j1):
                        g = ti * NCH + j
                        if g >= 2:
                            t.wait_ge(s_e, g - 1)
                        for c in range(NCH):
                            i = t.matmul(psA[j % 2][:],
                                         wslotA[:, c, j * 128:(j + 1) * 128],
                                         xm2[ti % 2][:, c, :],
                                         start=(c == 0), stop=(c == NCH - 1))
                        i.then_inc(s_rmm, 1)

                def wo_proj(t, ti):
                    t.wait_ge(s_rw, ti * NCH + NCH)
                    for sl in range(NSL):
                        gs = ti * NSL + sl
                        if gs >= 2:
                            t.wait_ge(s_y1p, gs - 1)
                        for j in range(NCH):
                            for h in range(2):
                                i = t.matmul(
                                    psW[sl % 2][:, h * 512:(h + 1) * 512],
                                    rwkv[:, j, sl * 128:(sl + 1) * 128],
                                    wslotC[:, j, h * 512:(h + 1) * 512],
                                    start=(j == 0), stop=(j == NCH - 1))
                        i.then_inc(s_omm, 1)

                for ti in range(NT):
                    r_proj(t, ti, 0, NCH)
                    wo_proj(t, ti)

            @blk.scalar
            def _(sc):
                for ti in range(NT):
                    for j in range(NCH):
                        g = ti * NCH + j
                        sc.wait_ge(s_rmm, g + 1)
                        if g >= 3:
                            sc.wait_ge(s_rw, g - 2)
                        sc.activation(e_sb[g % 3][:], psA[j % 2][:], Act.Exp,
                                      scale=cpool[:, CB_M1:CB_M1 + 1]
                                      ).then_inc(s_e, 1)
                        sc.wait_ge(s_den, g + 1)
                        sc.drain()
                        sc.activation(e_sb[g % 3][:], e_sb[g % 3][:],
                                      Act.Ln,
                                      bias=cpool[:, CB_EPS16:CB_EPS16 + 1])
                        sc.drain()
                        sc.activation(e_sb[g % 3][:], e_sb[g % 3][:],
                                      Act.Exp, scale=cpool[:, CB_M1:CB_M1 + 1]
                                      ).then_inc(s_inv, 1)
                    for sl in range(NSL):
                        gs = ti * NSL + sl
                        sc.wait_ge(s_y1p, gs + 1)
                        sc.drain()
                        sc.activation(sq_junk[:],
                                      y1p[gs % 4][:, 0:512].bitcast(F32),
                                      Act.Square,
                                      accum_out=sumsq[:, 2 * sl:2 * sl + 1])
                        sc.drain()
                        sc.activation(sq_junk[:],
                                      y1p[gs % 4][:, 512:C].bitcast(F32),
                                      Act.Square,
                                      accum_out=sumsq[:, 2 * sl + 1:2 * sl + 2]
                                      ).then_inc(s_sq, 1)
                    sc.wait_ge(s_var, ti + 1)
                    sc.activation(is_col[:], v_col[:], Act.Ln,
                                  bias=cpool[:, CB_LNEPS:CB_LNEPS + 1])
                    sc.drain()
                    sc.activation(is_col[:], is_col[:], Act.Exp,
                                  scale=cpool[:, CB_M05:CB_M05 + 1]
                                  ).then_inc(s_rsq, 1)

            @blk.vector
            def _(v):
                if exchange:
                    v.wait_ge(s_ikv, 1)
                v.wait_ge(s_dx3, 1)
                for ti in range(NT):
                    if ti >= 1:
                        v.wait_ge(s_omm, ti * NSL)
                    for j in range(NCH):
                        g = ti * NCH + j
                        v.wait_ge(s_l2w[g % 3], 32 * (g // 3 + 1))
                        v.wait_ge(s_e, g + 1)
                        if g >= 3:
                            v.wait_ge(s_inv, g - 2)
                        if exchange:
                            v.scalar_tensor_tensor(
                                wkv_in[g % 3][:], dexpt[:, j, :].bitcast(F32),
                                i_kv[:, j:j + 1], wkv_in[g % 3][:],
                                Alu.mult, Alu.add)
                            v.scalar_tensor_tensor(
                                sumk_in[g % 3][:], dexpt[:, j, :].bitcast(F32),
                                i_k[:, j:j + 1], sumk_in[g % 3][:],
                                Alu.mult, Alu.add)
                            v.drain()
                        v.scalar_tensor_tensor(
                            e_sb[g % 3][:], e_sb[g % 3][:], 1.0,
                            sumk_in[g % 3][:], Alu.add, Alu.mult
                            ).then_inc(s_den, 1)
                        v.wait_ge(s_inv, g + 1)
                        v.tensor_tensor(rwkv[:, j, :], wkv_in[g % 3][:],
                                        e_sb[g % 3][:], Alu.mult
                                        ).then_inc(s_rw, 1)

                    if exchange:
                        v.tensor_tensor(i_kv[:], i_kv[:], w512_c[:], Alu.mult)
                        v.tensor_tensor(i_k[:], i_k[:], w512_c[:], Alu.mult)
                        v.drain()
                    for sl in range(NSL):
                        gs = ti * NSL + sl
                        v.wait_ge(s_omm, gs + 1)
                        v.wait_ge(s_l2x[gs % 4], 16 * (gs // 4 + 1))
                        v.scalar_tensor_tensor(
                            y1p[gs % 4][:], psW[sl % 2][:], 1.0,
                            y1p[gs % 4][:].bitcast(F32),
                            Alu.mult, Alu.add,
                            accum_out=sums[:, sl:sl + 1]).then_inc(s_y1p, 1)
                    v.wait_ge(s_sq, ti * NSL + NSL)
                    v.drain()
                    v.tensor_scalar(m_col[:], sums[:], 1.0 / C, None, Alu.mult)
                    v.tensor_tensor(v_col[:], sumsq[:].rearrange(
                        "p (a b) -> p a b", b=2)[:, :, 0],
                        sumsq[:].rearrange("p (a b) -> p a b", b=2)[:, :, 1],
                        Alu.add)
                    v.drain()
                    v.tensor_tensor(b_col[:], m_col[:], m_col[:], Alu.mult)
                    v.drain()
                    v.scalar_tensor_tensor(v_col[:], v_col[:], 1.0 / C,
                                           b_col[:], Alu.mult, Alu.subtract
                                           ).then_inc(s_var, 1)
                    v.wait_ge(s_rsq, ti + 1)
                    v.scalar_tensor_tensor(b_col[:], m_col[:], -1.0,
                                           is_col[:], Alu.mult, Alu.mult)
                    v.drain()
                    for sl in range(NSL):
                        gs = ti * NSL + sl
                        v.tensor_scalar(y1p[gs % 4][:],
                                        y1p[gs % 4][:].bitcast(F32),
                                        is_col[:, sl:sl + 1],
                                        b_col[:, sl:sl + 1],
                                        Alu.mult, Alu.add).then_inc(s_y1w, 1)


            @blk.gpsimd
            def _(g):
                for ti in range(NT):
                    for sl in range(NSL):
                        gs = ti * NSL + sl
                        g.wait_ge(s_y1w, gs + 1)
                        g.dma_start(out=y1_d[ti, sl], in_=y1p[gs % 4][:]
                                    ).then_inc(s_sty[gs % 4], 16)

            if exchange:
                LASTS = (NT * NSL - 1) % 4

                @blk.vector
                def _(v):
                    v.wait_ge(s_y1w, NT * NSL)
                    v.drain()
                    v.tensor_scalar(pay2[:], y1p[LASTS][:].bitcast(F32),
                                    flagc[:, 0:1], None, Alu.mult
                                    ).then_inc(s_pay2, 1)

                @blk.gpsimd
                def _(g):
                    g.wait_ge(s_pay2, 1)
                    g.dma_start(out=cc2_in.ap(), in_=pay2[127:128, :]
                                ).then_inc(s_ccd, 16)
                    g.wait_ge(s_ccd, 48)
                    g.collective_compute(
                        "AllReduce", Alu.add,
                        replica_groups=[[0, 1], [2, 3], [4, 5], [6, 7]],
                        ins=[cc2_in.ap()], outs=[cc2_out.ap()],
                    ).then_inc(s_ccc, 1)
                    g.wait_ge(s_ccc, 2)

                @blk.gpsimd
                def _(g):
                    g.dma_start(out=rcv2fm[:],
                                in_=cc2_out.ap().rearrange(
                                    "o (a p) -> (o p) a", p=128),
                                allow_slow_non_contiguous=True
                                ).then_inc(s_ccd, 16)

                @blk.vector
                def _(v):
                    v.wait_ge(s_ccd, 64)
                    v.tensor_scalar(ycol0[:], rcv2fm[:],
                                    flagu[:, 0:1], None, Alu.mult
                                    ).then_inc(s_ycol, 1)

        ph2a.close()
        ph2b = ExitStack()
        d_buf = [ph2b.enter_context(nc.sbuf_tensor(f"d_buf{i}", [128, TILE], F32))
                 for i in range(2)]
        y1t = [ph2b.enter_context(nc.sbuf_tensor(f"y1t{i}", [128, C], F32R))
               for i in range(2)]
        y1r = [ph2b.enter_context(nc.sbuf_tensor(f"y1r{i}", [128, C], F32R))
               for i in range(2)]
        y1fm = ph2b.enter_context(nc.sbuf_tensor("y1fm", [128, NCH, TILE + 1], F32))
        ybcol = ph2b.enter_context(nc.sbuf_tensor("ybcol", [128, NCH], F32))
        xr_fm = ph2b.enter_context(nc.sbuf_tensor("xr_fm", [128, NCH, TILE], F32R))
        xk_fm = dexpt
        kact = ph2b.enter_context(nc.sbuf_tensor("kact", [128, NCH, TILE], F32R))
        sigf = [ph2b.enter_context(nc.sbuf_tensor(f"sigf{i}", [128, C], F32))
                for i in range(2)]
        y2p = [ph2b.enter_context(nc.sbuf_tensor(f"y2p{i}", [128, C], F32))
               for i in range(2)]
        if True:
            @blk.sync
            def _(sync):
                for _q in range(4):
                    sync.wait_ge(s_sty[_q], 16 * NT)
                sync.wait_ge(s_rmm, NT * NCH)
                sync.wait_ge(s_y1w, NT * NSL)

                def y1t_loads(sync, ti):
                    for sl in range(NSL):
                        gs = ti * NSL + sl
                        if gs >= 2:
                            sync.wait_ge(s_T3, gs - 1)
                        sync.dma_start(out=y1t[gs % 2][:], in_=y1_d[ti, sl]
                                       ).then_inc(s_l3t[gs % 2], 16)

                y1t_loads(sync, 0)
                sync.dma_start(out=wslotA[:], in_=w_d["fWrT"].rearrange(
                    "(a p) c -> p a c", p=128)).then_inc(s_wfr, 16)
                sync.dma_start(out=wslotC[:], in_=w_d["fWvT"].rearrange(
                    "(a p) c -> p a c", p=128)).then_inc(s_wfv, 16)
                for ti in range(NT):
                    if ti + 1 < NT:
                        y1t_loads(sync, ti + 1)
                    for sl in range(NSL):
                        gs = ti * NSL + sl
                        if gs >= 2:
                            sync.wait_ge(s_pr, gs - 1)
                        sync.dma_start(out=y1r[gs % 2][:], in_=y1_d[ti, sl]
                                       ).then_inc(s_l3r[gs % 2], 16)

            @blk.vector
            def _(v):
                i = v.memset(ybcol[:], 0.0)
                if exchange:
                    v.wait_ge(s_ycol, 1)
                    v.drain()
                    i = v.tensor_copy(ybcol[:], ycol0[:])
                i.then_inc(s_bnd3, 1)
                v.drain()

            @blk.tensor
            def _(t):
                t.wait_ge(s_wfk, 16)

                def transp(t, k, sl):
                    gs = k * NSL + sl
                    t.wait_ge(s_l3t[gs % 2], 16 * (gs // 2 + 1))
                    if gs >= 1:
                        t.wait_ge(s_ev3, gs)
                    for j in range(NCH):
                        dst = psA[0] if j < 4 else psA[1]
                        i = t.transpose(
                            dst[:].bitcast(F32R)[:, (j % 4) * 128:(j % 4 + 1) * 128],
                            y1t[gs % 2][:, j * 128:(j + 1) * 128],
                            ident[:])
                    i.then_inc(s_T3, 1)

                def rf_mm(t, ti, sl):
                    gs = ti * NSL + sl
                    if gs == 0:
                        t.wait_ge(s_wfr, 16)
                    if gs >= 2:
                        t.wait_ge(s_pr, gs - 1)
                    for c in range(NCH):
                        for h in range(2):
                            i = t.matmul(
                                psW[sl % 2][:, h * 512:(h + 1) * 512],
                                xr_fm[:, c, sl * 128:(sl + 1) * 128],
                                wslotA[:, c, h * 512:(h + 1) * 512],
                                start=(c == 0), stop=(c == NCH - 1))
                    i.then_inc(s_rfmm, 1)

                def vo_mm(t, ti, sl):
                    gs = ti * NSL + sl
                    t.wait_ge(s_kact, ti * NCH + NCH)
                    if gs == 0:
                        t.wait_ge(s_wfv, 16)
                    t.wait_ge(s_fe, gs + 1)
                    for j in range(NCH):
                        for h in range(2):
                            i = t.matmul(
                                psW[sl % 2][:, h * 512:(h + 1) * 512],
                                kact[:, j, sl * 128:(sl + 1) * 128],
                                wslotC[:, j, h * 512:(h + 1) * 512],
                                start=(j == 0), stop=(j == NCH - 1))
                    i.then_inc(s_vomm, 1)

                for sl in range(NSL):
                    transp(t, 0, sl)
                for ti in range(NT):
                    nxt = ti + 1 < NT
                    if nxt:
                        transp(t, ti + 1, 0)
                    t.wait_ge(s_mixk, ti * NCH + NCH)
                    for j in range(NCH):
                        g = ti * NCH + j
                        if g >= 2:
                            t.wait_ge(s_kact, g - 1)
                        for c in range(NCH):
                            i = t.matmul(psB[j % 2][:],
                                         wslotB[:, c, j * 128:(j + 1) * 128],
                                         xk_fm[:, c, :],
                                         start=(c == 0), stop=(c == NCH - 1))
                        i.then_inc(s_kfmm, 1)
                    t.wait_ge(s_mixr, ti * NCH + NCH)
                    for sl in range(NSL):
                        if nxt and sl + 1 < NSL:
                            transp(t, ti + 1, sl + 1)
                        rf_mm(t, ti, sl)
                        vo_mm(t, ti, sl)

            @blk.scalar
            def _(sc):
                def evac(sc, k, sl):
                    gs = k * NSL + sl
                    sc.wait_ge(s_T3, gs + 1)
                    if k >= 1 and sl == 0:
                        sc.wait_ge(s_mixr, k * NCH)
                    if sl == NSL - 1:
                        sc.wait_ge(s_bnd3, k + 1)
                    sc.copy(y1fm[:, 0:4, 1 + sl * 128:1 + (sl + 1) * 128],
                            psA[0][:].bitcast(F32R).rearrange(
                                "p (a b) -> p a b", a=4))
                    sc.copy(y1fm[:, 4:8, 1 + sl * 128:1 + (sl + 1) * 128],
                            psA[1][:].bitcast(F32R).rearrange(
                                "p (a b) -> p a b", a=4)).then_inc(s_ev3, 1)

                def sig_block(sc, ti, sl):
                    gs = ti * NSL + sl
                    sc.wait_ge(s_rfmm, gs + 1)
                    sf = sigf[sl % 2]
                    if gs >= 2:
                        sc.wait_ge(s_pr, gs - 1)
                    sc.activation(sf[:], psW[sl % 2][:], Act.Exp,
                                  scale=cpool[:, CB_M1:CB_M1 + 1]
                                  ).then_inc(s_fe, 1)
                    sc.drain()
                    sc.activation(sf[:], sf[:], Act.Ln,
                                  bias=cpool[:, CB_1:CB_1 + 1])
                    sc.drain()
                    sc.activation(sf[:], sf[:], Act.Exp,
                                  scale=cpool[:, CB_M1:CB_M1 + 1]
                                  ).then_inc(s_sigf, 1)

                def sq_rsq(sc, ti, sl):
                    gs = ti * NSL + sl
                    sc.wait_ge(s_pr, gs + 1)
                    sc.drain()
                    sc.activation(sq_junk[:], y2p[sl % 2][:, 0:512],
                                  Act.Square,
                                  accum_out=sumsq[:, 2 * sl:2 * sl + 1])
                    sc.drain()
                    sc.activation(sq_junk[:], y2p[sl % 2][:, 512:C],
                                  Act.Square,
                                  accum_out=sumsq[:, 2 * sl + 1:2 * sl + 2]
                                  ).then_inc(s_sq3, 1)
                    sc.wait_ge(s_var3, gs + 1)
                    sc.activation(is_col[:, sl:sl + 1], v_col[:, sl:sl + 1],
                                  Act.Ln,
                                  bias=cpool[:, CB_LNEPS:CB_LNEPS + 1])
                    sc.drain()
                    sc.activation(is_col[:, sl:sl + 1], is_col[:, sl:sl + 1],
                                  Act.Exp, scale=cpool[:, CB_M05:CB_M05 + 1]
                                  ).then_inc(s_rsq3, 1)

                for sl in range(NSL):
                    evac(sc, 0, sl)
                for ti in range(NT):
                    nxt = ti + 1 < NT
                    for j in range(NCH):
                        g = ti * NCH + j
                        sc.wait_ge(s_kfmm, g + 1)
                        sc.drain()
                        sc.activation(sq_junk[:], psB[j % 2][:], Act.Relu)
                        sc.drain()
                        sc.activation(kact[:, j, :], sq_junk[:], Act.Square
                                      ).then_inc(s_kact, 1)
                    for sl in range(NSL):
                        if nxt:
                            evac(sc, ti + 1, sl)
                        sig_block(sc, ti, sl)
                        if sl >= 1:
                            sq_rsq(sc, ti, sl - 1)
                    sq_rsq(sc, ti, NSL - 1)

            @blk.vector
            def _(v):
                i = v.memset(ybcol[:], 0.0)
                if exchange:
                    v.wait_ge(s_ycol, 1)
                    v.drain()
                    i = v.tensor_copy(ybcol[:], ycol0[:])
                i.then_inc(s_bnd3, 1)
                v.drain()

                def xk_pass(v, k):
                    if k >= 1:
                        v.wait_ge(s_kfmm, k * NCH)
                    if k == 0:
                        v.tensor_copy(y1fm[:, :, 0:1],
                                      ybcol[:].rearrange("p (a b) -> p a b", b=1))
                    v.wait_ge(s_ev3, k * NSL + NSL)
                    for j in range(NCH + 1):
                        v.drain()
                        if j < NCH:
                            v.tensor_tensor(d_buf[j % 2][:],
                                            y1fm[:, j, 1:TILE + 1],
                                            y1fm[:, j, 0:TILE], Alu.subtract)
                        if j >= 1:
                            jj = j - 1
                            v.scalar_tensor_tensor(
                                xk_fm[:, jj, :], d_buf[jj % 2][:],
                                muk1_c[:, jj:jj + 1],
                                y1fm[:, jj, 1:TILE + 1], Alu.mult, Alu.add
                                ).then_inc(s_mixk, 1)
                    v.drain()

                def xr_pass(v, k):
                    if k >= 1:
                        v.wait_ge(s_rfmm, k * NSL)
                    for j in range(NCH + 1):
                        v.drain()
                        if j < NCH:
                            v.tensor_tensor(d_buf[j % 2][:],
                                            y1fm[:, j, 1:TILE + 1],
                                            y1fm[:, j, 0:TILE], Alu.subtract)
                        if j >= 1:
                            jj = j - 1
                            v.scalar_tensor_tensor(
                                xr_fm[:, jj, :], d_buf[jj % 2][:],
                                mur1_c[:, jj:jj + 1],
                                y1fm[:, jj, 1:TILE + 1], Alu.mult, Alu.add
                                ).then_inc(s_mixr, 1)
                    v.tensor_copy(y1fm[:, :, 0:1],
                                  y1fm[:, :, TILE:TILE + 1]).then_inc(s_bnd3, 1)
                    v.drain()

                def prod_ln(v, ti, sl):
                    gs = ti * NSL + sl
                    v.wait_ge(s_vomm, gs + 1)
                    v.wait_ge(s_sigf, gs + 1)
                    v.wait_ge(s_l3r[gs % 2], 16 * (gs // 2 + 1))
                    if gs >= 2:
                        v.wait_ge(s_sto[gs % 2], 16 * (gs // 2))
                    v.tensor_tensor(y2p[sl % 2][:], psW[sl % 2][:],
                                    sigf[sl % 2][:], Alu.mult)
                    v.drain()
                    v.scalar_tensor_tensor(
                        y2p[sl % 2][:], y2p[sl % 2][:], 1.0,
                        y1r[gs % 2][:].bitcast(F32),
                        Alu.mult, Alu.add,
                        accum_out=sums[:, sl:sl + 1]).then_inc(s_pr, 1)
                    v.wait_ge(s_sq3, gs + 1)
                    v.drain()
                    v.tensor_scalar(m_col[:, sl:sl + 1], sums[:, sl:sl + 1],
                                    1.0 / C, None, Alu.mult)
                    v.tensor_tensor(v_col[:, sl:sl + 1],
                                    sumsq[:, 2 * sl:2 * sl + 1],
                                    sumsq[:, 2 * sl + 1:2 * sl + 2], Alu.add)
                    v.drain()
                    v.tensor_tensor(b_col[:, sl:sl + 1], m_col[:, sl:sl + 1],
                                    m_col[:, sl:sl + 1], Alu.mult)
                    v.drain()
                    v.scalar_tensor_tensor(v_col[:, sl:sl + 1],
                                           v_col[:, sl:sl + 1], 1.0 / C,
                                           b_col[:, sl:sl + 1],
                                           Alu.mult, Alu.subtract
                                           ).then_inc(s_var3, 1)
                    v.wait_ge(s_rsq3, gs + 1)
                    v.scalar_tensor_tensor(b_col[:, sl:sl + 1],
                                           m_col[:, sl:sl + 1], -1.0,
                                           is_col[:, sl:sl + 1],
                                           Alu.mult, Alu.mult)
                    v.drain()
                    v.tensor_scalar(y2p[sl % 2][:], y2p[sl % 2][:],
                                    is_col[:, sl:sl + 1],
                                    b_col[:, sl:sl + 1],
                                    Alu.mult, Alu.add).then_inc(s_ow, 1)

                xk_pass(v, 0)
                xr_pass(v, 0)
                for ti in range(NT):
                    nxt = ti + 1 < NT
                    prod_ln(v, ti, 0)
                    prod_ln(v, ti, 1)
                    if nxt:
                        xk_pass(v, ti + 1)
                    prod_ln(v, ti, 2)
                    prod_ln(v, ti, 3)
                    if nxt:
                        xr_pass(v, ti + 1)

            @blk.gpsimd
            def _(g):
                for ti in range(NT):
                    for sl in range(NSL):
                        gs = ti * NSL + sl
                        g.wait_ge(s_ow, gs + 1)
                        g.dma_start(out=out_d[gs * 128:(gs + 1) * 128, :],
                                    in_=y2p[sl % 2][:]).then_inc(s_sto[gs % 2], 16)
                g.wait_ge(s_sto[0], 16 * 2 * NT)
                g.wait_ge(s_sto[1], 16 * 2 * NT)
        ph2b.close()
    return nc


_PROG_CACHE = {}


def _get_prog(T_c, exchange):
    key = (T_c, exchange)
    if key not in _PROG_CACHE:
        _PROG_CACHE[key] = build(T_c, exchange)
    return _PROG_CACHE[key]


def _fm(vec):
    return np.ascontiguousarray(np.asarray(vec, np.float32).reshape(NCH, 128).T)


def kernel(L0_attn_x, L0_attn_time_decay, L0_attn_time_first, L0_attn_time_mix,
           L0_attn_Wk, L0_attn_Wv, L0_attn_Wr, L0_attn_Wo,
           L0_ffn_W_r, L0_ffn_W_k, L0_ffn_W_v, L0_ffn_mu_r, L0_ffn_mu_k):
    x = np.ascontiguousarray(np.asarray(L0_attn_x, np.float32))
    B, T, _ = x.shape
    wT = {"WkT": L0_attn_Wk, "WvT": L0_attn_Wv, "WrT": L0_attn_Wr,
          "WoT": L0_attn_Wo, "fWrT": L0_ffn_W_r, "fWkT": L0_ffn_W_k,
          "fWvT": L0_ffn_W_v}
    wT = {k: np.ascontiguousarray(np.asarray(v, np.float32).T)
          for k, v in wT.items()}
    base = {
        "tm_fm": _fm(np.asarray(L0_attn_time_mix, np.float32).reshape(C)),
        "mur_fm": _fm(L0_ffn_mu_r),
        "muk_fm": _fm(L0_ffn_mu_k),
        "td_fm": _fm(np.repeat(np.asarray(L0_attn_time_decay, np.float32)[:, 0], 64)),
        "tf_fm": _fm(np.repeat(np.asarray(L0_attn_time_first, np.float32)[:, 0], 64)),
    }
    base.update(wT)
    zrow = np.zeros((128, NCH), np.float32)

    if not EXCHANGE:
        nc = _get_prog(T, False)
        in_maps = [dict(base, x=x[core % B], xprev_fm=zrow) for core in range(8)]
        res = run_bass_kernel_spmd(nc, in_maps, core_ids=list(range(8)))
        return np.stack([res.results[b]["out"] for b in range(B)], axis=0)
    else:
        T_c = T // 2
        nc = _get_prog(T_c, True)
        one = np.ones((128, 1), np.float32)
        zero = np.zeros((128, 1), np.float32)
        in_maps = []
        for core in range(8):
            b, half = core // 2, core % 2
            t0 = half * T_c
            in_maps.append(dict(
                base,
                x=np.ascontiguousarray(x[b, t0:t0 + T_c]),
                xprev_fm=(zrow if half == 0 else _fm(x[b, t0 - 1])),
                flagc=(one if half == 0 else zero),
                flagu=(zero if half == 0 else one)))
        res = run_bass_kernel_spmd(nc, in_maps, core_ids=list(range(8)))
        return np.stack(
            [np.concatenate([res.results[2 * b]["out"],
                             res.results[2 * b + 1]["out"]], axis=0)
             for b in range(B)], axis=0)


# version 21
# speedup vs baseline: 1.1507x; 1.0195x over previous
import numpy as np
import concourse.bass as bass
import concourse.mybir as mybir
from concourse.bass_utils import run_bass_kernel_spmd

F32 = mybir.dt.float32
F32R = mybir.dt.float32r
Act = mybir.ActivationFunctionType
Alu = mybir.AluOpType

C = 1024
NCH = C // 128
TILE = 512
NSL = TILE // 128

EXCHANGE = True


def build(T_c, exchange):
    NT = T_c // TILE
    nc = bass.Bass("TRN2", target_bir_lowering=False,
                   **({"num_devices": 8} if exchange else {}))

    x_d = nc.dram_tensor("x", [T_c, C], F32R, kind="ExternalInput").ap()
    xprev_d = nc.dram_tensor("xprev_fm", [128, NCH], F32, kind="ExternalInput").ap()
    w_d = {n: nc.dram_tensor(n, [C, C], F32R, kind="ExternalInput").ap()
           for n in ["WkT", "WvT", "WrT", "WoT", "fWrT", "fWkT", "fWvT"]}
    tm_d = nc.dram_tensor("tm_fm", [128, NCH], F32, kind="ExternalInput").ap()
    mur_d = nc.dram_tensor("mur_fm", [128, NCH], F32, kind="ExternalInput").ap()
    muk_d = nc.dram_tensor("muk_fm", [128, NCH], F32, kind="ExternalInput").ap()
    td_d = nc.dram_tensor("td_fm", [128, NCH], F32, kind="ExternalInput").ap()
    tf_d = nc.dram_tensor("tf_fm", [128, NCH], F32, kind="ExternalInput").ap()
    out_d = nc.dram_tensor("out", [T_c, C], F32, kind="ExternalOutput").ap()

    xm_d = nc.dram_tensor("xm_s", [NT, NCH, 128, TILE], F32R, kind="Internal").ap()
    wkv_d = nc.dram_tensor("wkv_s", [NT, NCH, 128, TILE], F32, kind="Internal").ap()
    sumk_d = nc.dram_tensor("sumk_s", [NT, NCH, 128, TILE], F32, kind="Internal").ap()
    y1_d = nc.dram_tensor("y1_s", [NT, NSL, 128, C], F32R, kind="Internal").ap()

    if exchange:
        flagc_d = nc.dram_tensor("flagc", [128, 1], F32, kind="ExternalInput").ap()
        flagu_d = nc.dram_tensor("flagu", [128, 1], F32, kind="ExternalInput").ap()
        cc1_in = nc.dram_tensor("cc1_in", [128, 2 * NCH], F32, kind="Internal")
        cc1_out = nc.dram_tensor("cc1_out", [128, 2 * NCH], F32, kind="Internal")
        cc2_in = nc.dram_tensor("cc2_in", [1, C], F32, kind="Internal")
        cc2_out = nc.dram_tensor("cc2_out", [1, C], F32, kind="Internal")

    xr_view = x_d.rearrange("(a p) c -> p a c", p=128)

    def st1(ti, j):
        return 272 * ti + 16 + 32 * (j + 1)
    ST1_ALL = 272 * NT
    ST2_ALL = ST1_ALL + 16 * NT * NSL
    def ld2(ti, part):
        return 16 * (ti * (NSL + 1 + 2 * NCH) + part)

    from contextlib import ExitStack
    with ExitStack() as ctx:
        sbuf = lambda *a: ctx.enter_context(nc.sbuf_tensor(*a))

        cpool = sbuf("cpool", [128, 8], F32)
        CB_60, CB_M1, CB_1, CB_EPS16, CB_LNEPS, CB_M05, CB_M512 = range(7)
        ident = sbuf("ident", [128, 128], F32R)
        tm_c = sbuf("tm_c", [128, NCH], F32)
        mur_c = sbuf("mur_c", [128, NCH], F32)
        muk_c = sbuf("muk_c", [128, NCH], F32)
        mur1_c = sbuf("mur1_c", [128, NCH], F32)
        muk1_c = sbuf("muk1_c", [128, NCH], F32)
        td_c = sbuf("td_c", [128, NCH], F32)
        tf_c = sbuf("tf_c", [128, NCH], F32)
        dec_c = sbuf("dec_c", [128, NCH], F32)
        wdec_c = sbuf("wdec_c", [128, NCH], F32)
        etf_c = sbuf("etf_c", [128, NCH], F32)
        w512_c = sbuf("w512_c", [128, NCH], F32)
        dexpt = sbuf("dexpt", [128, NCH, TILE], F32R)
        carry_kv = sbuf("carry_kv", [128, NCH], F32)
        carry_k = sbuf("carry_k", [128, NCH], F32)
        i_kv = sbuf("i_kv", [128, NCH], F32)
        i_k = sbuf("i_k", [128, NCH], F32)
        sq_junk = sbuf("sq_junk", [128, 512], F32)
        sums = sbuf("sums", [128, NSL], F32)
        sumsq = sbuf("sumsq", [128, 2 * NSL], F32)
        m_col = sbuf("m_col", [128, NSL], F32)
        v_col = sbuf("v_col", [128, NSL], F32)
        is_col = sbuf("is_col", [128, NSL], F32)
        b_col = sbuf("b_col", [128, NSL], F32)
        wslotA = sbuf("wslotA", [128, NCH, C], F32R)
        wslotB = sbuf("wslotB", [128, NCH, C], F32R)
        wslotC = sbuf("wslotC", [128, NCH, C], F32R)
        if exchange:
            flagc = sbuf("flagc_s", [128, 1], F32)
            flagu = sbuf("flagu_s", [128, 1], F32)
            pay1 = sbuf("pay1", [128, 2 * NCH], F32)
            rcv1 = sbuf("rcv1", [128, 2 * NCH], F32)
            pay2 = sbuf("pay2", [128, C], F32)
            rcv2fm = sbuf("rcv2fm", [128, NCH], F32)
            ycol0 = sbuf("ycol0", [128, NCH], F32)

        psA = [ctx.enter_context(nc.psum_tensor(f"psA{i}", [128, 512], F32))
               for i in range(2)]
        psB = [ctx.enter_context(nc.psum_tensor(f"psB{i}", [128, 512], F32))
               for i in range(2)]
        psW = [ctx.enter_context(nc.psum_tensor(f"psW{i}", [128, C], F32))
               for i in range(2)]
        psK3 = [psA[0][:], psA[1][:], psW[0][:, 0:512]]
        psV3 = [psB[0][:], psB[1][:], psW[0][:, 512:C]]

        S = lambda name: ctx.enter_context(nc.semaphore(name))
        s_init = S("s_init")
        s_wl = S("s_wl")
        s_wlo = S("s_wlo")
        s_wlr = S("s_wlr")
        s_wfk = S("s_wfk")
        s_wfr = S("s_wfr")
        s_wfv = S("s_wfv")
        s_xld = [S("s_xld0"), S("s_xld1")]
        s_T = S("s_T")
        s_ev = S("s_ev")
        s_bnd = S("s_bnd")
        s_bndd = S("s_bndd")
        s_xm = S("s_xm")
        s_kmm = S("s_kmm")
        s_k = S("s_k")
        s_vmm = S("s_vmm")
        s_kv = S("s_kv")
        s_scan = S("s_scan")
        s_tts = S("s_tts")
        s_gw = S("s_gw")
        s_stx = S("s_stx")
        s_stw = [S("s_stw0"), S("s_stw1")]
        s_sty = [S("s_sty0"), S("s_sty1"), S("s_sty2"), S("s_sty3")]
        s_sto = [S("s_sto0"), S("s_sto1")]
        s_l2x = [S("s_l2x0"), S("s_l2x1"), S("s_l2x2"), S("s_l2x3")]
        s_l2m = [S("s_l2m0"), S("s_l2m1")]
        s_l2w = [S("s_l2w0"), S("s_l2w1"), S("s_l2w2")]
        s_rmm = S("s_rmm")
        s_e = S("s_e")
        s_den = S("s_den")
        s_inv = S("s_inv")
        s_rw = S("s_rw")
        s_omm = S("s_omm")
        s_y1p = S("s_y1p")
        s_sq = S("s_sq")
        s_var = S("s_var")
        s_rsq = S("s_rsq")
        s_y1w = S("s_y1w")
        s_l3t = [S("s_l3t0"), S("s_l3t1")]
        s_l3r = [S("s_l3r0"), S("s_l3r1")]
        s_T3 = S("s_T3")
        s_ev3 = S("s_ev3")
        s_bnd3 = S("s_bnd3")
        s_mixk = S("s_mixk")
        s_mixr = S("s_mixr")
        s_kfmm = S("s_kfmm")
        s_kact = S("s_kact")
        s_rfmm = S("s_rfmm")
        s_fe = S("s_fe")
        s_sigf = S("s_sigf")
        s_vomm = S("s_vomm")
        s_pr = S("s_pr")
        s_sq3 = S("s_sq3")
        s_var3 = S("s_var3")
        s_rsq3 = S("s_rsq3")
        s_ow = S("s_ow")
        s_pay1 = S("s_pay1")
        s_ccd = S("s_ccd")
        s_ccc = S("s_ccc")
        s_ikv = S("s_ikv")
        s_dx1 = S("s_dx1")
        s_dx2 = S("s_dx2")
        s_dx3 = S("s_dx3")
        s_pay2 = S("s_pay2")
        s_rowmm = S("s_rowmm")
        s_ycol = S("s_ycol")
        s_misc = S("s_misc")
        s_misc2 = S("s_misc2")

        blk = ctx.enter_context(nc.Block())

        ph0 = ExitStack()
        iot_r = ph0.enter_context(nc.sbuf_tensor("iot_r", [128, 128], F32))
        iot_c = ph0.enter_context(nc.sbuf_tensor("iot_c", [128, 128], F32))
        if True:

            @blk.sync
            def _(sync):
                sync.dma_start(out=tm_c[:], in_=tm_d).then_inc(s_misc, 16)
                sync.dma_start(out=mur_c[:], in_=mur_d).then_inc(s_misc, 16)
                sync.dma_start(out=muk_c[:], in_=muk_d).then_inc(s_misc, 16)
                sync.dma_start(out=td_c[:], in_=td_d).then_inc(s_misc, 16)
                sync.dma_start(out=tf_c[:], in_=tf_d).then_inc(s_misc, 16)
                if exchange:
                    sync.dma_start(out=flagc[:], in_=flagc_d).then_inc(s_misc2, 16)
                    sync.dma_start(out=flagu[:], in_=flagu_d).then_inc(s_misc2, 16)

            @blk.gpsimd
            def _(g):
                g.iota(iot_r[:], [[0, 128]], channel_multiplier=1,
                       allow_small_or_imprecise_dtypes=True)
                g.iota(iot_c[:], [[1, 128]], channel_multiplier=0,
                       allow_small_or_imprecise_dtypes=True).then_inc(s_init, 1)

            @blk.vector
            def _(v):
                for col, val in [(CB_60, 60.0), (CB_M1, -1.0), (CB_1, 1.0),
                                 (CB_EPS16, 1e-16), (CB_LNEPS, 1e-5),
                                 (CB_M05, -0.5), (CB_M512, -512.0)]:
                    v.memset(cpool[:, col:col + 1], val)
                v.memset(carry_kv[:], 0.0)
                v.memset(carry_k[:], 0.0)
                v.memset(i_kv[:], 0.0)
                v.memset(i_k[:], 0.0)
                v.wait_ge(s_misc, 16 * 5)
                v.tensor_scalar(mur1_c[:], mur_c[:], -1.0, None, Alu.add)
                v.tensor_scalar(muk1_c[:], muk_c[:], -1.0, None, Alu.add)
                v.wait_ge(s_init, 1)
                v.tensor_tensor(ident[:], iot_r[:], iot_c[:],
                                Alu.is_equal).then_inc(s_init, 1)

            @blk.scalar
            def _(sc):
                sc.wait_ge(s_misc, 16 * 5)
                sc.wait_ge(s_init, 2)
                sc.activation(dec_c[:], td_c[:], Act.Exp)
                sc.drain()
                sc.activation(wdec_c[:], dec_c[:], Act.Exp,
                              scale=cpool[:, CB_M1:CB_M1 + 1])
                sc.activation(w512_c[:], dec_c[:], Act.Exp,
                              scale=cpool[:, CB_M512:CB_M512 + 1])
                sc.activation(etf_c[:], tf_c[:], Act.Exp).then_inc(s_init, 1)


        @blk.gpsimd
        def _(g):
            g.wait_ge(s_init, 3)
            g.iota(sq_junk[:, 0:TILE], [[1, TILE]], channel_multiplier=0,
                   allow_small_or_imprecise_dtypes=True).then_inc(s_dx1, 1)

        @blk.vector
        def _(v):
            v.wait_ge(s_dx1, 1)
            v.wait_ge(s_init, 3)
            for j in range(NCH):
                i = v.tensor_scalar(dexpt[:, j, :], sq_junk[:, 0:TILE],
                                    dec_c[:, j:j + 1], None, Alu.mult)
            i.then_inc(s_dx2, 1)

        @blk.scalar
        def _(sc):
            sc.wait_ge(s_dx2, 1)
            for j in range(NCH):
                i = sc.activation(dexpt[:, j, :],
                                  dexpt[:, j, :].bitcast(F32), Act.Exp,
                                  scale=cpool[:, CB_M1:CB_M1 + 1])
            i.then_inc(s_dx3, 1)

        ph0.close()
        ph1 = ExitStack()
        x_tm0 = ph1.enter_context(nc.sbuf_tensor("x_tm0", [128, C], F32R))
        x_tm1 = ph1.enter_context(nc.sbuf_tensor("x_tm1", [128, C], F32R))
        x_fm = ph1.enter_context(nc.sbuf_tensor("x_fm", [128, NCH, TILE + 1], F32))
        d_full = ph1.enter_context(nc.sbuf_tensor("d_full", [128, NCH, TILE], F32))
        xm_fm = ph1.enter_context(nc.sbuf_tensor("xm_fm", [128, NCH, TILE], F32R))
        k_sb0 = ph1.enter_context(nc.sbuf_tensor("k_sb0", [128, TILE], F32))
        k_sb1 = ph1.enter_context(nc.sbuf_tensor("k_sb1", [128, TILE], F32))
        k_sb2 = ph1.enter_context(nc.sbuf_tensor("k_sb2", [128, TILE], F32))
        kv_sb0 = ph1.enter_context(nc.sbuf_tensor("kv_sb0", [128, TILE], F32))
        kv_sb1 = ph1.enter_context(nc.sbuf_tensor("kv_sb1", [128, TILE], F32))
        kv_sb2 = ph1.enter_context(nc.sbuf_tensor("kv_sb2", [128, TILE], F32))
        s_buf0 = ph1.enter_context(nc.sbuf_tensor("s_buf0", [128, TILE + 1], F32))
        s_buf1 = ph1.enter_context(nc.sbuf_tensor("s_buf1", [128, TILE + 1], F32))
        s_buf2 = ph1.enter_context(nc.sbuf_tensor("s_buf2", [128, TILE + 1], F32))
        sk_buf0 = ph1.enter_context(nc.sbuf_tensor("sk_buf0", [128, TILE + 1], F32))
        sk_buf1 = ph1.enter_context(nc.sbuf_tensor("sk_buf1", [128, TILE + 1], F32))
        sk_buf2 = ph1.enter_context(nc.sbuf_tensor("sk_buf2", [128, TILE + 1], F32))
        wkv_sb0 = ph1.enter_context(nc.sbuf_tensor("wkv_sb0", [128, TILE], F32))
        wkv_sb1 = ph1.enter_context(nc.sbuf_tensor("wkv_sb1", [128, TILE], F32))
        sumk_sb0 = ph1.enter_context(nc.sbuf_tensor("sumk_sb0", [128, TILE], F32))
        sumk_sb1 = ph1.enter_context(nc.sbuf_tensor("sumk_sb1", [128, TILE], F32))
        if True:
            x_tm = [x_tm0, x_tm1]
            k_sb, kv_sb = [k_sb0, k_sb1, k_sb2], [kv_sb0, kv_sb1, kv_sb2]
            s_bufs, sk_bufs = [s_buf0, s_buf1, s_buf2], [sk_buf0, sk_buf1, sk_buf2]
            wkv_sb, sumk_sb = [wkv_sb0, wkv_sb1], [sumk_sb0, sumk_sb1]

            @blk.gpsimd
            def _(g):
                g.wait_ge(s_init, 3)
                g.nop().then_inc(s_init, 1)
                g.dma_start(out=x_fm[:, :, 0:1],
                            in_=xprev_d.rearrange("p (a b) -> p a b", b=1),
                            allow_slow_non_contiguous=True
                            ).then_inc(s_bndd, 16)


            @blk.scalar
            def _(sc):
                sc.dma_start(out=wslotA[:], in_=w_d["WkT"].rearrange(
                    "(a p) c -> p a c", p=128)).then_inc(s_wl, 16)
                sc.dma_start(out=wslotB[:], in_=w_d["WvT"].rearrange(
                    "(a p) c -> p a c", p=128)).then_inc(s_wl, 16)
                sc.dma_start(out=wslotC[:], in_=w_d["WoT"].rearrange(
                    "(a p) c -> p a c", p=128)).then_inc(s_wlo, 16)

            @blk.sync
            def _(sync):
                sync.wait_ge(s_init, 3)
                for ti in range(NT):
                    for sl in range(NSL):
                        gs = ti * NSL + sl
                        if gs >= 2:
                            sync.wait_ge(s_T, gs - 1)
                        sync.dma_start(
                            out=x_tm[gs % 2][:],
                            in_=xr_view[:, gs, :],
                        ).then_inc(s_xld[gs % 2], 16)

            @blk.tensor
            def _(t):
                t.wait_ge(s_init, 2)
                for ti in range(NT):
                    for sl in range(NSL):
                        gs = ti * NSL + sl
                        t.wait_ge(s_xld[gs % 2], 16 * (gs // 2 + 1))
                        if gs >= 1:
                            t.wait_ge(s_ev, gs)
                        if ti >= 1 and sl == 0:
                            t.wait_ge(s_k, ti * NCH)
                        for j in range(NCH):
                            dst = psA[0] if j < 4 else psA[1]
                            i = t.transpose(
                                dst[:].bitcast(F32R)[:, (j % 4) * 128:(j % 4 + 1) * 128],
                                x_tm[gs % 2][:, j * 128:(j + 1) * 128],
                                ident[:])
                        i.then_inc(s_T, 1)
                    if ti == 0:
                        t.wait_ge(s_wl, 32)
                    t.wait_ge(s_xm, ti + 1)
                    for j in range(NCH):
                        g = ti * NCH + j
                        if g >= 3:
                            t.wait_ge(s_k, g - 2)
                        pk = psK3[g % 3]
                        for c in range(NCH):
                            i = t.matmul(pk,
                                         wslotA[:, c, j * 128:(j + 1) * 128],
                                         xm_fm[:, c, :],
                                         start=(c == 0), stop=(c == NCH - 1))
                        i.then_inc(s_kmm, 1)
                        if g >= 3:
                            t.wait_ge(s_kv, g - 2)
                        pv = psV3[g % 3]
                        for c in range(NCH):
                            i = t.matmul(pv,
                                         wslotB[:, c, j * 128:(j + 1) * 128],
                                         xm_fm[:, c, :],
                                         start=(c == 0), stop=(c == NCH - 1))
                        i.then_inc(s_vmm, 1)

            @blk.scalar
            def _(sc):
                for ti in range(NT):
                    for sl in range(NSL):
                        gs = ti * NSL + sl
                        sc.wait_ge(s_T, gs + 1)
                        if ti >= 1 and sl == 0:
                            sc.wait_ge(s_xm, ti)
                        if sl == NSL - 1 and ti >= 1:
                            sc.wait_ge(s_bnd, ti)
                        sc.copy(x_fm[:, 0:4, 1 + sl * 128:1 + (sl + 1) * 128],
                                psA[0][:].bitcast(F32R).rearrange(
                                    "p (a b) -> p a b", a=4))
                        sc.copy(x_fm[:, 4:8, 1 + sl * 128:1 + (sl + 1) * 128],
                                psA[1][:].bitcast(F32R).rearrange(
                                    "p (a b) -> p a b", a=4)).then_inc(s_ev, 1)
                    for j in range(NCH):
                        g = ti * NCH + j
                        sc.wait_ge(s_kmm, g + 1)
                        if g >= 3:
                            sc.wait_ge(s_scan, g - 2)
                        kb = k_sb[g % 3]
                        sc.activation(kb[:], psK3[g % 3], Act.Relu,
                                      bias=cpool[:, CB_60:CB_60 + 1],
                                      scale=cpool[:, CB_M1:CB_M1 + 1])
                        sc.drain()
                        sc.activation(kb[:], kb[:], Act.Exp,
                                      bias=cpool[:, CB_60:CB_60 + 1],
                                      scale=cpool[:, CB_M1:CB_M1 + 1]
                                      ).then_inc(s_k, 1)

            @blk.vector
            def _(v):
                v.wait_ge(s_init, 4)
                for ti in range(NT):
                    v.wait_ge(s_ev, ti * NSL + NSL)
                    if ti == 0:
                        v.wait_ge(s_bndd, 16)
                    else:
                        v.wait_ge(s_stx, 16 * ti)
                    v.drain()
                    v.tensor_tensor(d_full[:], x_fm[:, :, 1:TILE + 1],
                                    x_fm[:, :, 0:TILE], Alu.subtract)
                    v.drain()
                    for j in range(NCH):
                        i = v.scalar_tensor_tensor(
                            xm_fm[:, j, :], d_full[:, j, :], tm_c[:, j:j + 1],
                            x_fm[:, j, 0:TILE], Alu.mult, Alu.add)
                    i.then_inc(s_xm, 1)
                    v.drain()
                    v.tensor_copy(x_fm[:, :, 0:1],
                                  x_fm[:, :, TILE:TILE + 1]).then_inc(s_bnd, 1)
                    g0 = ti * NCH
                    v.wait_ge(s_k, g0 + 1)
                    v.wait_ge(s_vmm, g0 + 1)

                    v.tensor_tensor(kv_sb[g0 % 3][:], k_sb[g0 % 3][:],
                                    psV3[g0 % 3], Alu.mult).then_inc(s_kv, 1)
                    v.tensor_copy(s_bufs[g0 % 3][:, 0:1], carry_kv[:, 0:1])
                    v.tensor_copy(sk_bufs[g0 % 3][:, 0:1], carry_k[:, 0:1])
                    for j in range(NCH + 1):
                        g = ti * NCH + j
                        v.drain()
                        if j < NCH:
                            wj = wdec_c[:, j:j + 1].broadcast_to([128, TILE])
                            v.tensor_tensor_scan(s_bufs[g % 3][:, 1:TILE + 1],
                                                 wj, kv_sb[g % 3][:],
                                                 s_bufs[g % 3][:, 0:1],
                                                 Alu.mult, Alu.add)
                            i = v.tensor_tensor_scan(sk_bufs[g % 3][:, 1:TILE + 1],
                                                 wj, k_sb[g % 3][:],
                                                 sk_bufs[g % 3][:, 0:1],
                                                 Alu.mult, Alu.add)
                        if j >= 1:
                            jj, gg = j - 1, g - 1
                            if gg >= 2:
                                v.wait_ge(s_stw[gg % 2], 32 * (gg // 2))
                            v.scalar_tensor_tensor(
                                wkv_sb[gg % 2][:], kv_sb[gg % 3][:],
                                etf_c[:, jj:jj + 1],
                                s_bufs[gg % 3][:, 0:TILE], Alu.mult, Alu.add)
                            v.scalar_tensor_tensor(
                                sumk_sb[gg % 2][:], k_sb[gg % 3][:],
                                etf_c[:, jj:jj + 1],
                                sk_bufs[gg % 3][:, 0:TILE], Alu.mult, Alu.add)
                            v.tensor_copy(carry_kv[:, jj:jj + 1],
                                          s_bufs[gg % 3][:, TILE:TILE + 1])
                            v.tensor_copy(carry_k[:, jj:jj + 1],
                                          sk_bufs[gg % 3][:, TILE:TILE + 1]
                                          ).then_inc(s_scan, 1)
                        if j + 1 < NCH:
                            gn = g + 1
                            v.wait_ge(s_k, gn + 1)
                            v.wait_ge(s_vmm, gn + 1)

                            v.tensor_tensor(kv_sb[gn % 3][:], k_sb[gn % 3][:],
                                            psV3[gn % 3], Alu.mult
                                            ).then_inc(s_kv, 1)
                            v.tensor_copy(s_bufs[gn % 3][:, 0:1],
                                          carry_kv[:, j + 1:j + 2])
                            v.tensor_copy(sk_bufs[gn % 3][:, 0:1],
                                          carry_k[:, j + 1:j + 2])

            @blk.gpsimd
            def _(g):
                for ti in range(NT):
                    g.wait_ge(s_xm, ti + 1)
                    g.dma_start(out=xm_d[ti].rearrange("a p t -> p a t"),
                                in_=xm_fm[:]).then_inc(s_stx, 16)
                    for j in range(NCH):
                        gg = ti * NCH + j
                        g.wait_ge(s_scan, gg + 1)
                        g.dma_start(out=wkv_d[ti, j], in_=wkv_sb[gg % 2][:]
                                    ).then_inc(s_stw[gg % 2], 16)
                        g.dma_start(out=sumk_d[ti, j], in_=sumk_sb[gg % 2][:]
                                    ).then_inc(s_stw[gg % 2], 16)

            if exchange:
                @blk.vector
                def _(v):
                    v.wait_ge(s_misc2, 32)
                    v.drain()
                    v.tensor_scalar(pay1[:, 0:NCH], carry_kv[:],
                                    flagc[:, 0:1], None, Alu.mult)
                    v.tensor_scalar(pay1[:, NCH:2 * NCH], carry_k[:],
                                    flagc[:, 0:1], None, Alu.mult
                                    ).then_inc(s_pay1, 1)

                @blk.gpsimd
                def _(g):
                    g.wait_ge(s_pay1, 1)
                    g.dma_start(out=cc1_in.ap(), in_=pay1[:]).then_inc(s_ccd, 16)
                    g.wait_ge(s_ccd, 16)
                    g.collective_compute(
                        "AllReduce", Alu.add,
                        replica_groups=[[0, 1], [2, 3], [4, 5], [6, 7]],
                        ins=[cc1_in.ap()], outs=[cc1_out.ap()],
                    ).then_inc(s_ccc, 1)
                    g.wait_ge(s_ccc, 1)
                    g.dma_start(out=rcv1[:], in_=cc1_out.ap()).then_inc(s_ccd, 16)

                @blk.vector
                def _(v):
                    v.wait_ge(s_ccd, 32)
                    v.tensor_scalar(i_kv[:], rcv1[:, 0:NCH], flagu[:, 0:1],
                                    None, Alu.mult)
                    v.tensor_scalar(i_k[:], rcv1[:, NCH:2 * NCH], flagu[:, 0:1],
                                    None, Alu.mult).then_inc(s_ikv, 1)

        ph1.close()
        ph2a = ExitStack()
        xm2 = [ph2a.enter_context(nc.sbuf_tensor(f"xm2_{i}", [128, NCH, TILE], F32R)) for i in range(2)]
        wkv_in = [ph2a.enter_context(nc.sbuf_tensor(f"wkv_in{i}", [128, TILE], F32)) for i in range(3)]
        sumk_in = [ph2a.enter_context(nc.sbuf_tensor(f"sumk_in{i}", [128, TILE], F32)) for i in range(3)]
        e_sb = [ph2a.enter_context(nc.sbuf_tensor(f"e_sb{i}", [128, TILE], F32)) for i in range(3)]
        rwkv = ph2a.enter_context(nc.sbuf_tensor("rwkv", [128, NCH, TILE], F32R))
        y1p = [ph2a.enter_context(nc.sbuf_tensor(f"y1p{i}", [128, C], F32R)) for i in range(4)]
        if True:

            @blk.scalar
            def _(sc):
                sc.wait_ge(s_kmm, NT * NCH)
                sc.dma_start(out=wslotA[:], in_=w_d["WrT"].rearrange(
                    "(a p) c -> p a c", p=128)).then_inc(s_wlr, 16)
                sc.wait_ge(s_vmm, NT * NCH)
                sc.dma_start(out=wslotB[:], in_=w_d["fWkT"].rearrange(
                    "(a p) c -> p a c", p=128)).then_inc(s_wfk, 16)

            @blk.sync
            def _(sync):
                sync.wait_ge(s_stx, 16 * NT)
                sync.wait_ge(s_stw[0], 32 * 4 * NT)
                sync.wait_ge(s_stw[1], 32 * 4 * NT)
                def xm2_load(sync, ti):
                    if ti >= 2:
                        sync.wait_ge(s_rmm, (ti - 1) * NCH)
                    sync.dma_start(out=xm2[ti % 2][:],
                                   in_=xm_d[ti].rearrange("a p t -> p a t")
                                   ).then_inc(s_l2m[ti % 2], 16)

                xm2_load(sync, 0)
                xm2_load(sync, 1)
                for ti in range(NT):
                    for j in range(NCH):
                        g = ti * NCH + j
                        if g >= 3:
                            sync.wait_ge(s_rw, g - 2)
                        sync.dma_start(out=wkv_in[g % 3][:], in_=wkv_d[ti, j]
                                       ).then_inc(s_l2w[g % 3], 16)
                        sync.dma_start(out=sumk_in[g % 3][:], in_=sumk_d[ti, j]
                                       ).then_inc(s_l2w[g % 3], 16)
                    for sl in range(NSL):
                        gs = ti * NSL + sl
                        if gs >= 4:
                            sync.wait_ge(s_sty[gs % 4], 16 * (gs // 4))
                        sync.dma_start(out=y1p[gs % 4][:],
                                       in_=xr_view[:, gs, :]
                                       ).then_inc(s_l2x[gs % 4], 16)
                    if ti + 2 < NT:
                        xm2_load(sync, ti + 2)

            @blk.tensor
            def _(t):
                t.wait_ge(s_wlr, 16)
                t.wait_ge(s_wlo, 16)

                def r_proj(t, ti, j0, j1):
                    if j0 == 0:
                        t.wait_ge(s_l2m[ti % 2], 16 * (ti // 2 + 1))
                    for j in range(j0, j1):
                        g = ti * NCH + j
                        if g >= 2:
                            t.wait_ge(s_e, g - 1)
                        for c in range(NCH):
                            i = t.matmul(psA[j % 2][:],
                                         wslotA[:, c, j * 128:(j + 1) * 128],
                                         xm2[ti % 2][:, c, :],
                                         start=(c == 0), stop=(c == NCH - 1))
                        i.then_inc(s_rmm, 1)

                def wo_proj(t, ti):
                    for sl in range(NSL):
                        gs = ti * NSL + sl
                        if sl == 1:
                            t.wait_ge(s_rw, ti * NCH + NCH)
                        if gs >= 2:
                            t.wait_ge(s_y1p, gs - 1)
                        for j in range(NCH):
                            if sl == 0:
                                t.wait_ge(s_rw, ti * NCH + j + 1)
                            for h in range(2):
                                i = t.matmul(
                                    psW[sl % 2][:, h * 512:(h + 1) * 512],
                                    rwkv[:, j, sl * 128:(sl + 1) * 128],
                                    wslotC[:, j, h * 512:(h + 1) * 512],
                                    start=(j == 0), stop=(j == NCH - 1))
                        i.then_inc(s_omm, 1)

                for ti in range(NT):
                    r_proj(t, ti, 0, NCH)
                    wo_proj(t, ti)

            @blk.scalar
            def _(sc):
                for ti in range(NT):
                    for j in range(NCH):
                        g = ti * NCH + j
                        sc.wait_ge(s_rmm, g + 1)
                        if g >= 3:
                            sc.wait_ge(s_rw, g - 2)
                        sc.activation(e_sb[g % 3][:], psA[j % 2][:], Act.Exp,
                                      scale=cpool[:, CB_M1:CB_M1 + 1]
                                      ).then_inc(s_e, 1)
                        sc.wait_ge(s_den, g + 1)
                        sc.drain()
                        sc.activation(e_sb[g % 3][:], e_sb[g % 3][:],
                                      Act.Ln,
                                      bias=cpool[:, CB_EPS16:CB_EPS16 + 1])
                        sc.drain()
                        sc.activation(e_sb[g % 3][:], e_sb[g % 3][:],
                                      Act.Exp, scale=cpool[:, CB_M1:CB_M1 + 1]
                                      ).then_inc(s_inv, 1)
                    for sl in range(NSL):
                        gs = ti * NSL + sl
                        sc.wait_ge(s_y1p, gs + 1)
                        sc.drain()
                        sc.activation(sq_junk[:],
                                      y1p[gs % 4][:, 0:512].bitcast(F32),
                                      Act.Square,
                                      accum_out=sumsq[:, 2 * sl:2 * sl + 1])
                        sc.drain()
                        sc.activation(sq_junk[:],
                                      y1p[gs % 4][:, 512:C].bitcast(F32),
                                      Act.Square,
                                      accum_out=sumsq[:, 2 * sl + 1:2 * sl + 2]
                                      ).then_inc(s_sq, 1)
                    sc.wait_ge(s_var, ti + 1)
                    sc.activation(is_col[:], v_col[:], Act.Ln,
                                  bias=cpool[:, CB_LNEPS:CB_LNEPS + 1])
                    sc.drain()
                    sc.activation(is_col[:], is_col[:], Act.Exp,
                                  scale=cpool[:, CB_M05:CB_M05 + 1]
                                  ).then_inc(s_rsq, 1)

            @blk.vector
            def _(v):
                if exchange:
                    v.wait_ge(s_ikv, 1)
                v.wait_ge(s_dx3, 1)
                for ti in range(NT):
                    if ti >= 1:
                        v.wait_ge(s_omm, ti * NSL)
                    for j in range(NCH):
                        g = ti * NCH + j
                        v.wait_ge(s_l2w[g % 3], 32 * (g // 3 + 1))
                        v.wait_ge(s_e, g + 1)
                        if g >= 3:
                            v.wait_ge(s_inv, g - 2)
                        if exchange:
                            v.scalar_tensor_tensor(
                                wkv_in[g % 3][:], dexpt[:, j, :].bitcast(F32),
                                i_kv[:, j:j + 1], wkv_in[g % 3][:],
                                Alu.mult, Alu.add)
                            v.scalar_tensor_tensor(
                                sumk_in[g % 3][:], dexpt[:, j, :].bitcast(F32),
                                i_k[:, j:j + 1], sumk_in[g % 3][:],
                                Alu.mult, Alu.add)
                            v.drain()
                        v.scalar_tensor_tensor(
                            e_sb[g % 3][:], e_sb[g % 3][:], 1.0,
                            sumk_in[g % 3][:], Alu.add, Alu.mult
                            ).then_inc(s_den, 1)
                        v.wait_ge(s_inv, g + 1)
                        v.tensor_tensor(rwkv[:, j, :], wkv_in[g % 3][:],
                                        e_sb[g % 3][:], Alu.mult
                                        ).then_inc(s_rw, 1)

                    if exchange:
                        v.tensor_tensor(i_kv[:], i_kv[:], w512_c[:], Alu.mult)
                        v.tensor_tensor(i_k[:], i_k[:], w512_c[:], Alu.mult)
                        v.drain()
                    for sl in range(NSL):
                        gs = ti * NSL + sl
                        v.wait_ge(s_omm, gs + 1)
                        v.wait_ge(s_l2x[gs % 4], 16 * (gs // 4 + 1))
                        v.scalar_tensor_tensor(
                            y1p[gs % 4][:], psW[sl % 2][:], 1.0,
                            y1p[gs % 4][:].bitcast(F32),
                            Alu.mult, Alu.add,
                            accum_out=sums[:, sl:sl + 1]).then_inc(s_y1p, 1)
                    v.wait_ge(s_sq, ti * NSL + NSL)
                    v.drain()
                    v.tensor_scalar(m_col[:], sums[:], 1.0 / C, None, Alu.mult)
                    v.tensor_tensor(v_col[:], sumsq[:].rearrange(
                        "p (a b) -> p a b", b=2)[:, :, 0],
                        sumsq[:].rearrange("p (a b) -> p a b", b=2)[:, :, 1],
                        Alu.add)
                    v.drain()
                    v.tensor_tensor(b_col[:], m_col[:], m_col[:], Alu.mult)
                    v.drain()
                    v.scalar_tensor_tensor(v_col[:], v_col[:], 1.0 / C,
                                           b_col[:], Alu.mult, Alu.subtract
                                           ).then_inc(s_var, 1)
                    v.wait_ge(s_rsq, ti + 1)
                    v.scalar_tensor_tensor(b_col[:], m_col[:], -1.0,
                                           is_col[:], Alu.mult, Alu.mult)
                    v.drain()
                    for sl in range(NSL):
                        gs = ti * NSL + sl
                        v.tensor_scalar(y1p[gs % 4][:],
                                        y1p[gs % 4][:].bitcast(F32),
                                        is_col[:, sl:sl + 1],
                                        b_col[:, sl:sl + 1],
                                        Alu.mult, Alu.add).then_inc(s_y1w, 1)


            @blk.gpsimd
            def _(g):
                for ti in range(NT):
                    for sl in range(NSL):
                        gs = ti * NSL + sl
                        g.wait_ge(s_y1w, gs + 1)
                        g.dma_start(out=y1_d[ti, sl], in_=y1p[gs % 4][:]
                                    ).then_inc(s_sty[gs % 4], 16)

            if exchange:
                LASTS = (NT * NSL - 1) % 4

                @blk.vector
                def _(v):
                    v.wait_ge(s_y1w, NT * NSL)
                    v.drain()
                    v.tensor_scalar(pay2[:], y1p[LASTS][:].bitcast(F32),
                                    flagc[:, 0:1], None, Alu.mult
                                    ).then_inc(s_pay2, 1)

                @blk.gpsimd
                def _(g):
                    g.wait_ge(s_pay2, 1)
                    g.dma_start(out=cc2_in.ap(), in_=pay2[127:128, :]
                                ).then_inc(s_ccd, 16)
                    g.wait_ge(s_ccd, 48)
                    g.collective_compute(
                        "AllReduce", Alu.add,
                        replica_groups=[[0, 1], [2, 3], [4, 5], [6, 7]],
                        ins=[cc2_in.ap()], outs=[cc2_out.ap()],
                    ).then_inc(s_ccc, 1)
                    g.wait_ge(s_ccc, 2)

                @blk.gpsimd
                def _(g):
                    g.dma_start(out=rcv2fm[:],
                                in_=cc2_out.ap().rearrange(
                                    "o (a p) -> (o p) a", p=128),
                                allow_slow_non_contiguous=True
                                ).then_inc(s_ccd, 16)

                @blk.vector
                def _(v):
                    v.wait_ge(s_ccd, 64)
                    v.tensor_scalar(ycol0[:], rcv2fm[:],
                                    flagu[:, 0:1], None, Alu.mult
                                    ).then_inc(s_ycol, 1)

        ph2a.close()
        ph2b = ExitStack()
        d_buf = [ph2b.enter_context(nc.sbuf_tensor(f"d_buf{i}", [128, TILE], F32))
                 for i in range(2)]
        y1t = [ph2b.enter_context(nc.sbuf_tensor(f"y1t{i}", [128, C], F32R))
               for i in range(2)]
        y1r = [ph2b.enter_context(nc.sbuf_tensor(f"y1r{i}", [128, C], F32R))
               for i in range(2)]
        y1fm = ph2b.enter_context(nc.sbuf_tensor("y1fm", [128, NCH, TILE + 1], F32))
        ybcol = ph2b.enter_context(nc.sbuf_tensor("ybcol", [128, NCH], F32))
        xr_fm = ph2b.enter_context(nc.sbuf_tensor("xr_fm", [128, NCH, TILE], F32R))
        xk_fm = dexpt
        kact = ph2b.enter_context(nc.sbuf_tensor("kact", [128, NCH, TILE], F32R))
        sigf = [ph2b.enter_context(nc.sbuf_tensor(f"sigf{i}", [128, C], F32))
                for i in range(2)]
        y2p = [ph2b.enter_context(nc.sbuf_tensor(f"y2p{i}", [128, C], F32))
               for i in range(2)]
        if True:
            @blk.sync
            def _(sync):
                for _q in range(4):
                    sync.wait_ge(s_sty[_q], 16 * NT)
                sync.wait_ge(s_rmm, NT * NCH)
                sync.wait_ge(s_y1w, NT * NSL)

                def y1t_loads(sync, ti):
                    for sl in range(NSL):
                        gs = ti * NSL + sl
                        if gs >= 2:
                            sync.wait_ge(s_T3, gs - 1)
                        sync.dma_start(out=y1t[gs % 2][:], in_=y1_d[ti, sl]
                                       ).then_inc(s_l3t[gs % 2], 16)

                y1t_loads(sync, 0)
                sync.dma_start(out=wslotA[:], in_=w_d["fWrT"].rearrange(
                    "(a p) c -> p a c", p=128)).then_inc(s_wfr, 16)
                sync.dma_start(out=wslotC[:], in_=w_d["fWvT"].rearrange(
                    "(a p) c -> p a c", p=128)).then_inc(s_wfv, 16)
                for ti in range(NT):
                    if ti + 1 < NT:
                        y1t_loads(sync, ti + 1)
                    for sl in range(NSL):
                        gs = ti * NSL + sl
                        if gs >= 2:
                            sync.wait_ge(s_pr, gs - 1)
                        sync.dma_start(out=y1r[gs % 2][:], in_=y1_d[ti, sl]
                                       ).then_inc(s_l3r[gs % 2], 16)

            @blk.vector
            def _(v):
                i = v.memset(ybcol[:], 0.0)
                if exchange:
                    v.wait_ge(s_ycol, 1)
                    v.drain()
                    i = v.tensor_copy(ybcol[:], ycol0[:])
                i.then_inc(s_bnd3, 1)
                v.drain()

            @blk.tensor
            def _(t):
                t.wait_ge(s_wfk, 16)

                def transp(t, k, sl):
                    gs = k * NSL + sl
                    t.wait_ge(s_l3t[gs % 2], 16 * (gs // 2 + 1))
                    if gs >= 1:
                        t.wait_ge(s_ev3, gs)
                    for j in range(NCH):
                        dst = psA[0] if j < 4 else psA[1]
                        i = t.transpose(
                            dst[:].bitcast(F32R)[:, (j % 4) * 128:(j % 4 + 1) * 128],
                            y1t[gs % 2][:, j * 128:(j + 1) * 128],
                            ident[:])
                    i.then_inc(s_T3, 1)

                def rf_mm(t, ti, sl):
                    gs = ti * NSL + sl
                    if gs == 0:
                        t.wait_ge(s_wfr, 16)
                    if gs >= 2:
                        t.wait_ge(s_pr, gs - 1)
                    for c in range(NCH):
                        for h in range(2):
                            i = t.matmul(
                                psW[sl % 2][:, h * 512:(h + 1) * 512],
                                xr_fm[:, c, sl * 128:(sl + 1) * 128],
                                wslotA[:, c, h * 512:(h + 1) * 512],
                                start=(c == 0), stop=(c == NCH - 1))
                    i.then_inc(s_rfmm, 1)

                def vo_mm(t, ti, sl):
                    gs = ti * NSL + sl
                    t.wait_ge(s_kact, ti * NCH + NCH)
                    if gs == 0:
                        t.wait_ge(s_wfv, 16)
                    t.wait_ge(s_fe, gs + 1)
                    for j in range(NCH):
                        for h in range(2):
                            i = t.matmul(
                                psW[sl % 2][:, h * 512:(h + 1) * 512],
                                kact[:, j, sl * 128:(sl + 1) * 128],
                                wslotC[:, j, h * 512:(h + 1) * 512],
                                start=(j == 0), stop=(j == NCH - 1))
                    i.then_inc(s_vomm, 1)

                for sl in range(NSL):
                    transp(t, 0, sl)
                for ti in range(NT):
                    nxt = ti + 1 < NT
                    if nxt:
                        transp(t, ti + 1, 0)
                    t.wait_ge(s_mixk, ti * NCH + NCH)
                    for j in range(NCH):
                        g = ti * NCH + j
                        if g >= 2:
                            t.wait_ge(s_kact, g - 1)
                        for c in range(NCH):
                            i = t.matmul(psB[j % 2][:],
                                         wslotB[:, c, j * 128:(j + 1) * 128],
                                         xk_fm[:, c, :],
                                         start=(c == 0), stop=(c == NCH - 1))
                        i.then_inc(s_kfmm, 1)
                    t.wait_ge(s_mixr, ti * NCH + NCH)
                    for sl in range(NSL):
                        if nxt and sl + 1 < NSL:
                            transp(t, ti + 1, sl + 1)
                        rf_mm(t, ti, sl)
                        vo_mm(t, ti, sl)

            @blk.scalar
            def _(sc):
                def evac(sc, k, sl):
                    gs = k * NSL + sl
                    sc.wait_ge(s_T3, gs + 1)
                    if k >= 1 and sl == 0:
                        sc.wait_ge(s_mixr, k * NCH)
                    if sl == NSL - 1:
                        sc.wait_ge(s_bnd3, k + 1)
                    sc.copy(y1fm[:, 0:4, 1 + sl * 128:1 + (sl + 1) * 128],
                            psA[0][:].bitcast(F32R).rearrange(
                                "p (a b) -> p a b", a=4))
                    sc.copy(y1fm[:, 4:8, 1 + sl * 128:1 + (sl + 1) * 128],
                            psA[1][:].bitcast(F32R).rearrange(
                                "p (a b) -> p a b", a=4)).then_inc(s_ev3, 1)

                def sig_block(sc, ti, sl):
                    gs = ti * NSL + sl
                    sc.wait_ge(s_rfmm, gs + 1)
                    sf = sigf[sl % 2]
                    if gs >= 2:
                        sc.wait_ge(s_pr, gs - 1)
                    sc.activation(sf[:], psW[sl % 2][:], Act.Exp,
                                  scale=cpool[:, CB_M1:CB_M1 + 1]
                                  ).then_inc(s_fe, 1)
                    sc.drain()
                    sc.activation(sf[:], sf[:], Act.Ln,
                                  bias=cpool[:, CB_1:CB_1 + 1])
                    sc.drain()
                    sc.activation(sf[:], sf[:], Act.Exp,
                                  scale=cpool[:, CB_M1:CB_M1 + 1]
                                  ).then_inc(s_sigf, 1)

                def sq_rsq(sc, ti, sl):
                    gs = ti * NSL + sl
                    sc.wait_ge(s_pr, gs + 1)
                    sc.drain()
                    sc.activation(sq_junk[:], y2p[sl % 2][:, 0:512],
                                  Act.Square,
                                  accum_out=sumsq[:, 2 * sl:2 * sl + 1])
                    sc.drain()
                    sc.activation(sq_junk[:], y2p[sl % 2][:, 512:C],
                                  Act.Square,
                                  accum_out=sumsq[:, 2 * sl + 1:2 * sl + 2]
                                  ).then_inc(s_sq3, 1)
                    sc.wait_ge(s_var3, gs + 1)
                    sc.activation(is_col[:, sl:sl + 1], v_col[:, sl:sl + 1],
                                  Act.Ln,
                                  bias=cpool[:, CB_LNEPS:CB_LNEPS + 1])
                    sc.drain()
                    sc.activation(is_col[:, sl:sl + 1], is_col[:, sl:sl + 1],
                                  Act.Exp, scale=cpool[:, CB_M05:CB_M05 + 1]
                                  ).then_inc(s_rsq3, 1)

                for sl in range(NSL):
                    evac(sc, 0, sl)
                for ti in range(NT):
                    nxt = ti + 1 < NT
                    for j in range(NCH):
                        g = ti * NCH + j
                        sc.wait_ge(s_kfmm, g + 1)
                        sc.drain()
                        sc.activation(sq_junk[:], psB[j % 2][:], Act.Relu)
                        sc.drain()
                        sc.activation(kact[:, j, :], sq_junk[:], Act.Square
                                      ).then_inc(s_kact, 1)
                    for sl in range(NSL):
                        if nxt:
                            evac(sc, ti + 1, sl)
                        sig_block(sc, ti, sl)
                        if sl >= 1:
                            sq_rsq(sc, ti, sl - 1)
                    sq_rsq(sc, ti, NSL - 1)

            @blk.vector
            def _(v):
                i = v.memset(ybcol[:], 0.0)
                if exchange:
                    v.wait_ge(s_ycol, 1)
                    v.drain()
                    i = v.tensor_copy(ybcol[:], ycol0[:])
                i.then_inc(s_bnd3, 1)
                v.drain()

                def xk_pass(v, k):
                    if k >= 1:
                        v.wait_ge(s_kfmm, k * NCH)
                    if k == 0:
                        v.tensor_copy(y1fm[:, :, 0:1],
                                      ybcol[:].rearrange("p (a b) -> p a b", b=1))
                    v.wait_ge(s_ev3, k * NSL + NSL)
                    for j in range(NCH + 1):
                        v.drain()
                        if j < NCH:
                            v.tensor_tensor(d_buf[j % 2][:],
                                            y1fm[:, j, 1:TILE + 1],
                                            y1fm[:, j, 0:TILE], Alu.subtract)
                        if j >= 1:
                            jj = j - 1
                            v.scalar_tensor_tensor(
                                xk_fm[:, jj, :], d_buf[jj % 2][:],
                                muk1_c[:, jj:jj + 1],
                                y1fm[:, jj, 1:TILE + 1], Alu.mult, Alu.add
                                ).then_inc(s_mixk, 1)
                    v.drain()

                def xr_pass(v, k):
                    if k >= 1:
                        v.wait_ge(s_rfmm, k * NSL)
                    for j in range(NCH + 1):
                        v.drain()
                        if j < NCH:
                            v.tensor_tensor(d_buf[j % 2][:],
                                            y1fm[:, j, 1:TILE + 1],
                                            y1fm[:, j, 0:TILE], Alu.subtract)
                        if j >= 1:
                            jj = j - 1
                            v.scalar_tensor_tensor(
                                xr_fm[:, jj, :], d_buf[jj % 2][:],
                                mur1_c[:, jj:jj + 1],
                                y1fm[:, jj, 1:TILE + 1], Alu.mult, Alu.add
                                ).then_inc(s_mixr, 1)
                    v.tensor_copy(y1fm[:, :, 0:1],
                                  y1fm[:, :, TILE:TILE + 1]).then_inc(s_bnd3, 1)
                    v.drain()

                def prod_ln(v, ti, sl):
                    gs = ti * NSL + sl
                    v.wait_ge(s_vomm, gs + 1)
                    v.wait_ge(s_sigf, gs + 1)
                    v.wait_ge(s_l3r[gs % 2], 16 * (gs // 2 + 1))
                    if gs >= 2:
                        v.wait_ge(s_sto[gs % 2], 16 * (gs // 2))
                    v.tensor_tensor(y2p[sl % 2][:], psW[sl % 2][:],
                                    sigf[sl % 2][:], Alu.mult)
                    v.drain()
                    v.scalar_tensor_tensor(
                        y2p[sl % 2][:], y2p[sl % 2][:], 1.0,
                        y1r[gs % 2][:].bitcast(F32),
                        Alu.mult, Alu.add,
                        accum_out=sums[:, sl:sl + 1]).then_inc(s_pr, 1)
                    v.wait_ge(s_sq3, gs + 1)
                    v.drain()
                    v.tensor_scalar(m_col[:, sl:sl + 1], sums[:, sl:sl + 1],
                                    1.0 / C, None, Alu.mult)
                    v.tensor_tensor(v_col[:, sl:sl + 1],
                                    sumsq[:, 2 * sl:2 * sl + 1],
                                    sumsq[:, 2 * sl + 1:2 * sl + 2], Alu.add)
                    v.drain()
                    v.tensor_tensor(b_col[:, sl:sl + 1], m_col[:, sl:sl + 1],
                                    m_col[:, sl:sl + 1], Alu.mult)
                    v.drain()
                    v.scalar_tensor_tensor(v_col[:, sl:sl + 1],
                                           v_col[:, sl:sl + 1], 1.0 / C,
                                           b_col[:, sl:sl + 1],
                                           Alu.mult, Alu.subtract
                                           ).then_inc(s_var3, 1)
                    v.wait_ge(s_rsq3, gs + 1)
                    v.scalar_tensor_tensor(b_col[:, sl:sl + 1],
                                           m_col[:, sl:sl + 1], -1.0,
                                           is_col[:, sl:sl + 1],
                                           Alu.mult, Alu.mult)
                    v.drain()
                    v.tensor_scalar(y2p[sl % 2][:], y2p[sl % 2][:],
                                    is_col[:, sl:sl + 1],
                                    b_col[:, sl:sl + 1],
                                    Alu.mult, Alu.add).then_inc(s_ow, 1)

                xk_pass(v, 0)
                xr_pass(v, 0)
                for ti in range(NT):
                    nxt = ti + 1 < NT
                    prod_ln(v, ti, 0)
                    prod_ln(v, ti, 1)
                    if nxt:
                        xk_pass(v, ti + 1)
                    prod_ln(v, ti, 2)
                    prod_ln(v, ti, 3)
                    if nxt:
                        xr_pass(v, ti + 1)

            @blk.gpsimd
            def _(g):
                for ti in range(NT):
                    for sl in range(NSL):
                        gs = ti * NSL + sl
                        g.wait_ge(s_ow, gs + 1)
                        g.dma_start(out=out_d[gs * 128:(gs + 1) * 128, :],
                                    in_=y2p[sl % 2][:]).then_inc(s_sto[gs % 2], 16)
                g.wait_ge(s_sto[0], 16 * 2 * NT)
                g.wait_ge(s_sto[1], 16 * 2 * NT)
        ph2b.close()
    return nc


_PROG_CACHE = {}


def _get_prog(T_c, exchange):
    key = (T_c, exchange)
    if key not in _PROG_CACHE:
        _PROG_CACHE[key] = build(T_c, exchange)
    return _PROG_CACHE[key]


def _fm(vec):
    return np.ascontiguousarray(np.asarray(vec, np.float32).reshape(NCH, 128).T)


def kernel(L0_attn_x, L0_attn_time_decay, L0_attn_time_first, L0_attn_time_mix,
           L0_attn_Wk, L0_attn_Wv, L0_attn_Wr, L0_attn_Wo,
           L0_ffn_W_r, L0_ffn_W_k, L0_ffn_W_v, L0_ffn_mu_r, L0_ffn_mu_k):
    x = np.ascontiguousarray(np.asarray(L0_attn_x, np.float32))
    B, T, _ = x.shape
    wT = {"WkT": L0_attn_Wk, "WvT": L0_attn_Wv, "WrT": L0_attn_Wr,
          "WoT": L0_attn_Wo, "fWrT": L0_ffn_W_r, "fWkT": L0_ffn_W_k,
          "fWvT": L0_ffn_W_v}
    wT = {k: np.ascontiguousarray(np.asarray(v, np.float32).T)
          for k, v in wT.items()}
    base = {
        "tm_fm": _fm(np.asarray(L0_attn_time_mix, np.float32).reshape(C)),
        "mur_fm": _fm(L0_ffn_mu_r),
        "muk_fm": _fm(L0_ffn_mu_k),
        "td_fm": _fm(np.repeat(np.asarray(L0_attn_time_decay, np.float32)[:, 0], 64)),
        "tf_fm": _fm(np.repeat(np.asarray(L0_attn_time_first, np.float32)[:, 0], 64)),
    }
    base.update(wT)
    zrow = np.zeros((128, NCH), np.float32)

    if not EXCHANGE:
        nc = _get_prog(T, False)
        in_maps = [dict(base, x=x[core % B], xprev_fm=zrow) for core in range(8)]
        res = run_bass_kernel_spmd(nc, in_maps, core_ids=list(range(8)))
        return np.stack([res.results[b]["out"] for b in range(B)], axis=0)
    else:
        T_c = T // 2
        nc = _get_prog(T_c, True)
        one = np.ones((128, 1), np.float32)
        zero = np.zeros((128, 1), np.float32)
        in_maps = []
        for core in range(8):
            b, half = core // 2, core % 2
            t0 = half * T_c
            in_maps.append(dict(
                base,
                x=np.ascontiguousarray(x[b, t0:t0 + T_c]),
                xprev_fm=(zrow if half == 0 else _fm(x[b, t0 - 1])),
                flagc=(one if half == 0 else zero),
                flagu=(zero if half == 0 else one)))
        res = run_bass_kernel_spmd(nc, in_maps, core_ids=list(range(8)))
        return np.stack(
            [np.concatenate([res.results[2 * b]["out"],
                             res.results[2 * b + 1]["out"]], axis=0)
             for b in range(B)], axis=0)
